# version 19 (final)
# baseline (speedup 1.0000x reference)
import numpy as np
import ml_dtypes
from contextlib import ExitStack
import concourse.bass as bass
import concourse.mybir as mybir
from concourse.bass_utils import run_bass_kernel_spmd

F32 = mybir.dt.float32
BF16 = mybir.dt.bfloat16
I32 = mybir.dt.int32
AF = mybir.ActivationFunctionType
ALU = mybir.AluOpType
AX = mybir.AxisListType
BF = ml_dtypes.bfloat16

T, D = 4096, 1024
NCH = D // 128

ENGS = ("sync", "tensor", "vector", "scalar", "gpsimd")


class Sem:
    def __init__(self, h):
        self.h = h
        self.count = 0


class _Pending:
    __slots__ = ("rec",)

    def __init__(self, rec):
        self.rec = rec

    def inc(self, sem, n=1):
        sem.count += n
        self.rec[3].append((sem.h, n, sem.count))
        return self


class _EngQ:
    def __init__(self, prog, name):
        self.__dict__["_prog"] = prog
        self.__dict__["_name"] = name

    def __getattr__(self, op):
        prog, name = self._prog, self._name

        def call(*args, **kwargs):
            rec = (op, args, kwargs, [])
            prog.q[name].append(rec)
            return _Pending(rec)

        return call

    def wait(self, sem, val=None):
        v = sem.count if val is None else val
        if v > 0:
            self._prog.q[self._name].append(("wait_ge", (sem.h, v), {}, []))


NOCHAIN = {"wait_ge", "dma_start", "dma_start_transpose", "nop", "memset", "memzero"}


class Prog:

    def __init__(self, nc):
        self.nc = nc
        self.q = {e: [] for e in ENGS}
        self.chain_sems = {}
        for e in ENGS:
            setattr(self, e, _EngQ(self, e))

    def replay(self):
        with self.nc.Block() as blk:
            for e in ENGS:
                def body(eng, _e=e):
                    ch = self.chain_sems.get(_e)
                    cnt = 0
                    last = None
                    for (op, args, kwargs, incs) in self.q[_e]:
                        chainable = ch is not None and op not in NOCHAIN
                        if chainable and last is not None:
                            eng.wait_ge(*last)
                        instr = getattr(eng, op)(*args, **kwargs)
                        if chainable and not incs:
                            cnt += 1
                            instr.then_inc(ch, 1)
                            last = (ch, cnt)
                        for (semh, n, after) in incs:
                            instr.then_inc(semh, n)
                        if chainable and incs:
                            last = (incs[0][0], incs[0][2])
                    if ch is not None and cnt:
                        pass
                getattr(blk, e)(body)


class Ctx:
    def __init__(self, nc):
        self.nc = nc
        self.p = Prog(nc)
        self.stack = ExitStack()
        self.S = {}
        for name in ("ld", "st", "te", "ve", "ac", "ld2", "po", "lda", "ldb", "stz"):
            self.S[name] = Sem(self.stack.enter_context(nc.semaphore(name)))
        for e in ("vector", "scalar", "gpsimd"):
            self.p.chain_sems[e] = self.stack.enter_context(nc.semaphore("ch_" + e))
        self.pb = [self.stack.enter_context(nc.psum_tensor(f"pb{i}", [128, 512], F32))
                   for i in range(6)]
        self.pbt = self.stack.enter_context(nc.psum_tensor("pbt", [128, 512], F32))
        self.pbx = self.stack.enter_context(nc.psum_tensor("pbx", [128, 1024], BF16))

    def sb(self, stk, name, shape, dt):
        return stk.enter_context(self.nc.sbuf_tensor(name, shape, dt))

    def barrier(self):
        for e in ENGS:
            eq = getattr(self.p, e)
            for s in self.S.values():
                if s.count > 0:
                    eq.wait(s)


def _dma(ctx, q, out, in_, sem="ld"):
    getattr(ctx.p, q).dma_start(out=out, in_=in_).inc(ctx.S[sem], 16)


def chunked(ap2d, csize=128):
    return ap2d.rearrange("(c p) x -> p c x", p=csize)



def host_consts_a():
    t = np.arange(T, dtype=np.float64)
    j = np.arange(0, D, 2, dtype=np.float64)
    freqs = 1.0 / (10000.0 ** (j / D))
    ang = np.outer(freqs, t)
    cosT = np.repeat(np.cos(ang), 2, axis=0).astype(np.float32).astype(BF)
    sinT = np.repeat(np.sin(ang), 2, axis=0).astype(np.float32).astype(BF)
    pm = np.zeros((128, 128), np.float32)
    for a in range(0, 128, 2):
        pm[a, a + 1] = -1.0
        pm[a + 1, a] = 1.0
    pmT = np.ascontiguousarray(pm.T).astype(BF)
    s = np.arange(128)[:, None]
    tt = np.arange(128)[None, :]
    ustri = (s <= tt).astype(np.float32).astype(BF)
    identbf = np.eye(128, dtype=np.float32).astype(BF)
    return {"ropecos": np.ascontiguousarray(cosT), "ropesin": np.ascontiguousarray(sinT),
            "pmT": pmT, "ustri": ustri, "identbf": identbf}


def decl_a(nc, H):
    def di(name, shape, dt):
        H[name] = nc.dram_tensor(name, list(shape), dt, kind="ExternalInput").ap()
    def ds(name, shape, dt):
        H[name] = nc.dram_tensor(name, list(shape), dt, kind="Internal").ap()
    for n in ("wqT", "wkT", "wvT", "wfT"):
        di(n, (D, D), F32)
    for n in ("bq", "bk", "bv", "bfb"):
        di(n, (1, D), F32)
    di("ropecos", (D, T), BF16)
    di("ropesin", (D, T), BF16)
    di("pmT", (128, 128), BF16)
    di("ustri", (128, 128), BF16)
    di("identbf", (128, 128), BF16)
    ds("x0bf", (T, D), BF16)
    ds("vtokA", (T, D), BF16)
    return H


def host_inputs_a(inp):
    f32 = np.float32
    d = {
        "wqT": np.ascontiguousarray(np.asarray(inp["L0_attn_Wq"], f32).T),
        "wkT": np.ascontiguousarray(np.asarray(inp["L0_attn_Wk"], f32).T),
        "wvT": np.ascontiguousarray(np.asarray(inp["L0_attn_Wv"], f32).T),
        "wfT": np.ascontiguousarray(np.asarray(inp["L0_attn_Wf"], f32).T),
        "bq": np.asarray(inp["L0_attn_bq"], f32).reshape(1, D),
        "bk": np.asarray(inp["L0_attn_bk"], f32).reshape(1, D),
        "bv": np.asarray(inp["L0_attn_bv"], f32).reshape(1, D),
        "bfb": np.asarray(inp["L0_attn_bf"], f32).reshape(1, D),
    }
    d.update(host_consts_a())
    return d




def emit_phase_a(ctx, H, x_src, x_dst, K, qdx_dst=None, wq_items=None):
    F32, BF16, AF, ALU, AX = K.F32, K.BF16, K.AF, K.ALU, K.AX
    T, D, NCH = K.T, K.D, K.NCH
    p, nc, S = ctx.p, ctx.nc, ctx.S
    ld, st, te, ve, ac, ld2, po = S["ld"], S["st"], S["te"], S["ve"], S["ac"], S["ld2"], S["po"]
    pb, pbt = ctx.pb, ctx.pbt
    MM = p.tensor.matmul
    _dma = lambda q, out, in_, sem="ld": getattr(p, q).dma_start(out=out, in_=in_).inc(S[sem], 16)
    chunked = K.chunked
    with ExitStack() as stk:
        sb = lambda n, sh, dt: ctx.sb(stk, n, sh, dt)
        ktS = sb("a_kt", [128, NCH * T], BF16)
        wS = sb("a_w", [128, NCH * 1024], BF16)
        wS2 = sb("a_w2", [128, NCH * 1024], BF16)
        biasS = sb("a_bias", [1, 1024], BF16)
        bias2S = sb("a_bias2", [1, 1024], BF16)
        onesS = sb("a_ones", [1, 512], BF16)
        ones128S = sb("a_ones128", [128, 1], BF16)
        pmS = sb("a_pm", [128, 128], BF16)
        idbfS = sb("a_idbf", [128, 128], BF16)
        ustriS = sb("a_ustri", [128, 128], BF16)
        knorm = sb("a_knorm", [1, 512], F32)
        kmax2 = sb("a_kmax2", [1, 1], F32)

        x0bf, vtok = H["x0bf"], H["vtokA"]
        for i in range(8):
            _dma("gpsimd", x0bf[i * 512:(i + 1) * 512, :], x_src[i * 512:(i + 1) * 512, :], "st")
        _dma("sync", pmS[:], H["pmT"][:, :])
        _dma("sync", idbfS[:], H["identbf"][:, :])
        _dma("sync", ustriS[:], H["ustri"][:, :])
        _dma("gpsimd", wS[:].rearrange("p (c x) -> p c x", c=NCH), chunked(H["wkT"]), "st")
        _dma("gpsimd", wS2[:].rearrange("p (c x) -> p c x", c=NCH), chunked(H["wvT"]), "st")
        _dma("gpsimd", biasS[:], H["bk"][:, :], "st")
        _dma("gpsimd", bias2S[:], H["bv"][:, :], "st")
        p.vector.memset(onesS[:], 1.0)
        p.vector.memset(ones128S[:], 1.0)
        p.vector.memset(knorm[:], 0.0).inc(ve)

        with ExitStack() as stk1:
            sb1 = lambda n, sh, dt: ctx.sb(stk1, n, sh, dt)
            xtb = [sb1(f"a_xtb{i}", [128, NCH * 512], BF16) for i in range(2)]
            cosb = sb1("a_cosb", [128, NCH * 512], BF16)
            sinb = sb1("a_sinb", [128, NCH * 512], BF16)
            tmpre = [sb1(f"a_tmpre{i}", [128, 512], BF16) for i in range(2)]
            tmp2 = sb1("a_tmp2", [128, 512], BF16)
            sqbf = [sb1(f"a_sqbf{i}", [128, 512], BF16) for i in range(2)]
            vstage = [sb1(f"a_vst{i}", [128, 1024], BF16) for i in range(2)]

            xtb_done = [0, 0]
            cs_done = 0
            kn_done = 0
            vst_done = [0, 0]
            for tb in range(8):
                i = tb % 2
                p.sync.wait(st)
                if xtb_done[i]:
                    p.sync.wait(te, xtb_done[i])
                if cs_done:
                    p.sync.wait(ve, cs_done)
                for c in range(NCH):
                    p.sync.dma_start_transpose(xtb[i][:, c * 512:(c + 1) * 512],
                         x0bf[tb * 512:(tb + 1) * 512, c * 128:(c + 1) * 128]).inc(ld, 16)
                _dma("sync", cosb[:].rearrange("p (c x) -> p c x", c=NCH),
                     chunked(H["ropecos"])[:, :, tb * 512:(tb + 1) * 512], "ld")
                _dma("sync", sinb[:].rearrange("p (c x) -> p c x", c=NCH),
                     chunked(H["ropesin"])[:, :, tb * 512:(tb + 1) * 512], "ld")
                p.tensor.wait(ld)
                p.tensor.wait(st)
                for ic in range(NCH):
                    psK = pb[ic % 2]
                    j = ic % 2
                    p.tensor.wait(ac)
                    for cc in range(NCH):
                        MM(psK[:, 0:512], wS[:, cc * 1024 + ic * 128: cc * 1024 + ic * 128 + 128],
                           xtb[i][:, cc * 512:(cc + 1) * 512], start=(cc == 0), stop=False)
                    MM(psK[:, 0:512], biasS[0:1, ic * 128:ic * 128 + 128],
                       onesS[0:1, 0:512], start=False, stop=True).inc(te)
                    p.scalar.wait(te)
                    p.scalar.wait(ve)
                    p.scalar.copy(tmpre[j][:], psK[:, 0:512])
                    p.scalar.square(sqbf[j][:], tmpre[j][:]).inc(ac)
                    p.tensor.wait(ac)
                    p.tensor.wait(ve)
                    MM(pb[2][:, 0:512], pmS[:], tmpre[j][:], start=True, stop=True).inc(te)
                    ks = ktS[:, ic * T + tb * 512: ic * T + tb * 512 + 512]
                    p.vector.wait(te)
                    p.vector.tensor_tensor(ks, tmpre[j][:], cosb[:, ic * 512:(ic + 1) * 512], op=ALU.mult)
                    p.vector.tensor_tensor(tmp2[:], pb[2][:, 0:512], sinb[:, ic * 512:(ic + 1) * 512], op=ALU.mult)
                    p.vector.tensor_tensor(ks, ks, tmp2[:], op=ALU.add).inc(ve)
                    if ic == 0 and kn_done:
                        p.tensor.wait(ve, kn_done)
                    MM(pb[3][0:1, 0:512], ones128S[:], sqbf[j][:], start=(ic == 0), stop=(ic == NCH - 1),
                       skip_group_check=True).inc(te)
                    if ic == NCH - 1:
                        p.vector.wait(te)
                        p.vector.tensor_tensor(knorm[:], knorm[:], pb[3][0:1, 0:512], op=ALU.max).inc(ve)
                        kn_done = ve.count
                cs_done = ve.count
                for tl in range(4):
                    vb = vstage[tl % 2]
                    for ih in range(2):
                        psV = pb[4 + ih]
                        p.tensor.wait(ac)
                        for cc in range(NCH):
                            MM(psV[:, 0:512],
                               xtb[i][:, cc * 512 + tl * 128: cc * 512 + tl * 128 + 128],
                               wS2[:, cc * 1024 + ih * 512: cc * 1024 + (ih + 1) * 512],
                               start=(cc == 0), stop=False)
                        MM(psV[:, 0:512], onesS[0:1, 0:128],
                           bias2S[0:1, ih * 512:(ih + 1) * 512], start=False, stop=True).inc(te)
                        p.scalar.wait(te)
                        if vst_done[tl % 2]:
                            p.scalar.wait(st, vst_done[tl % 2])
                        p.scalar.copy(vb[:, ih * 512:(ih + 1) * 512], psV[:, 0:512]).inc(ac)
                    p.gpsimd.wait(ac)
                    _dma("gpsimd", vtok[(tb * 4 + tl) * 128:(tb * 4 + tl + 1) * 128, :], vb[:], "st")
                    vst_done[tl % 2] = st.count
                xtb_done[i] = te.count

            p.vector.wait(te)
            p.vector.tensor_reduce(kmax2[:], knorm[:], axis=AX.X, op=ALU.max).inc(ve)

            p.gpsimd.wait(te)
            p.gpsimd.wait(ac)
            _dma("gpsimd", wS[:].rearrange("p (c x) -> p c x", c=NCH), chunked(H["wqT"]), "st")
            _dma("gpsimd", wS2[:].rearrange("p (c x) -> p c x", c=NCH), chunked(H["wfT"]), "st")
            _dma("gpsimd", biasS[:], H["bq"][:, :], "st")
            _dma("gpsimd", bias2S[:], H["bfb"][:, :], "st")
            p.vector.wait(st)
            p.vector.tensor_scalar_mul(wS[:], wS[:], 1.0 / 32.0)
            p.vector.tensor_scalar_mul(biasS[:], biasS[:], 1.0 / 32.0).inc(ve)

        dbg = getattr(ctx, 'dbg', None)
        if dbg is not None:
            p.gpsimd.wait(ve)
            p.gpsimd.wait(te)
            p.gpsimd.wait(ac)
            _dma("gpsimd", dbg["d_kt"][:, :], ktS[:], "st")
            _dma("gpsimd", dbg["d_x0bf"][0:512, :], x0bf[0:512, :], "st")
            _dma("gpsimd", dbg["d_vtok"][0:512, :], vtok[0:512, :], "st")
        ctx.barrier()
        with ExitStack() as stk2:
            sb2 = lambda n, sh, dt: ctx.sb(stk2, n, sh, dt)
            xtg = sb2("a_xtg", [128, NCH * 256], BF16)
            cosg = sb2("a_cosg", [128, NCH * 256], BF16)
            sing = sb2("a_sing", [128, NCH * 256], BF16)
            qTs = sb2("a_qT", [128, NCH * 256], BF16)
            q2bf = sb2("a_q2", [128, 256], BF16)
            tmq = [sb2(f"a_tmq{i}", [128, 256], BF16) for i in range(2)]
            tmp2q = sb2("a_tmp2q", [128, 256], BF16)
            mrow = sb2("a_mrow", [1, 256], F32)
            negm = sb2("a_negm", [1, 256], BF16)
            ptile = [sb2(f"a_pt{i}", [128, 128], BF16) for i in range(2)]
            vblk = [sb2(f"a_vb{i}", [128, 1024], BF16) for i in range(2)]
            rcol = sb2("a_rcol", [128, 1], F32)
            obf = sb2("a_obf", [128, 1024], BF16)
            oT = sb2("a_oT", [128, 1024], BF16)
            x0t = sb2("a_x0t", [128, 1024], F32)
            zs = sb2("a_zs", [128, 1024], F32)
            b6 = sb2("a_b6", [128, 12], F32)
            b2 = sb2("a_b2", [128, 2], F32)
            svar = sb2("a_svar", [128, 1], F32)
            rstd = sb2("a_rstd", [128, 1], F32)
            if qdx_dst is not None:
                qbA = mk_qdb(ctx, stk2, K, Wmax=1024, pfx="qa")
                qsrc = [sb2(f"a_qsrc{i}", [128, 1024], F32) for i in range(2)]
                qdst = [sb2(f"a_qdst{i}", [128, 1024], BF16) for i in range(2)]
                qdw = [sb2(f"a_qdw{i}", [128, 1024], BF16) for i in range(2)]
                wq_queue = list(wq_items or [])
                qst_done = [0, 0]
                qld = {}

            prev_ve = 0
            vblk_done = [0, 0]
            pt_done = [0, 0]
            e_ac = [0, 0]
            qe_ac = [0, 0]
            rope_v = [0, 0]
            rope_last = 0
            norm_te = 0
            xtg_te = 0
            negm_ve = 0
            obf_ve = 0
            oT_ac = 0
            psY_te = 0
            zq_ac = zq_po = 0
            for g in range(16):
                if xtg_te:
                    p.sync.wait(te, xtg_te)
                if rope_last:
                    p.sync.wait(ve, rope_last)
                for c in range(NCH):
                    p.sync.dma_start_transpose(xtg[:, c * 256:(c + 1) * 256],
                         x0bf[g * 256:(g + 1) * 256, c * 128:(c + 1) * 128]).inc(ld, 16)
                _dma("sync", cosg[:].rearrange("p (c x) -> p c x", c=NCH),
                     chunked(H["ropecos"])[:, :, g * 256:(g + 1) * 256], "ld")
                _dma("sync", sing[:].rearrange("p (c x) -> p c x", c=NCH),
                     chunked(H["ropesin"])[:, :, g * 256:(g + 1) * 256], "ld")
                p.tensor.wait(ld)
                if g == 0:
                    p.tensor.wait(st)
                for ic in range(NCH):
                    psQ = pb[ic % 2]
                    j = ic % 2
                    if qe_ac[j]:
                        p.tensor.wait(ac, qe_ac[j])
                    if ic < 2 and obf_ve:
                        p.tensor.wait(ve, obf_ve)
                    for cc in range(NCH):
                        MM(psQ[:, 0:256], wS[:, cc * 1024 + ic * 128: cc * 1024 + ic * 128 + 128],
                           xtg[:, cc * 256:(cc + 1) * 256], start=(cc == 0), stop=False)
                    MM(psQ[:, 0:256], biasS[0:1, ic * 128:ic * 128 + 128],
                       onesS[0:1, 0:256], start=False, stop=True).inc(te)
                    qmm_te = te.count
                    p.scalar.wait(te, qmm_te)
                    if rope_v[j]:
                        p.scalar.wait(ve, rope_v[j])
                    p.scalar.copy(tmq[j][:], psQ[:, 0:256]).inc(ac)
                    qe_ac[j] = S["ac"].count
                    p.tensor.wait(ac, qe_ac[j])
                    rv_g = rope_v[1 - j] if ic > 0 else obf_ve
                    if rv_g:
                        p.tensor.wait(ve, rv_g)
                    MM(pb[2][:, 0:256], pmS[:], tmq[j][:], start=True, stop=True).inc(te)
                    rot_te = te.count
                    qs = qTs[:, ic * 256:(ic + 1) * 256]
                    if norm_te:
                        p.scalar.wait(te, norm_te)
                    p.scalar.square(q2bf[:], tmq[j][:]).inc(ac)
                    sq_ac = S["ac"].count
                    p.vector.wait(te, rot_te)
                    p.vector.tensor_tensor(qs, tmq[j][:], cosg[:, ic * 256:(ic + 1) * 256], op=ALU.mult)
                    p.vector.tensor_tensor(tmp2q[:], pb[2][:, 0:256], sing[:, ic * 256:(ic + 1) * 256], op=ALU.mult)
                    p.vector.tensor_tensor(qs, qs, tmp2q[:], op=ALU.add).inc(ve)
                    rope_v[j] = S["ve"].count
                    if ic == NCH - 1:
                        rope_last = rope_v[j]
                    p.tensor.wait(ac, sq_ac)
                    MM(pb[3][0:1, 0:256], ones128S[:], q2bf[:],
                       start=(ic == 0), stop=(ic == NCH - 1), skip_group_check=True).inc(te)
                    norm_te = te.count
                xtg_te = norm_te
                p.scalar.wait(te, norm_te)
                if negm_ve:
                    p.scalar.wait(ve, negm_ve)
                p.scalar.activation(mrow[:], pb[3][0:1, 0:256], AF.Sqrt, scale=kmax2[0:1, 0:1]).inc(ac)
                mrow_ac = S["ac"].count
                p.vector.wait(ac, mrow_ac)
                if psY_te:
                    p.vector.wait(te, psY_te)
                p.vector.tensor_scalar_mul(negm[:], mrow[:], -1.0).inc(ve)
                negm_ve = S["ve"].count
                p.tensor.wait(ve, negm_ve)
                for th in range(2):
                    tt = 2 * g + th
                    nsb = tt + 1
                    if prev_ve:
                        p.tensor.wait(ve, prev_ve)
                    qsl = lambda ic: qTs[:, ic * 256 + th * 128: ic * 256 + th * 128 + 128]
                    ngm = negm[0:1, th * 128:(th + 1) * 128]
                    s_te = {}
                    vld = {}

                    def emit_S(sbk):
                        j = sbk % 2
                        sl = "lda" if j == 0 else "ldb"
                        psS = pb[4 + j]
                        if vblk_done[j]:
                            p.gpsimd.wait(te, vblk_done[j])
                        _dma("gpsimd", vblk[j][:], vtok[sbk * 128:(sbk + 1) * 128, :], sl)
                        vld[sbk] = S[sl].count
                        if e_ac[j]:
                            p.tensor.wait(ac, e_ac[j])
                        for cc in range(NCH):
                            MM(psS[:, 0:128],
                               ktS[:, cc * T + sbk * 128: cc * T + sbk * 128 + 128],
                               qsl(cc), start=(cc == 0), stop=False)
                        MM(psS[:, 0:128], onesS[0:1, 0:128], ngm, start=False, stop=True).inc(te)
                        s_te[sbk] = S["te"].count

                    emit_S(0)
                    for sbk in range(nsb):
                        j = sbk % 2
                        if sbk + 1 < nsb:
                            emit_S(sbk + 1)
                        p.scalar.wait(te, s_te[sbk])
                        if pt_done[j]:
                            p.scalar.wait(te, pt_done[j])
                        p.scalar.activation(ptile[j][:], pb[4 + j][:, 0:128], AF.Exp).inc(ac)
                        e_ac[j] = S["ac"].count
                        if sbk == tt:
                            p.vector.wait(ac, e_ac[j])
                            p.vector.tensor_tensor(ptile[j][:], ptile[j][:], ustriS[:], op=ALU.mult).inc(ve)
                            p.tensor.wait(ve)
                        else:
                            p.tensor.wait(ac, e_ac[j])
                        p.tensor.wait(S["lda" if j == 0 else "ldb"], vld[sbk])
                        MM(pb[2][:, 0:1], ptile[j][:], ones128S[:],
                           start=(sbk == 0), stop=(sbk == nsb - 1), skip_group_check=True)
                        MM(pb[0][:, 0:512], ptile[j][:], vblk[j][:, 0:512],
                           start=(sbk == 0), stop=(sbk == nsb - 1), skip_group_check=True)
                        MM(pb[1][:, 0:512], ptile[j][:], vblk[j][:, 512:1024],
                           start=(sbk == 0), stop=(sbk == nsb - 1), skip_group_check=True).inc(te)
                        pt_done[j] = te.count
                        vblk_done[j] = te.count
                    sloop_te = te.count
                    p.vector.wait(te, sloop_te)
                    p.vector.reciprocal(rcol[:], pb[2][:, 0:1])
                    p.vector.tensor_scalar(obf[:, 0:512], pb[0][:, 0:512], rcol[:], None, op0=ALU.mult)
                    p.vector.tensor_scalar(obf[:, 512:1024], pb[1][:, 0:512], rcol[:], None, op0=ALU.mult).inc(ve)
                    obf_ve = S["ve"].count
                    p.tensor.wait(ve, obf_ve)
                    for ic in range(NCH):
                        mm_i = MM(ctx.pbx[:, ic * 128:(ic + 1) * 128], obf[:, ic * 128:(ic + 1) * 128],
                                  idbfS[:], start=True, stop=True, is_transpose=True)
                    mm_i.inc(te)
                    oTT_te = te.count
                    p.scalar.wait(te, oTT_te)
                    p.scalar.copy(oT[:], ctx.pbx[:, 0:1024]).inc(ac)
                    oT_ac = S["ac"].count
                    if prev_ve:
                        p.sync.wait(ve, prev_ve)
                    _dma("sync", x0t[:], x_src[tt * 128:(tt + 1) * 128, :], "ld")
                    p.tensor.wait(ac, oT_ac)
                    for ih in range(2):
                        psY = pb[4 + ih]
                        for dc in range(NCH):
                            MM(psY[:, 0:512], oT[:, dc * 128:(dc + 1) * 128],
                               wS2[:, dc * 1024 + ih * 512: dc * 1024 + (ih + 1) * 512],
                               start=(dc == 0), stop=False)
                        MM(psY[:, 0:512], onesS[0:1, 0:128],
                           bias2S[0:1, ih * 512:(ih + 1) * 512], start=False, stop=True).inc(te)
                    psY_te = te.count
                    p.vector.wait(te, psY_te)
                    p.vector.wait(ld)
                    p.vector.wait(S["stz"])
                    if qdx_dst is not None:
                        if zq_po:
                            p.vector.wait(po, zq_po)
                        if zq_ac:
                            p.vector.wait(ac, zq_ac)
                    p.vector.tensor_tensor(zs[:, 0:512], pb[4][:, 0:512], x0t[:, 0:512], op=ALU.add)
                    p.vector.tensor_tensor(zs[:, 512:1024], pb[5][:, 0:512], x0t[:, 512:1024], op=ALU.add).inc(ve)
                    prev_ve = ve.count
                    if dbg is not None and tt == 0:
                        p.gpsimd.wait(ve)
                        _dma("gpsimd", dbg["d_zs"][:, :], zs[:], "st")
                        _dma("gpsimd", dbg["d_obf"][:, :], obf[:], "st")
                    if dbg is not None and g == 1 and th == 1:
                        p.gpsimd.wait(ve)
                        _dma("gpsimd", dbg["d_qT"][:, :], qTs[:], "st")
                    p.vector.bn_stats(b6[:, 0:6], zs[:, 0:512])
                    p.vector.bn_stats(b6[:, 6:12], zs[:, 512:1024])
                    p.vector.bn_aggr(b2[:], b6[:])
                    p.vector.tensor_single_scalar(svar[:], b2[:, 1:2], 1e-5, op=ALU.add).inc(ve)
                    p.scalar.wait(ve)
                    p.scalar.activation(rstd[:], svar[:], AF.Sqrt).inc(ac)
                    p.vector.wait(ac)
                    p.vector.reciprocal(rstd[:], rstd[:])
                    p.vector.tensor_scalar(zs[:], zs[:], b2[:, 0:1], rstd[:],
                                           op0=ALU.subtract, op1=ALU.mult).inc(ve)
                    p.gpsimd.wait(ve)
                    _dma("gpsimd", x_dst[tt * 128:(tt + 1) * 128, :], zs[:], "stz")
                    if qdx_dst is not None:
                        p.vector.wait(st)
                        p.gpsimd.wait(st)
                        emit_qd(ctx, K, qbA, 1024, zs, qdst[tt % 2])
                        zq_ac = S["ac"].count
                        zq_po = S["po"].count
                        p.gpsimd.wait(ve)
                        _dma("gpsimd", qdx_dst[tt * 128:(tt + 1) * 128, :], qdst[tt % 2][:], "st")
                        nw = min(2, len(wq_queue))
                        for _ in range(nw):
                            Wq_, src_ap, dst_ap = wq_queue.pop(0)
                            bi = len(wq_queue) % 2
                            if qst_done[bi]:
                                p.sync.wait(ve, qst_done[bi][0])
                                p.sync.wait(po, qst_done[bi][1])
                                p.sync.wait(ac, qst_done[bi][2])
                            _dma("sync", qsrc[bi][:, 0:Wq_], src_ap, "ld2")
                            qn = S["ld2"].count
                            for e_ in ("vector", "gpsimd"):
                                getattr(p, e_).wait(S["ld2"], qn)
                                getattr(p, e_).wait(st)
                            p.scalar.wait(S["ld2"], qn)
                            emit_qd(ctx, K, qbA, Wq_, qsrc[bi], qdw[bi])
                            qst_done[bi] = (S["ve"].count, S["po"].count, S["ac"].count)
                            p.gpsimd.wait(ve)
                            _dma("gpsimd", dst_ap, qdw[bi][:, 0:Wq_], "st")
                    if dbg is not None and tt == 0:
                        p.gpsimd.wait(ve)
                        _dma("gpsimd", dbg["d_zpost"][:, :], zs[:], "st")
                        _dma("gpsimd", dbg["d_sm"][:, :], smean[:], "st")
                        _dma("gpsimd", dbg["d_sv"][:, :], svar[:], "st")
                        _dma("gpsimd", dbg["d_rs"][:, :], rstd[:], "st")
                        _dma("gpsimd", dbg["d_ss"][:, :], ssum[:], "st")
        if dbg is not None:
            p.sync.wait(st)
            _dma("sync", dbg["d_x1r"][:, :], x_dst[0:256, :], "ld")
        p.sync.wait(st)
        p.sync.wait(S["stz"])



IBLOCKS = [(0, 512), (512, 512), (1024, 512), (1536, 512), (2048, 512), (2560, 256)]


def emit_ln(ctx, K, zs, lnb, out_bf=None):
    p, S = ctx.p, ctx.S
    AF, ALU, AX = K.AF, K.ALU, K.AX
    ve, ac = S["ve"], S["ac"]
    p.vector.bn_stats(lnb["b6"][:, 0:6], zs[:, 0:512])
    p.vector.bn_stats(lnb["b6"][:, 6:12], zs[:, 512:1024])
    p.vector.bn_aggr(lnb["b2"][:], lnb["b6"][:])
    p.vector.tensor_single_scalar(lnb["svar"][:], lnb["b2"][:, 1:2], 1e-5, op=ALU.add).inc(ve)
    p.scalar.wait(ve)
    p.scalar.activation(lnb["rstd"][:], lnb["svar"][:], AF.Sqrt).inc(ac)
    p.vector.wait(ac)
    p.vector.reciprocal(lnb["rstd"][:], lnb["rstd"][:])
    p.vector.tensor_scalar(zs[:], zs[:], lnb["b2"][:, 0:1], lnb["rstd"][:],
                           op0=ALU.subtract, op1=ALU.mult).inc(ve)
    if out_bf is not None:
        p.vector.tensor_copy(out_bf[:], zs[:]).inc(ve)


def mk_lnb(ctx, stk, pfx, K):
    sb = lambda n, sh, dt: ctx.sb(stk, n, sh, dt)
    return {
        "dumb": sb(pfx + "_dumb", [128, 1024], K.BF16),
        "b6": sb(pfx + "_b6", [128, 12], K.F32),
        "b2": sb(pfx + "_b2", [128, 2], K.F32),
        "svar": sb(pfx + "_svar", [128, 1], K.F32),
        "rstd": sb(pfx + "_rstd", [128, 1], K.F32),
    }


def mk_qdb(ctx, stk, K, Wmax=2816, pfx="qd"):
    sb = lambda n, sh, dt: ctx.sb(stk, n, sh, dt)
    qb = {
        "z": sb(pfx + "_z", [128, Wmax], K.F32),
        "t1": sb(pfx + "_t1", [128, Wmax], K.F32),
        "s1": sb(pfx + "_s1", [128, Wmax], K.F32),
        "bq": sb(pfx + "_b", [128, Wmax], K.F32),
        "sgn": sb(pfx + "_sgn", [128, Wmax], K.BF16),
        "u": sb(pfx + "_u", [128, Wmax], K.F32),
        "red": sb(pfx + "_red", [128, Wmax // 16], K.F32),
        "scl": sb(pfx + "_scl", [128, Wmax // 16], K.F32),
        "sc4": sb(pfx + "_sc4", [128, Wmax // 16], K.F32),
        "rcp": sb(pfx + "_rcp", [128, Wmax // 16], K.F32),
        "sa": [sb(pfx + f"_sa{i}", [128, Wmax], K.BF16) for i in range(4)],
    }
    ths = (0.25, 0.75, 1.25, 1.75, 2.5, 3.5, 5.0)
    qb["th"] = {}
    for t in ths:
        ap = sb(pfx + "_th%d" % int(t * 100), [128, 1], K.F32)
        mi = ctx.p.vector.memset(ap[:], -t)
        qb["th"][t] = ap
    mi.inc(ctx.S["ve"])
    return qb


def emit_qd(ctx, K, qb, W, src, dst):
    p, S = ctx.p, ctx.S
    AF, ALU, AX = K.AF, K.ALU, K.AX
    ve, ac, po = S["ve"], S["ac"], S["po"]
    nb = W // 16
    W1 = ((nb * 93) // 100) * 16
    halves = [("vector", 0, W1), ("gpsimd", W1, W)]
    v3 = lambda ap, lo, hi: ap[:, lo:hi].rearrange("p (a b) -> p a b", b=16)
    p.scalar.wait(ve)
    p.scalar.wait(po)
    p.scalar.sign(qb["sgn"][:, 0:W], src[:, 0:W]).inc(ac)
    p.scalar.activation(qb["z"][:, 0:W], src[:, 0:W], AF.Abs).inc(ac)
    entry_ac = S["ac"].count
    p.vector.wait(po)
    p.vector.tensor_reduce(qb["red"][:, 0:nb], v3(src, 0, W), axis=AX.X, op=ALU.max,
                           apply_absolute_value=True)
    p.vector.tensor_scalar(qb["scl"][:, 0:nb], qb["red"][:, 0:nb], 1.0 / 6.0, 1e-12,
                           op0=ALU.mult, op1=ALU.max)
    p.vector.tensor_scalar_mul(qb["sc4"][:, 0:nb], qb["scl"][:, 0:nb], 0.25)
    p.vector.reciprocal(qb["rcp"][:, 0:nb], qb["scl"][:, 0:nb]).inc(ve)
    p.gpsimd.wait(ve)
    for eng, lo, hi in halves:
        e = getattr(p, eng)
        bl, bh = lo // 16, hi // 16
        rb = qb["rcp"][:, bl:bh].unsqueeze(2).broadcast_to([128, bh - bl, 16])
        sb_ = qb["scl"][:, bl:bh].unsqueeze(2).broadcast_to([128, bh - bl, 16])
        s4_ = qb["sc4"][:, bl:bh].unsqueeze(2).broadcast_to([128, bh - bl, 16])
        z, t1, s1, b = qb["z"][:, lo:hi], qb["t1"][:, lo:hi], qb["s1"][:, lo:hi], qb["bq"][:, lo:hi]
        u = qb["u"][:, lo:hi]
        e.wait(ac, entry_ac)
        if eng == "vector":
            sa, sb2, sc, sd = (qb["sa"][i][:, lo:hi] for i in range(4))
            e.tensor_tensor(v3(qb["z"], lo, hi), v3(qb["z"], lo, hi), rb, op=ALU.mult).inc(ve)
            zv = S["ve"].count
            p.scalar.wait(ve, zv)
            p.scalar.activation(sa, z, AF.Sign, bias=qb["th"][0.25][:])
            p.scalar.activation(sb2, z, AF.Sign, bias=qb["th"][0.75][:])
            p.scalar.activation(sc, z, AF.Sign, bias=qb["th"][1.25][:])
            p.scalar.activation(sd, z, AF.Sign, bias=qb["th"][1.75][:]).inc(ac)
            c1 = S["ac"].count
            e.wait(ac, c1)
            e.tensor_tensor(sa, sa, sb2, op=ALU.add)
            e.tensor_tensor(sc, sc, sd, op=ALU.add)
            e.tensor_tensor(sa, sa, sc, op=ALU.add).inc(ve)
            v1 = S["ve"].count
            p.scalar.wait(ve, v1)
            p.scalar.activation(sb2, z, AF.Sign, bias=qb["th"][2.5][:])
            p.scalar.activation(sc, z, AF.Sign, bias=qb["th"][3.5][:])
            p.scalar.activation(sd, z, AF.Sign, bias=qb["th"][5.0][:]).inc(ac)
            c2 = S["ac"].count
            e.wait(ac, c2)
            e.tensor_tensor(sb2, sb2, sc, op=ALU.add)
            e.tensor_tensor(sb2, sb2, sd, op=ALU.add)
            e.tensor_tensor(sb2, sb2, sd, op=ALU.add)
            e.scalar_tensor_tensor(t1, sb2, 2.0, sa, op0=ALU.mult, op1=ALU.add)
            e.tensor_single_scalar(t1, t1, 12.0, op=ALU.add)
            e.tensor_tensor(s1, t1, qb["sgn"][:, lo:hi], op=ALU.mult)
            ei = e.tensor_tensor(v3(dst, lo, hi), v3(qb["s1"], lo, hi), s4_, op=ALU.mult)
            ei.inc(ve)
        else:
            e.tensor_tensor(v3(qb["z"], lo, hi), v3(qb["z"], lo, hi), rb, op=ALU.mult)
            e.tensor_single_scalar(s1, z, 5.0, op=ALU.is_gt)
            e.tensor_single_scalar(u, z, 3.5, op=ALU.is_gt)
            e.tensor_tensor(t1, u, s1, op=ALU.add)
            e.tensor_single_scalar(u, z, 2.5, op=ALU.is_gt)
            e.tensor_tensor(t1, t1, u, op=ALU.add)
            e.tensor_tensor(t1, t1, s1, op=ALU.add)
            e.tensor_single_scalar(b, z, 0.25, op=ALU.is_gt)
            for th in (0.75, 1.25, 1.75):
                e.tensor_single_scalar(u, z, th, op=ALU.is_gt)
                e.tensor_tensor(b, b, u, op=ALU.add)
            e.tensor_scalar_mul(b, b, 0.5)
            e.tensor_tensor(t1, t1, b, op=ALU.add)
            e.wait(ac, entry_ac)
            e.tensor_tensor(s1, t1, qb["sgn"][:, lo:hi], op=ALU.mult)
            ei = e.tensor_tensor(v3(dst, lo, hi), v3(qb["s1"], lo, hi), sb_, op=ALU.mult)
            ei.inc(po)


def emit_phase_b(ctx, H, x_src, x_dst, K, dst_bf=None, coquant=False):
    p, nc, S = ctx.p, ctx.nc, ctx.S
    F32, BF16, AF, ALU, AX = K.F32, K.BF16, K.AF, K.ALU, K.AX
    NCH = K.NCH
    ld, st, te, ve, ac, po = S["ld"], S["st"], S["te"], S["ve"], S["ac"], S["po"]
    pb = ctx.pb
    MM = p.tensor.matmul
    _dma = lambda q, out, in_, sem="ld": getattr(p, q).dma_start(out=out, in_=in_).inc(S[sem], 16)
    _dmaT = lambda out, in_: p.sync.dma_start_transpose(out, in_).inc(ld, 16)
    chunked = K.chunked

    with ExitStack() as stk:
        sb = lambda n, sh, dt: ctx.sb(stk, n, sh, dt)
        qb = mk_qdb(ctx, stk, K)
        lnb = mk_lnb(ctx, stk, "b", K)
        stkb1 = ExitStack()
        srcb2 = [ctx.sb(stkb1, f"b_src{i}", [128, 2816], F32) for i in range(2)]
        dstb2 = [ctx.sb(stkb1, f"b_dst{i}", [128, 2816], BF16) for i in range(2)]
        tiles = []
        if not coquant:
            for t in range(32):
                tiles.append((1024, x_src[t * 128:(t + 1) * 128, :], H["qdx"][t * 128:(t + 1) * 128, :]))
        wlist = (("wd", "qwd", 8, 2816),) if coquant else (
            ("wg", "qwg", 22, 1024), ("wu", "qwu", 22, 1024), ("wd", "qwd", 8, 2816))
        for wname, qname, nch, Wc in wlist:
            for ch in range(nch):
                tiles.append((Wc, H[wname][ch * 128:(ch + 1) * 128, :],
                              H[qname][ch * 128:(ch + 1) * 128, :]))
        buf_done = [(0, 0, 0, 0), (0, 0, 0, 0)]
        ld_cnt = {}

        def b1_load(idx):
            W, load_in, _ = tiles[idx]
            bi = idx % 2
            sl = "lda" if bi == 0 else "ldb"
            dn = buf_done[bi]
            if dn[0]:
                p.sync.wait(ve, dn[0])
                p.sync.wait(po, dn[1])
                p.sync.wait(ac, dn[2])
                p.sync.wait(st, dn[3])
            _dma("sync", srcb2[bi][:, 0:W], load_in, sl)
            ld_cnt[idx] = S[sl].count

        b1_load(0)
        for idx in range(len(tiles)):
            W, _, store_out = tiles[idx]
            bi = idx % 2
            if idx + 1 < len(tiles):
                b1_load(idx + 1)
            for e in ("vector", "gpsimd", "scalar"):
                getattr(p, e).wait(S["lda" if bi == 0 else "ldb"], ld_cnt[idx])
                getattr(p, e).wait(st, buf_done[bi][3])
            emit_qd(ctx, K, qb, W, srcb2[bi], dstb2[bi])
            p.gpsimd.wait(ve)
            _dma("gpsimd", store_out, dstb2[bi][:, 0:W], "st")
            buf_done[bi] = (S["ve"].count, S["po"].count, S["ac"].count, S["st"].count)
        ctx.barrier()
        stkb1.close()

        with ExitStack() as stk2:
            sb2 = lambda n, sh, dt: ctx.sb(stk2, n, sh, dt)
            qwgT = sb2("b_qwgT", [128, NCH * 2816], BF16)
            qwuT = sb2("b_qwuT", [128, NCH * 2816], BF16)
            qxT = sb2("b_qxT", [128, NCH * 512], BF16)
            dstb = sb2("b_dstb2", [128, 2816], BF16)
            sg = sb2("b_sg", [128, 1024], F32)
            hbuf = sb2("b_h", [128, 2816], F32)
            for c in range(NCH):
                _dmaT(qwgT[:, c * 2816:(c + 1) * 2816], H["qwg"][:, c * 128:(c + 1) * 128])
                _dmaT(qwuT[:, c * 2816:(c + 1) * 2816], H["qwu"][:, c * 128:(c + 1) * 128])
            for g in range(8):
                p.sync.wait(te)
                for c in range(NCH):
                    _dmaT(qxT[:, c * 512:(c + 1) * 512],
                          H["qdx"][g * 512:(g + 1) * 512, c * 128:(c + 1) * 128])
                for tl in range(4):
                    con_done = [(0, 0), (0, 0)]
                    for nib, (ib, bw) in enumerate(IBLOCKS):
                        jp = nib % 2
                        psG, psU = pb[2 * jp], pb[2 * jp + 1]
                        p.tensor.wait(ld)
                        if con_done[jp][0]:
                            p.tensor.wait(ac, con_done[jp][0])
                            p.tensor.wait(ve, con_done[jp][1])
                        for cc in range(NCH):
                            MM(psG[:, 0:bw], qxT[:, cc * 512 + tl * 128: cc * 512 + tl * 128 + 128],
                               qwgT[:, cc * 2816 + ib: cc * 2816 + ib + bw], start=(cc == 0), stop=(cc == NCH - 1))
                        for cc in range(NCH):
                            mm = MM(psU[:, 0:bw], qxT[:, cc * 512 + tl * 128: cc * 512 + tl * 128 + 128],
                                    qwuT[:, cc * 2816 + ib: cc * 2816 + ib + bw], start=(cc == 0),
                                    stop=(cc == NCH - 1))
                        mm.inc(te)
                        p.scalar.wait(te)
                        p.scalar.wait(po)
                        p.scalar.wait(ve)
                        p.scalar.activation(sg[:, jp * 512: jp * 512 + bw], psG[:, 0:bw], AF.Silu).inc(ac)
                        p.vector.wait(ac)
                        p.vector.wait(po)
                        p.vector.tensor_tensor(hbuf[:, ib:ib + bw], sg[:, jp * 512: jp * 512 + bw], psU[:, 0:bw],
                                               op=ALU.mult).inc(ve)
                        con_done[jp] = (S["ac"].count, S["ve"].count)
                    for e in ("vector", "gpsimd", "scalar"):
                        getattr(p, e).wait(ve)
                        getattr(p, e).wait(st)
                    emit_qd(ctx, K, qb, 2816, hbuf, dstb)
                    p.gpsimd.wait(ve)
                    tt = g * 4 + tl
                    _dma("gpsimd", H["qdh"][tt * 128:(tt + 1) * 128, :], dstb[:, 0:2816], "st")
        ctx.barrier()

        with ExitStack() as stk3:
            sb3 = lambda n, sh, dt: ctx.sb(stk3, n, sh, dt)
            qwdT = sb3("b_qwdT", [128, 22 * 1024], BF16)
            qhT = sb3("b_qhT", [128, 22 * 512], BF16)
            x1t = sb3("b_x1t", [128, 1024], F32)
            zs = sb3("b_zs", [128, 1024], F32)
            zbf = sb3("b_zbf", [128, 1024], BF16)
            for ch in range(22):
                _dmaT(qwdT[:, ch * 1024:(ch + 1) * 1024], H["qwd"][:, ch * 128:(ch + 1) * 128])
            bank_done = [0, 0]
            for g in range(8):
                p.sync.wait(te)
                for ch in range(22):
                    _dmaT(qhT[:, ch * 512:(ch + 1) * 512],
                          H["qdh"][g * 512:(g + 1) * 512, ch * 128:(ch + 1) * 128])
                for tl in range(4):
                    tt = g * 4 + tl
                    jp = tt % 2
                    psD0, psD1 = pb[2 * jp], pb[2 * jp + 1]
                    p.sync.wait(ve)
                    _dma("sync", x1t[:], x_src[tt * 128:(tt + 1) * 128, :], "ld")
                    p.tensor.wait(ld)
                    if tt >= 2:
                        p.tensor.wait(ve, bank_done[jp])
                    for ch in range(22):
                        MM(psD0[:, 0:512], qhT[:, ch * 512 + tl * 128: ch * 512 + tl * 128 + 128],
                           qwdT[:, ch * 1024: ch * 1024 + 512], start=(ch == 0), stop=(ch == 21))
                    for ch in range(22):
                        mm = MM(psD1[:, 0:512], qhT[:, ch * 512 + tl * 128: ch * 512 + tl * 128 + 128],
                                qwdT[:, ch * 1024 + 512: ch * 1024 + 1024], start=(ch == 0),
                                stop=(ch == 21))
                    mm.inc(te)
                    p.vector.wait(te)
                    p.vector.wait(ld)
                    p.vector.wait(st)
                    p.vector.tensor_tensor(zs[:, 0:512], psD0[:, 0:512], x1t[:, 0:512], op=ALU.add)
                    p.vector.tensor_tensor(zs[:, 512:1024], psD1[:, 0:512], x1t[:, 512:1024],
                                           op=ALU.add).inc(ve)
                    bank_done[jp] = S["ve"].count
                    emit_ln(ctx, K, zs, lnb, out_bf=zbf if dst_bf is not None else None)
                    p.gpsimd.wait(ve)
                    _dma("gpsimd", x_dst[tt * 128:(tt + 1) * 128, :], zs[:], "st")
                    if dst_bf is not None:
                        _dma("gpsimd", dst_bf[tt * 128:(tt + 1) * 128, :], zbf[:], "st")
        p.sync.wait(st)


def decl_b(nc, H, K):
    F32, BF16 = K.F32, K.BF16
    def di(name, shape, dt):
        H[name] = nc.dram_tensor(name, list(shape), dt, kind="ExternalInput").ap()
    def ds(name, shape, dt):
        H[name] = nc.dram_tensor(name, list(shape), dt, kind="Internal").ap()
    di("wg", (2816, 1024), F32)
    di("wu", (2816, 1024), F32)
    di("wd", (1024, 2816), F32)
    ds("qdx", (4096, 1024), BF16)
    ds("qdh", (4096, 2816), BF16)
    ds("qwg", (2816, 1024), BF16)
    ds("qwu", (2816, 1024), BF16)
    ds("qwd", (1024, 2816), BF16)
    return H


def host_inputs_b(inp):
    f32 = np.float32
    return {
        "wg": np.ascontiguousarray(np.asarray(inp["L0_ffn_w_gate"], f32)),
        "wu": np.ascontiguousarray(np.asarray(inp["L0_ffn_w_up"], f32)),
        "wd": np.ascontiguousarray(np.asarray(inp["L0_ffn_w_down"], f32)),
    }




def emit_phase_c(ctx, H, x_src, x_src_bf, x_dst, K, dst_bf=None):
    p, nc, S = ctx.p, ctx.nc, ctx.S
    F32, BF16, AF, ALU, AX = K.F32, K.BF16, K.AF, K.ALU, K.AX
    NCH, T = K.NCH, K.T
    ld, st, te, ve, ac, po, ld2 = (S["ld"], S["st"], S["te"], S["ve"], S["ac"], S["po"], S["ld2"])
    pb, pbt, pbx = ctx.pb, ctx.pbt, ctx.pbx
    MM = p.tensor.matmul
    _dma = lambda q, out, in_, sem="ld": getattr(p, q).dma_start(out=out, in_=in_).inc(S[sem], 16)
    _dmaT = lambda out, in_: p.sync.dma_start_transpose(out, in_).inc(ld, 16)
    chunked = K.chunked

    with ExitStack() as stk:
        sb = lambda n, sh, dt: ctx.sb(stk, n, sh, dt)
        lnb = mk_lnb(ctx, stk, "c", K)
        onesS = sb("c_ones", [1, 512], BF16)
        ustriS = sb("c_ustri", [128, 128], BF16)
        identS = sb("c_ident", [128, 128], BF16)
        gnwS = sb("c_gnw", [128, 256], F32)
        skw2S = sb("c_skw2", [16, 256], BF16)
        svw2S = sb("c_svw2", [16, 256], BF16)
        skb2S = sb("c_skb2", [1, 256], BF16)
        svb2S = sb("c_svb2", [1, 256], BF16)
        sk1T = sb("c_sk1T", [16, T], BF16)
        sv1T = sb("c_sv1T", [16, T], BF16)
        wkTs = sb("c_wkT", [128, 2 * T], BF16)
        wvTs = sb("c_wvT", [128, 2 * T], BF16)
        zkTs = sb("c_zkT", [128, 2 * T], BF16)
        zvTs = sb("c_zvT", [128, 2 * T], BF16)

        p.vector.memset(onesS[:], 1.0)
        _dma("sync", ustriS[:], H["ustri"][:, :])
        _dma("sync", identS[:], H["identbf"][:, :])
        _dma("sync", gnwS[:], H["gnw_bc"][:, :])
        _dma("gpsimd", skw2S[:], H["skw2"][:, :], "st")
        _dma("gpsimd", svw2S[:], H["svw2"][:, :], "st")
        _dma("gpsimd", skb2S[:], H["skb2"][:, :], "st")
        _dma("gpsimd", svb2S[:], H["svb2"][:, :], "st")

        with ExitStack() as s1k:
            sb1 = lambda n, sh, dt: ctx.sb(s1k, n, sh, dt)
            wS = sb1("c_w", [128, NCH * 1024], BF16)
            skw1S = sb1("c_skw1", [128, NCH * 16], BF16)
            svw1S = sb1("c_svw1", [128, NCH * 16], BF16)
            cfm = sb1("c_cfm", [128, T], BF16)
            sfm = sb1("c_sfm", [128, T], BF16)
            ctk = sb1("c_ctk", [128, 128], BF16)
            stk_ = sb1("c_stk", [128, 128], BF16)
            xT = [sb1(f"c_xT{i}", [128, NCH * 512], BF16) for i in range(2)]
            e0 = sb1("c_e0", [128, 512], BF16)
            e1 = sb1("c_e1", [128, 512], BF16)
            t1b = sb1("c_t1", [128, 512], BF16)
            t2b = sb1("c_t2", [128, 512], BF16)
            ktile = sb1("c_ktile", [128, 1024], BF16)
            gtile = sb1("c_gtile", [128, 1024], F32)
            clipb = sb1("c_clip", [128, 512], F32)

            _dma("gpsimd", skw1S[:].rearrange("p (c x) -> p c x", c=NCH), chunked(H["skw1"]), "st")
            _dma("gpsimd", svw1S[:].rearrange("p (c x) -> p c x", c=NCH), chunked(H["svw1"]), "st")
            _dma("sync", cfm[:], H["c1cos"][:, :])
            _dma("sync", sfm[:], H["c1sin"][:, :])

            def load_w(name):
                p.gpsimd.wait(te)
                _dma("gpsimd", wS[:].rearrange("p (c x) -> p c x", c=NCH), chunked(H[name]), "st")

            def xbar_x2(i, tb):
                for c in range(NCH):
                    _dmaT(xT[i][:, c * 512:(c + 1) * 512],
                          x_src_bf[tb * 512:(tb + 1) * 512, c * 128:(c + 1) * 128])

            load_w("wq2")
            p.vector.wait(S["st"])
            p.vector.tensor_scalar_mul(wS[:], wS[:], 1.0 / 16.0).inc(ve)
            p.tensor.wait(ve)
            xT_done = [0, 0]
            q_done = [0, 0]
            for tb in range(8):
                i = tb % 2
                if xT_done[i]:
                    p.sync.wait(te, xT_done[i])
                xbar_x2(i, tb)
                p.tensor.wait(ld)
                p.tensor.wait(S["st"])
                for (w1S, oT) in ((skw1S, sk1T), (svw1S, sv1T)):
                    p.tensor.wait(ac)
                    for cc in range(NCH):
                        MM(pbt[0:16, 0:512], w1S[:, cc * 16:(cc + 1) * 16],
                           xT[i][:, cc * 512:(cc + 1) * 512], start=(cc == 0), stop=(cc == NCH - 1)).inc(te)
                    p.scalar.wait(te)
                    p.scalar.copy(oT[0:16, tb * 512:(tb + 1) * 512], pbt[0:16, 0:512]).inc(ac)
                for h in range(4):
                    hp = h % 2
                    for u in range(2):
                        psQ = pb[2 * hp + u]
                        ic = 2 * h + u
                        if q_done[hp]:
                            p.tensor.wait(ac, q_done[hp])
                        for cc in range(NCH):
                            MM(psQ[:, 0:512], wS[:, cc * 1024 + ic * 128: cc * 1024 + ic * 128 + 128],
                               xT[i][:, cc * 512:(cc + 1) * 512], start=(cc == 0), stop=(cc == NCH - 1)).inc(te)
                    p.scalar.wait(te)
                    p.scalar.wait(ve)
                    p.scalar.copy(e0[:], pb[2 * hp][:, 0:512])
                    p.scalar.copy(e1[:], pb[2 * hp + 1][:, 0:512]).inc(ac)
                    q_done[hp] = S["ac"].count
                    cs = cfm[:, tb * 512:(tb + 1) * 512]
                    sn = sfm[:, tb * 512:(tb + 1) * 512]
                    p.vector.wait(ac)
                    p.vector.wait(st)
                    p.vector.tensor_tensor(t1b[:], e0[:], cs, op=ALU.mult)
                    p.vector.tensor_tensor(t2b[:], e1[:], sn, op=ALU.mult)
                    p.vector.tensor_tensor(t1b[:], t1b[:], t2b[:], op=ALU.subtract)
                    p.vector.tensor_tensor(t2b[:], e1[:], cs, op=ALU.mult)
                    p.vector.tensor_tensor(e1[:], e0[:], sn, op=ALU.mult)
                    p.vector.tensor_tensor(t2b[:], t2b[:], e1[:], op=ALU.add).inc(ve)
                    p.gpsimd.wait(ve)
                    _dma("gpsimd", H["qT_hbm"][(2 * h) * 128:(2 * h + 1) * 128, tb * 512:(tb + 1) * 512],
                         t1b[:], "st")
                    _dma("gpsimd", H["qT_hbm"][(2 * h + 1) * 128:(2 * h + 2) * 128, tb * 512:(tb + 1) * 512],
                         t2b[:], "st")
                xT_done[i] = te.count

            for (wname, dst, isf32, rope) in (("wk2", H["k_hbm"], False, True),
                                              ("wv2", H["v_hbm"], False, False),
                                              ("wg2", H["g_hbm"], True, False)):
                load_w(wname)
                xT_done = [0, 0]
                kv_done = [0, 0, 0, 0]
                for tb in range(8):
                    i = tb % 2
                    if xT_done[i]:
                        p.sync.wait(te, xT_done[i])
                    xbar_x2(i, tb)
                    p.tensor.wait(ld)
                    p.tensor.wait(S["st"])
                    for tl in range(4):
                        tt = tb * 4 + tl
                        stage = gtile if isf32 else ktile
                        for ih in range(2):
                            bi = (2 * tl + ih) % 4
                            psV = pb[2 + bi]
                            if kv_done[bi]:
                                p.tensor.wait(ac, kv_done[bi])
                            for cc in range(NCH):
                                MM(psV[:, 0:512],
                                   xT[i][:, cc * 512 + tl * 128: cc * 512 + tl * 128 + 128],
                                   wS[:, cc * 1024 + ih * 512: cc * 1024 + (ih + 1) * 512],
                                   start=(cc == 0), stop=(cc == NCH - 1)).inc(te)
                            p.scalar.wait(te)
                            p.scalar.wait(st)
                            p.scalar.wait(ve)
                            p.scalar.copy(stage[:, ih * 512:(ih + 1) * 512], psV[:, 0:512]).inc(ac)
                            kv_done[bi] = S["ac"].count
                        if rope:
                            _dma("sync", ctk[:], H["c1cosTok"][tt * 128:(tt + 1) * 128, :])
                            _dma("sync", stk_[:], H["c1sinTok"][tt * 128:(tt + 1) * 128, :])
                            kv = ktile[:].rearrange("p (h u d) -> p h u d", h=4, u=2)
                            h4 = lambda ap: ap[:].rearrange("p (h d) -> p h d", h=4)
                            cb = ctk[:].unsqueeze(1).broadcast_to([128, 4, 128])
                            sbb = stk_[:].unsqueeze(1).broadcast_to([128, 4, 128])
                            p.vector.wait(ac)
                            p.vector.wait(ld)
                            p.vector.tensor_tensor(h4(t1b), kv[:, :, 0, :], cb, op=ALU.mult)
                            p.vector.tensor_tensor(h4(t2b), kv[:, :, 1, :], sbb, op=ALU.mult)
                            p.vector.tensor_tensor(t1b[:], t1b[:], t2b[:], op=ALU.subtract)
                            p.vector.tensor_tensor(h4(t2b), kv[:, :, 1, :], cb, op=ALU.mult)
                            p.vector.tensor_tensor(h4(e0), kv[:, :, 0, :], sbb, op=ALU.mult)
                            p.vector.tensor_tensor(t2b[:], t2b[:], e0[:], op=ALU.add)
                            p.vector.tensor_copy(kv[:, :, 0, :], h4(t1b))
                            p.vector.tensor_copy(kv[:, :, 1, :], h4(t2b)).inc(ve)
                            p.gpsimd.wait(ve)
                        else:
                            p.gpsimd.wait(ac)
                        _dma("gpsimd", dst[tt * 128:(tt + 1) * 128, :], stage[:], "st")
                    xT_done[i] = te.count

            for tt in range(32):
                for (w2S, b2S, dst) in ((skw2S, skb2S, H["wk_hbm"]), (svw2S, svb2S, H["wv_hbm"])):
                    p.tensor.wait(ve)
                    p.tensor.wait(ac)
                    MM(pb[0][:, 0:256], sk1T[0:16, tt * 128:(tt + 1) * 128] if w2S is skw2S else
                       sv1T[0:16, tt * 128:(tt + 1) * 128], w2S[:], start=True, stop=False)
                    MM(pb[0][:, 0:256], onesS[0:1, 0:128], b2S[:], start=False, stop=True).inc(te)
                    p.vector.wait(te)
                    p.vector.tensor_scalar(clipb[:, 0:256], pb[0][:, 0:256], 32.0, -32.0,
                                           op0=ALU.min, op1=ALU.max).inc(ve)
                    p.scalar.wait(ve)
                    p.scalar.wait(st)
                    p.scalar.activation(ktile[:, 0:256], clipb[:, 0:256], AF.Exp).inc(ac)
                    p.gpsimd.wait(ac)
                    _dma("gpsimd", dst[tt * 128:(tt + 1) * 128, :], ktile[:, 0:256], "st")
            for u in range(2):
                for (w2S, b2S, s1S, wT) in ((skw2S, skb2S, sk1T, wkTs), (svw2S, svb2S, sv1T, wvTs)):
                    for tb in range(8):
                        p.tensor.wait(ve)
                        MM(pb[0][:, 0:512], w2S[0:16, u * 128:(u + 1) * 128],
                           s1S[0:16, tb * 512:(tb + 1) * 512], start=True, stop=False)
                        MM(pb[0][:, 0:512], b2S[0:1, u * 128:(u + 1) * 128],
                           onesS[0:1, 0:512], start=False, stop=True).inc(te)
                        p.vector.wait(te)
                        p.vector.tensor_scalar(clipb[:, 0:512], pb[0][:, 0:512], 32.0, -32.0,
                                               op0=ALU.min, op1=ALU.max).inc(ve)
                        p.scalar.wait(ve)
                        p.scalar.activation(wT[:, u * T + tb * 512: u * T + (tb + 1) * 512],
                                            clipb[:, 0:512], AF.Exp).inc(ac)
            p.vector.wait(ac)
            for (wT, zT) in ((wkTs, zkTs), (wvTs, zvTs)):
                for u in range(2):
                    p.vector.tensor_tensor_scan(zT[:, u * T:(u + 1) * T], wT[:, u * T:(u + 1) * T],
                                                wT[:, u * T:(u + 1) * T], 0.0,
                                                op0=ALU.add, op1=ALU.bypass).inc(ve)
        ctx.barrier()

        with ExitStack() as s2k:
            sb2 = lambda n, sh, dt: ctx.sb(s2k, n, sh, dt)
            qTh = sb2("c_qTh", [128, 2 * T], BF16)
            kTh = sb2("c_kTh", [128, 2 * T], BF16)
            ktokh = sb2("c_ktokh", [128, 32 * 256], BF16)
            vh = sb2("c_vh", [128, 32 * 256], BF16)
            wkh = sb2("c_wkh", [128, 32 * 64], BF16)
            wvh = sb2("c_wvh", [128, 32 * 64], BF16)
            wvTh = sb2("c_wvTh", [128, 32 * 128], BF16)
            g2b = [sb2(f"c_g2b{i}", [128, 256], F32) for i in range(2)]
            slocb = sb2("c_sloc", [128, 128], BF16)
            abf = sb2("c_abf", [128, 128], BF16)
            pTb = sb2("c_pT", [128, 128], BF16)
            Skbf = [sb2(f"c_Skbf{i}", [128, 128], BF16) for i in range(2)]
            Svbf = [sb2(f"c_Svbf{i}", [128, 256], BF16) for i in range(2)]
            rk = sb2("c_rk", [128, 64], F32)
            rv = sb2("c_rv", [128, 64], F32)
            lg = sb2("c_lg", [128, 64], F32)
            mx = sb2("c_mx", [128, 1], F32)
            negmx = sb2("c_negmx", [128, 1], F32)
            es = sb2("c_es", [128, 64], F32)
            se = sb2("c_se", [128, 1], F32)
            rs = sb2("c_rs", [128, 1], F32)
            p1 = sb2("c_p1", [128, 64], F32)
            pbf = sb2("c_pbf", [128, 64], BF16)
            odum = sb2("c_odum", [128, 256], BF16)
            osum = sb2("c_osum", [128, 1], F32)
            ro = sb2("c_ro", [128, 1], F32)
            sgm = sb2("c_sgm", [128, 256], F32)
            gg = sb2("c_gg", [128, 256], F32)
            on_s = sb2("c_on", [128, 256], F32)
            on2 = sb2("c_on2", [128, 256], F32)
            obft = sb2("c_obft", [128, 256], BF16)

            for h in range(4):
                ctx.barrier()
                u2 = h // 2
                o2 = (h % 2) * 64
                _dma("sync", qTh[:].rearrange("p (c t) -> p c t", c=2),
                     H["qT_hbm"][h * 256:(h + 1) * 256, :].rearrange("(c p) t -> p c t", p=128))
                for u in range(2):
                    _dmaT(kTh[:, u * T:(u + 1) * T], H["k_hbm"][:, h * 256 + u * 128: h * 256 + (u + 1) * 128])
                _dma("sync", ktokh[:].rearrange("p (n c) -> p n c", n=32),
                     H["k_hbm"].rearrange("(n p) c -> p n c", p=128)[:, :, h * 256:(h + 1) * 256])
                _dma("sync", vh[:].rearrange("p (n c) -> p n c", n=32),
                     H["v_hbm"].rearrange("(n p) c -> p n c", p=128)[:, :, h * 256:(h + 1) * 256])
                _dma("sync", wkh[:].rearrange("p (n c) -> p n c", n=32),
                     H["wk_hbm"].rearrange("(n p) c -> p n c", p=128)[:, :, h * 64:(h + 1) * 64])
                _dma("sync", wvh[:].rearrange("p (n c) -> p n c", n=32),
                     H["wv_hbm"].rearrange("(n p) c -> p n c", p=128)[:, :, h * 64:(h + 1) * 64])
                _dmaT(wvTh[:, :], H["wv_hbm"][:, (h // 2) * 128:(h // 2 + 1) * 128])
                gld = [0, 0]
                a_E = v_sloc = v_rcp = v_pbf = v_abf = 0
                a_pT = v_ro = v_on = v_obf = a_sq = a_sg = 0
                t_U = t_S = t_L = t_T = t_A = t_O = 0
                for c in range(32):
                    j = c % 2
                    jm = (c - 1) % 2
                    if gld[j]:
                        p.sync.wait(po, gld[j])
                    _dma("sync", g2b[j][:], H["g_hbm"][c * 128:(c + 1) * 128, h * 256:(h + 1) * 256],
                         "lda" if j == 0 else "ldb")
                    gcnt = S["lda" if j == 0 else "ldb"].count
                    if c == 0:
                        p.tensor.wait(ld)
                    else:
                        p.tensor.wait(ac, a_E)
                    MM(pb[2][:, 0:64], ktokh[:, c * 256: c * 256 + 128], wkh[:, c * 64:(c + 1) * 64],
                       start=(c == 0), stop=(c == 31), skip_group_check=True)
                    MM(pb[3][:, 0:64], ktokh[:, c * 256 + 128: c * 256 + 256], wkh[:, c * 64:(c + 1) * 64],
                       start=(c == 0), stop=(c == 31), skip_group_check=True)
                    MM(pb[4][0:64, 0:256], wvh[:, c * 64:(c + 1) * 64], vh[:, c * 256:(c + 1) * 256],
                       start=(c == 0), stop=(c == 31), skip_group_check=True).inc(te)
                    t_U = te.count
                    if v_sloc:
                        p.tensor.wait(ve, v_sloc)
                    for u in range(2):
                        MM(pb[0][:, 0:128], kTh[:, u * T + c * 128: u * T + c * 128 + 128],
                           qTh[:, u * T + c * 128: u * T + c * 128 + 128],
                           start=(u == 0), stop=(u == 1)).inc(te)
                    t_S = te.count
                    if v_rcp:
                        p.tensor.wait(ve, v_rcp)
                    MM(pbx[:, 0:64], zkTs[o2:o2 + 64, u2 * T + c * 128: u2 * T + c * 128 + 128],
                       identS[o2:o2 + 64, o2:o2 + 64], start=True, stop=True, is_transpose=True)
                    MM(pbx[:, 64:128], zvTs[o2:o2 + 64, u2 * T + c * 128: u2 * T + c * 128 + 128],
                       identS[o2:o2 + 64, o2:o2 + 64], start=True, stop=True, is_transpose=True).inc(te)
                    p.scalar.wait(te, t_U)
                    p.scalar.copy(Skbf[j][:, 0:64], pb[2][:, 0:64])
                    p.scalar.copy(Skbf[j][:, 64:128], pb[3][:, 0:64])
                    p.scalar.copy(Svbf[j][o2:o2 + 64, :], pb[4][0:64, 0:256]).inc(ac)
                    a_E = S["ac"].count
                    p.vector.wait(te, t_S)
                    p.vector.tensor_tensor(slocb[:], pb[0][:, 0:128], ustriS[:], op=ALU.mult).inc(ve)
                    v_sloc = S["ve"].count
                    p.tensor.wait(ve, v_sloc)
                    if c == 0:
                        MM(pbt[:, 0:64], slocb[:], wkh[:, 0:64], start=True, stop=True).inc(te)
                    else:
                        MM(pbt[:, 0:64], slocb[:], wkh[:, c * 64:(c + 1) * 64], start=True, stop=False)
                        MM(pbt[:, 0:64], qTh[:, 0 * T + c * 128: 0 * T + c * 128 + 128],
                           Skbf[jm][:, 0:64], start=False, stop=False)
                        MM(pbt[:, 0:64], qTh[:, 1 * T + c * 128: 1 * T + c * 128 + 128],
                           Skbf[jm][:, 64:128], start=False, stop=True).inc(te)
                    t_L = te.count
                    p.vector.wait(te, t_L)
                    p.vector.reciprocal(rk[:], pbx[:, 0:64])
                    p.vector.reciprocal(rv[:], pbx[:, 64:128])
                    p.vector.tensor_tensor(lg[:], pbt[:, 0:64], rk[:], op=ALU.mult)
                    p.vector.tensor_reduce(mx[:], lg[:], axis=AX.X, op=ALU.max)
                    p.vector.tensor_scalar_mul(negmx[:], mx[:], -1.0).inc(ve)
                    v_rcp = S["ve"].count
                    p.scalar.wait(ve, v_rcp)
                    p.scalar.activation(es[:], lg[:], AF.Exp, bias=negmx[:], accum_out=se[:]).inc(ac)
                    a_exp = S["ac"].count
                    p.vector.wait(ac, a_exp)
                    p.vector.reciprocal(rs[:], se[:])
                    p.vector.tensor_scalar(p1[:], es[:], rs[:], None, op0=ALU.mult)
                    p.vector.tensor_tensor(pbf[:], p1[:], rv[:], op=ALU.mult).inc(ve)
                    v_pbf = S["ve"].count
                    p.tensor.wait(ve, v_pbf)
                    MM(pbx[0:64, 128:256], pbf[:], identS[:], start=True, stop=True,
                       is_transpose=True).inc(te)
                    t_T = te.count
                    p.scalar.wait(te, t_T)
                    p.scalar.copy(pTb[o2:o2 + 64, :], pbx[0:64, 128:256]).inc(ac)
                    a_pT = S["ac"].count
                    p.tensor.wait(ac, a_pT)
                    MM(pb[1][:, 0:128], wvTh[o2:o2 + 64, c * 128:(c + 1) * 128], pTb[o2:o2 + 64, :],
                       start=True, stop=True).inc(te)
                    t_A = te.count
                    p.vector.wait(te, t_A)
                    p.vector.tensor_tensor(abf[:], pb[1][:, 0:128], ustriS[:], op=ALU.mult).inc(ve)
                    v_abf = S["ve"].count
                    p.tensor.wait(ve, v_abf)
                    if c == 0:
                        MM(pb[5][:, 0:256], abf[:], vh[:, 0:256], start=True, stop=True).inc(te)
                    else:
                        MM(pb[5][:, 0:256], abf[:], vh[:, c * 256:(c + 1) * 256], start=True, stop=False)
                        MM(pb[5][:, 0:256], pTb[o2:o2 + 64, :], Svbf[jm][o2:o2 + 64, :],
                           start=False, stop=True).inc(te)
                    t_O = te.count
                    p.scalar.wait(te, t_O)
                    if v_ro:
                        p.scalar.wait(ve, v_ro)
                    p.scalar.activation(odum[:], pb[5][:, 0:256], AF.Square, accum_out=osum[:]).inc(ac)
                    a_sq = S["ac"].count
                    p.vector.wait(ac, a_sq)
                    p.vector.tensor_scalar(ro[:], osum[:], 1.0 / 256.0, 1e-5, op0=ALU.mult, op1=ALU.add).inc(ve)
                    v_ro = S["ve"].count
                    p.scalar.wait(ve, v_ro)
                    p.scalar.activation(ro[:], ro[:], AF.Sqrt)
                    p.scalar.wait(S["lda" if j == 0 else "ldb"], gcnt)
                    if gld[jm]:
                        p.scalar.wait(po, gld[jm])
                    p.scalar.activation(sgm[:], g2b[j][:], AF.Sigmoid).inc(ac)
                    a_sg = S["ac"].count
                    p.gpsimd.wait(ac, a_sg)
                    if v_obf:
                        p.gpsimd.wait(ve, v_obf)
                    p.gpsimd.tensor_tensor(gg[:], g2b[j][:], sgm[:], op=ALU.mult).inc(po)
                    gld[j] = S["po"].count
                    p.vector.wait(ac, a_sg)
                    p.vector.reciprocal(ro[:], ro[:])
                    p.vector.tensor_scalar(on_s[:], pb[5][:, 0:256], ro[:], None, op0=ALU.mult)
                    p.vector.wait(po, gld[j])
                    p.vector.tensor_tensor(on2[:], on_s[:], gnwS[:], op=ALU.mult)
                    p.vector.wait(st)
                    p.vector.tensor_tensor(obft[:], on2[:], gg[:], op=ALU.mult).inc(ve)
                    v_obf = S["ve"].count
                    p.gpsimd.wait(ve, v_obf)
                    _dma("gpsimd", H["o2_hbm"][c * 128:(c + 1) * 128, h * 256:(h + 1) * 256], obft[:], "st")
        ctx.barrier()

        with ExitStack() as s3k:
            sb3 = lambda n, sh, dt: ctx.sb(s3k, n, sh, dt)
            wo2S = sb3("c_wo2", [128, NCH * 1024], BF16)
            oTg = sb3("c_oTg", [128, NCH * 512], BF16)
            x2t = sb3("c_x2t", [128, 1024], F32)
            zs = sb3("c_zs", [128, 1024], F32)
            zbf = sb3("c_zbf", [128, 1024], BF16)
            _dma("gpsimd", wo2S[:].rearrange("p (c x) -> p c x", c=NCH), chunked(H["wo2"]), "st")
            for g in range(8):
                p.sync.wait(te)
                for dc in range(NCH):
                    _dmaT(oTg[:, dc * 512:(dc + 1) * 512],
                          H["o2_hbm"][g * 512:(g + 1) * 512, dc * 128:(dc + 1) * 128])
                for tl in range(4):
                    tt = g * 4 + tl
                    p.sync.wait(ve)
                    _dma("sync", x2t[:], x_src[tt * 128:(tt + 1) * 128, :], "ld")
                    p.tensor.wait(ld)
                    p.tensor.wait(S["st"])
                    p.tensor.wait(ve)
                    for ih in range(2):
                        for dc in range(NCH):
                            mm = MM(pb[ih][:, 0:512], oTg[:, dc * 512 + tl * 128: dc * 512 + tl * 128 + 128],
                                    wo2S[:, dc * 1024 + ih * 512: dc * 1024 + (ih + 1) * 512],
                                    start=(dc == 0), stop=(dc == NCH - 1))
                    mm.inc(te)
                    p.vector.wait(te)
                    p.vector.wait(ld)
                    p.vector.wait(st)
                    p.vector.tensor_tensor(zs[:, 0:512], pb[0][:, 0:512], x2t[:, 0:512], op=ALU.add)
                    p.vector.tensor_tensor(zs[:, 512:1024], pb[1][:, 0:512], x2t[:, 512:1024],
                                           op=ALU.add).inc(ve)
                    emit_ln(ctx, K, zs, lnb, out_bf=zbf if dst_bf is not None else None)
                    p.gpsimd.wait(ve)
                    _dma("gpsimd", x_dst[tt * 128:(tt + 1) * 128, :], zs[:], "st")
                    if dst_bf is not None:
                        _dma("gpsimd", dst_bf[tt * 128:(tt + 1) * 128, :], zbf[:], "st")
        p.sync.wait(st)


def decl_c(nc, H, K):
    F32, BF16, T = K.F32, K.BF16, K.T
    def di(name, shape, dt):
        if name not in H:
            H[name] = nc.dram_tensor(name, list(shape), dt, kind="ExternalInput").ap()
    def ds(name, shape, dt):
        if name not in H:
            H[name] = nc.dram_tensor(name, list(shape), dt, kind="Internal").ap()
    for n in ("wq2", "wk2", "wv2", "wg2", "wo2"):
        di(n, (1024, 1024), F32)
    di("skw1", (1024, 16), F32)
    di("svw1", (1024, 16), F32)
    di("skw2", (16, 256), F32)
    di("svw2", (16, 256), F32)
    di("skb2", (1, 256), F32)
    di("svb2", (1, 256), F32)
    di("gnw_bc", (128, 256), F32)
    di("c1cos", (128, T), BF16)
    di("c1sin", (128, T), BF16)
    di("c1cosTok", (T, 128), BF16)
    di("c1sinTok", (T, 128), BF16)
    di("ustri", (128, 128), BF16)
    di("identbf", (128, 128), BF16)
    ds("qT_hbm", (1024, T), BF16)
    ds("k_hbm", (T, 1024), BF16)
    ds("v_hbm", (T, 1024), BF16)
    ds("g_hbm", (T, 1024), F32)
    ds("wk_hbm", (T, 256), BF16)
    ds("wv_hbm", (T, 256), BF16)
    ds("o2_hbm", (T, 1024), BF16)
    return H


def host_consts_c(K):
    import ml_dtypes
    BF = ml_dtypes.bfloat16
    T = K.T
    j = np.arange(128, dtype=np.float64)
    inv = 1.0 / (10000.0 ** (j / 128.0))
    ang = np.arange(T, dtype=np.float64)[:, None] * inv[None, :]
    s_ = np.arange(128)[:, None]
    t_ = np.arange(128)[None, :]
    return {"c1cos": np.ascontiguousarray(np.cos(ang).T).astype(np.float32).astype(BF),
            "c1sin": np.ascontiguousarray(np.sin(ang).T).astype(np.float32).astype(BF),
            "c1cosTok": np.ascontiguousarray(np.cos(ang)).astype(np.float32).astype(BF),
            "c1sinTok": np.ascontiguousarray(np.sin(ang)).astype(np.float32).astype(BF),
            "ustri": (s_ <= t_).astype(np.float32).astype(BF),
            "identbf": np.eye(128, dtype=np.float32).astype(BF)}


def host_inputs_c(inp, K):
    f32 = np.float32
    d = {
        "wq2": np.ascontiguousarray(np.asarray(inp["L1_attn_Wq"], f32)),
        "wk2": np.ascontiguousarray(np.asarray(inp["L1_attn_Wk"], f32)),
        "wv2": np.ascontiguousarray(np.asarray(inp["L1_attn_Wv"], f32)),
        "wg2": np.ascontiguousarray(np.asarray(inp["L1_attn_Wg"], f32)),
        "wo2": np.ascontiguousarray(np.asarray(inp["L1_attn_Wo"], f32)),
        "skw1": np.ascontiguousarray(np.asarray(inp["L1_attn_sk_w1"], f32)),
        "svw1": np.ascontiguousarray(np.asarray(inp["L1_attn_sv_w1"], f32)),
        "skw2": np.ascontiguousarray(np.asarray(inp["L1_attn_sk_w2"], f32)),
        "svw2": np.ascontiguousarray(np.asarray(inp["L1_attn_sv_w2"], f32)),
        "skb2": np.asarray(inp["L1_attn_sk_b2"], f32).reshape(1, 256),
        "svb2": np.asarray(inp["L1_attn_sv_b2"], f32).reshape(1, 256),
        "gnw_bc": np.ascontiguousarray(np.broadcast_to(
            np.asarray(inp["L1_attn_gnorm_w"], f32).reshape(1, 256), (128, 256)).copy()),
    }
    d.update(host_consts_c(K))
    return d



AGB = [(0, 512), (512, 512), (1024, 512), (1536, 512), (2048, 512), (2560, 170)]


def emit_phase_d(ctx, H, x_src, x_dst, K, nlayers=2, x_src_bf=None):
    p, nc, S = ctx.p, ctx.nc, ctx.S
    F32, BF16, AF, ALU, AX = K.F32, K.BF16, K.AF, K.ALU, K.AX
    NCH, T = K.NCH, K.T
    ld, st, te, ve, ac, po, ld2 = (S["ld"], S["st"], S["te"], S["ve"], S["ac"], S["po"], S["ld2"])
    pb, pbt, pbx = ctx.pb, ctx.pbt, ctx.pbx
    MM = p.tensor.matmul
    _dma = lambda q, out, in_, sem="ld": getattr(p, q).dma_start(out=out, in_=in_).inc(S[sem], 16)
    _dmaT = lambda out, in_: p.sync.dma_start_transpose(out, in_).inc(ld, 16)
    chunked = K.chunked

    with ExitStack() as stk:
        sb = lambda n, sh, dt: ctx.sb(stk, n, sh, dt)
        lnb = mk_lnb(ctx, stk, "d", K)
        onesS = sb("d_ones", [1, 512], BF16)
        p.vector.memset(onesS[:], 1.0)

        xa, xb = H["xd_a"], H["xd_b"]
        for l in range(nlayers):
            lx_in = x_src if l == 0 else xb
            lx_mid = xa
            lx_out = xb if l == 0 else H["xd_d"]
            sfx = str(l)

            lxbf = (x_src_bf if l == 0 else H["xbbf"]) if x_src_bf is not None else None
            if lxbf is None:
                p.gpsimd.wait(st)
                for i in range(8):
                    _dma("gpsimd", H["h1bf"][i * 512:(i + 1) * 512, 0:512],
                         lx_in[i * 512:(i + 1) * 512, 0:512], "st")
                _dma("gpsimd", H["h1bf"][1:T, 512:1024], lx_in[0:T - 1, 512:1024], "st")
                p.vector.memset(lnb["dumb"][0:1, 0:512], 0.0)
                p.vector.nop().inc(ve)
                p.gpsimd.wait(ve)
                _dma("gpsimd", H["h1bf"][0:1, 512:1024], lnb["dumb"][0:1, 0:512], "st")
                ctx.barrier()

            with ExitStack() as s2k:
                sb2 = lambda n, sh, dt: ctx.sb(s2k, n + sfx, sh, dt)
                wqkvS = sb2("d_wqkv", [128, NCH * 1536], BF16)
                hT = [sb2(f"d_hT{i}", [128, NCH * 512], BF16) for i in range(2)]
                qkvf = sb2("d_qkvf", [128, 1536], F32)
                s2q = sb2("d_s2q", [128, 8], F32)
                rnq = sb2("d_rnq", [128, 8], F32)
                q2f = sb2("d_q2f", [128, 512], F32)
                t1d = sb2("d_t1", [128, 512], F32)
                t2d = sb2("d_t2", [128, 512], F32)
                outb = sb2("d_outb", [128, 512], BF16)
                qsS = sb2("d_qs", [128, 512], F32)
                ksS = sb2("d_ks", [128, 512], F32)
                tcq = sb2("d_tcq", [128, 4 * 64], BF16)
                tsq = sb2("d_tsq", [128, 4 * 64], BF16)
                tck = sb2("d_tck", [128, 4 * 64], BF16)
                tsk = sb2("d_tsk", [128, 4 * 64], BF16)
                _dma("gpsimd", wqkvS[:].rearrange("p (c x) -> p c x", c=NCH), chunked(H["wqkv" + sfx]), "st")
                _dma("sync", qsS[:], H["qs_bc" + sfx][:, :])
                _dma("sync", ksS[:], H["ks_bc" + sfx][:, :])
                if lxbf is not None:
                    p.vector.memset(lnb["dumb"][0:1, 0:512], 0.0)
                    p.vector.nop().inc(ve)
                    p.gpsimd.wait(ve)
                    _dma("gpsimd", H["xshift0"][0:1, :], lnb["dumb"][0:1, 0:512], "st")
                    _dma("gpsimd", H["xshift0"][1:512, :], lxbf[0:511, 512:1024], "st")
                v38 = lambda ap: ap[:].rearrange("p (h j) -> p h j", h=8)
                hT_done = [0, 0]
                d2_ve = 0
                qkv_done = [0, 0]
                for tb in range(8):
                    i = tb % 2
                    if hT_done[i]:
                        p.sync.wait(te, hT_done[i])
                    if d2_ve:
                        p.sync.wait(ve, d2_ve)
                    if lxbf is None:
                        for c in range(NCH):
                            _dmaT(hT[i][:, c * 512:(c + 1) * 512],
                                  H["h1bf"][tb * 512:(tb + 1) * 512, c * 128:(c + 1) * 128])
                    else:
                        if tb == 0:
                            p.sync.wait(st)
                        for c in range(NCH):
                            if c < 4:
                                _dmaT(hT[i][:, c * 512:(c + 1) * 512],
                                      lxbf[tb * 512:(tb + 1) * 512, c * 128:(c + 1) * 128])
                            elif tb == 0:
                                _dmaT(hT[i][:, c * 512:(c + 1) * 512],
                                      H["xshift0"][0:512, (c - 4) * 128:(c - 3) * 128])
                            else:
                                _dmaT(hT[i][:, c * 512:(c + 1) * 512],
                                      lxbf[tb * 512 - 1:(tb + 1) * 512 - 1, c * 128:(c + 1) * 128])
                    for nm, tbl in (("dcq", tcq), ("dsq", tsq), ("dck", tck), ("dsk", tsk)):
                        _dma("sync", tbl[:].rearrange("p (a j) -> p a j", a=4),
                             H[nm].rearrange("(a p) j -> p a j", p=128)[:, tb * 4:(tb + 1) * 4, :])
                    p.tensor.wait(ld)
                    p.tensor.wait(S["st"])
                    for tl in range(4):
                        tt = tb * 4 + tl
                        jp = tt % 2
                        if tt >= 2:
                            p.tensor.wait(ac, qkv_done[jp])
                        for ih in range(3):
                            for cc in range(NCH):
                                MM(pb[3 * jp + ih][:, 0:512],
                                   hT[i][:, cc * 512 + tl * 128: cc * 512 + tl * 128 + 128],
                                   wqkvS[:, cc * 1536 + ih * 512: cc * 1536 + (ih + 1) * 512],
                                   start=(cc == 0), stop=(cc == NCH - 1)).inc(te)
                        p.scalar.wait(te)
                        p.scalar.wait(ve)
                        for ih in range(3):
                            p.scalar.copy(qkvf[:, ih * 512:(ih + 1) * 512], pb[3 * jp + ih][:, 0:512]).inc(ac)
                        qkv_done[jp] = S["ac"].count
                        p.vector.wait(ac)
                        p.vector.wait(st)
                        p.vector.tensor_copy(outb[:], qkvf[:, 1024:1536]).inc(ve)
                        p.gpsimd.wait(ve)
                        _dma("gpsimd", H["vd_hbm"][tt * 128:(tt + 1) * 128, :], outb[:], "st")
                        for (off, scS, tc, ts_, dhbm) in ((0, qsS, tcq, tsq, H["qd_hbm"]),
                                                          (512, ksS, tck, tsk, H["kd_hbm"])):
                            qv = qkvf[:, off:off + 512]
                            p.vector.wait(st)
                            p.vector.wait(ac)
                            p.vector.tensor_tensor(t1d[:], qv, qv, op=ALU.mult)
                            p.vector.tensor_reduce(s2q[:], t1d[:].rearrange("p (h j) -> p h j", h=8),
                                                   axis=AX.X, op=ALU.add).inc(ve)
                            p.scalar.wait(ve)
                            p.scalar.sqrt(rnq[:], s2q[:]).inc(ac)
                            p.vector.wait(ac)
                            p.vector.tensor_single_scalar(rnq[:], rnq[:], 1e-12, op=ALU.max)
                            p.vector.reciprocal(rnq[:], rnq[:])
                            p.vector.tensor_tensor(v38(q2f), v38(qv),
                                                   rnq[:].unsqueeze(2).broadcast_to([128, 8, 64]), op=ALU.mult)
                            p.vector.tensor_tensor(q2f[:], q2f[:], scS[:], op=ALU.mult)
                            tcb = tc[:].rearrange("p (a j) -> p a j", a=4)[:, tl, :].unsqueeze(1).broadcast_to([128, 8, 64])
                            tsb = ts_[:].rearrange("p (a j) -> p a j", a=4)[:, tl, :].unsqueeze(1).broadcast_to([128, 8, 64])
                            p.vector.tensor_tensor(v38(t1d), v38(q2f), tcb, op=ALU.mult)
                            q3 = q2f[:].rearrange("p (h u j) -> p h u j", h=8, u=2)
                            t23 = t2d[:].rearrange("p (h u j) -> p h u j", h=8, u=2)
                            tsb2 = ts_[:].rearrange("p (a j) -> p a j", a=4)[:, tl, :]\
                                .rearrange("p (u j) -> p u j", u=2).unsqueeze(1).broadcast_to([128, 8, 2, 32])
                            p.vector.tensor_tensor(t23[:, :, 0, :], q3[:, :, 1, :], tsb2[:, :, 0, :], op=ALU.mult)
                            p.vector.tensor_tensor(t23[:, :, 1, :], q3[:, :, 0, :], tsb2[:, :, 1, :], op=ALU.mult)
                            p.vector.tensor_tensor(outb[:], t1d[:], t2d[:], op=ALU.add).inc(ve)
                            dbg = getattr(ctx, 'dbgd', None)
                            if dbg is not None and tt == 0 and l == 0 and off == 0:
                                p.gpsimd.wait(ve)
                                _dma("gpsimd", dbg["dq2f"][:, :], q2f[:], "st")
                                _dma("gpsimd", dbg["ds2q"][:, :], s2q[:], "st")
                                _dma("gpsimd", dbg["dt1d"][:, :], t1d[:], "st")
                                _dma("gpsimd", dbg["dt2d"][:, :], t2d[:], "st")
                                _dma("gpsimd", dbg["dqkvf"][:, :], qkvf[:], "st")
                            p.gpsimd.wait(ve)
                            _dma("gpsimd", dhbm[tt * 128:(tt + 1) * 128, :], outb[:], "st")
                    hT_done[i] = te.count
                    d2_ve = ve.count
            ctx.barrier()

            with ExitStack() as s3k:
                sb3 = lambda n, sh, dt: ctx.sb(s3k, n + sfx, sh, dt)
                woS = sb3("d_wo", [128, 4 * 1024], BF16)
                qTw = [sb3(f"d_qTw{i}", [128, 4 * 512], BF16) for i in range(2)]
                kTw = [sb3(f"d_kTw{i}", [128, 4 * 1024], BF16) for i in range(2)]
                vw = [sb3(f"d_vw{i}", [128, 8 * 512], BF16) for i in range(2)]
                oTw = sb3("d_oTw", [128, 4 * 512], BF16)
                dmaskS = sb3("d_dmask", [128, 8 * 512], BF16)
                ones128d = sb3("d_ones128", [128, 1], BF16)
                sfd = [sb3(f"d_sf{i}", [128, 512], F32) for i in range(2)]
                pbfd = [sb3(f"d_pbf{i}", [128, 512], BF16) for i in range(2)]
                rld = sb3("d_rl", [1, 512], F32)
                rlbfd = sb3("d_rlbf", [1, 512], BF16)
                rbd = sb3("d_rb", [64, 512], F32)
                xt = sb3("d_xt", [128, 1024], F32)
                zsd = sb3("d_zs", [128, 1024], F32)
                _dma("gpsimd", woS[:].rearrange("p (c x) -> p c x", c=4),
                     H["wo" + sfx].rearrange("(c p) x -> p c x", p=128), "st")
                _dma("sync", dmaskS[:], H["dmask"][:, :])
                p.vector.memset(kTw[0][:], 0.0)
                p.vector.memset(ones128d[:], 1.0)
                p.vector.memset(vw[0][:], 0.0).inc(ve)
                p.sync.wait(ve)
                pf_done = [0, 0]
                stt_done = [0, 0]
                wb_done = [0, 0]
                ld3 = {}

                def d3_load(w):
                    bw = w % 2
                    sl = "lda" if bw == 0 else "ldb"
                    if wb_done[bw]:
                        p.sync.wait(te, wb_done[bw])
                    for c in range(4):
                        p.sync.dma_start_transpose(qTw[bw][:, c * 512:(c + 1) * 512],
                              H["qd_hbm"][w * 512:(w + 1) * 512, c * 128:(c + 1) * 128]).inc(S[sl], 16)
                    lo = (w - 1) * 512
                    for c in range(4):
                        p.sync.dma_start_transpose(kTw[bw][:, c * 1024 + (0 if w > 0 else 512):(c + 1) * 1024],
                              H["kd_hbm"][max(lo, 0):(w + 1) * 512, c * 128:(c + 1) * 128]).inc(S[sl], 16)
                    _dma("sync", vw[bw][:].rearrange("p (n c) -> p n c", n=8)[:, (0 if w > 0 else 4):8, :],
                         H["vd_hbm"].rearrange("(n p) c -> p n c", p=128)[:, max(0, (w - 1) * 4):(w + 1) * 4, :], sl)
                    ld3[w] = S[sl].count

                d3_load(0)
                for w in range(8):
                    bw = w % 2
                    qTwb, kTwb, vwb = qTw[bw], kTw[bw], vw[bw]
                    if w + 1 < 8:
                        d3_load(w + 1)
                    p.tensor.wait(S["lda" if bw == 0 else "ldb"], ld3[w])
                    p.tensor.wait(S["st"])
                    for h in range(8):
                        hb = (h % 2) * 64
                        hc = h // 2
                        sb0 = 0 if w > 0 else 4
                        s3_te = {}

                        def emit_S3(sbk):
                            j = sbk % 2
                            if stt_done[j]:
                                p.tensor.wait(ve, stt_done[j])
                            MM(pb[4 + j][:, 0:512],
                               kTwb[hb:hb + 64, hc * 1024 + sbk * 128: hc * 1024 + sbk * 128 + 128],
                               qTwb[hb:hb + 64, hc * 512: (hc + 1) * 512], start=True, stop=True).inc(te)
                            s3_te[sbk] = S["te"].count

                        emit_S3(sb0)
                        for sbk in range(sb0, 8):
                            j = sbk % 2
                            if sbk + 1 < 8:
                                emit_S3(sbk + 1)
                            p.vector.wait(te, s3_te[sbk])
                            p.vector.wait(ac)
                            p.vector.scalar_tensor_tensor(sfd[j][:], pb[4 + j][:, 0:512], 1.0,
                                                          dmaskS[:, sbk * 512:(sbk + 1) * 512],
                                                          op0=ALU.mult, op1=ALU.add).inc(ve)
                            stt_done[j] = S["ve"].count
                            p.scalar.wait(ve, stt_done[j])
                            if pf_done[j]:
                                p.scalar.wait(te, pf_done[j])
                            p.scalar.activation(pbfd[j][:], sfd[j][:], AF.Exp).inc(ac)
                            p.tensor.wait(ac)
                            MM(pbt[0:1, 0:512], ones128d[:], pbfd[j][:],
                               start=(sbk == sb0), stop=(sbk == 7), skip_group_check=True)
                            MM(pb[0][0:64, 0:512], vwb[:, sbk * 512 + h * 64: sbk * 512 + (h + 1) * 64],
                               pbfd[j][:], start=(sbk == sb0), stop=(sbk == 7), skip_group_check=True).inc(te)
                            pf_done[j] = te.count
                        p.vector.wait(te)
                        p.vector.reciprocal(rld[:], pbt[0:1, 0:512])
                        p.vector.tensor_copy(rlbfd[:], rld[:]).inc(ve)
                        p.tensor.wait(ve)
                        MM(pb[1][0:64, 0:512], onesS[0:1, 0:64], rlbfd[:], start=True, stop=True).inc(te)
                        p.vector.wait(te)
                        p.vector.tensor_copy(rbd[0:64, :], pb[1][0:64, 0:512])
                        p.vector.tensor_tensor(oTw[hb:hb + 64, hc * 512:(hc + 1) * 512],
                                               pb[0][0:64, 0:512], rbd[0:64, :], op=ALU.mult).inc(ve)
                    for tl in range(4):
                        tt = w * 4 + tl
                        p.sync.wait(ve)
                        _dma("sync", xt[:], lx_in[tt * 128:(tt + 1) * 128, :], "ld")
                        p.tensor.wait(ld)
                        p.tensor.wait(ve)
                        for ih in range(2):
                            for dc in range(4):
                                mm = MM(pb[2 + ih][:, 0:512],
                                        oTw[:, dc * 512 + tl * 128: dc * 512 + tl * 128 + 128],
                                        woS[:, dc * 1024 + ih * 512: dc * 1024 + (ih + 1) * 512],
                                        start=(dc == 0), stop=(dc == 3))
                        mm.inc(te)
                        p.vector.wait(te)
                        p.vector.wait(ld)
                        p.vector.wait(st)
                        p.vector.tensor_tensor(zsd[:, 0:512], pb[2][:, 0:512], xt[:, 0:512], op=ALU.add)
                        p.vector.tensor_tensor(zsd[:, 512:1024], pb[3][:, 0:512], xt[:, 512:1024],
                                               op=ALU.add).inc(ve)
                        p.gpsimd.wait(ve)
                        _dma("gpsimd", lx_mid[tt * 128:(tt + 1) * 128, :], zsd[:], "st")
                    wb_done[bw] = S["te"].count
            ctx.barrier()

            with ExitStack() as s5k:
                sb5 = lambda n, sh, dt: ctx.sb(s5k, n + sfx, sh, dt)
                lngS = sb5("d_lng", [128, 1024], F32)
                lnbS = sb5("d_lnb", [128, 1024], F32)
                xt4 = [sb5(f"d_xt4{i}", [128, 1024], F32) for i in range(2)]
                zbf4 = [sb5(f"d_zbf4{i}", [128, 1024], BF16) for i in range(2)]
                _dma("sync", lngS[:], H["lng_bc" + sfx][:, :])
                _dma("sync", lnbS[:], H["lnb_bc" + sfx][:, :])
                xt_done = [0, 0]
                ld4 = {}

                def d4a_load(tt):
                    bi = tt % 2
                    sl = "lda" if bi == 0 else "ldb"
                    if xt_done[bi]:
                        p.sync.wait(ve, xt_done[bi])
                    _dma("sync", xt4[bi][:, 0:512], lx_mid[tt * 128:(tt + 1) * 128, 0:512], sl)
                    if tt == 0:
                        _dma("sync", xt4[bi][1:128, 512:1024], lx_mid[0:127, 512:1024], sl)
                    else:
                        _dma("sync", xt4[bi][:, 512:1024], lx_mid[tt * 128 - 1:(tt + 1) * 128 - 1, 512:1024], sl)
                    ld4[tt] = S[sl].count

                def d4a_tile(tt):
                    bi = tt % 2
                    if tt + 1 < 32:
                        d4a_load(tt + 1)
                    sl = "lda" if bi == 0 else "ldb"
                    p.vector.wait(S[sl], ld4[tt])
                    p.vector.wait(S["stz"])
                    if tt == 0:
                        p.vector.wait(ld)
                        p.vector.memset(xt4[0][0:1, 512:1024], 0.0).inc(ve)
                    p.scalar.wait(S[sl], ld4[tt])
                    emit_ln(ctx, K, xt4[bi], lnb)
                    p.vector.tensor_tensor(xt4[bi][:], xt4[bi][:], lngS[:], op=ALU.mult)
                    p.vector.tensor_tensor(zbf4[bi][:], xt4[bi][:], lnbS[:], op=ALU.add).inc(ve)
                    xt_done[bi] = S["ve"].count
                    p.gpsimd.wait(ve)
                    _dma("gpsimd", H["xh_bf"][tt * 128:(tt + 1) * 128, :], zbf4[bi][:], "stz")

                w1S = sb5("d_w1", [128, NCH * 5460], BF16)
                b1S = sb5("d_b1", [1, 5460], BF16)
                xhT = [sb5(f"d_xhT{i}", [128, NCH * 512], BF16) for i in range(2)]
                ggf = sb5("d_gg", [128, 1024], F32)
                h2b = sb5("d_h2", [128, 2816], BF16)
                _dma("gpsimd", w1S[:].rearrange("p (c x) -> p c x", c=NCH), chunked(H["w1" + sfx]), "st")
                _dma("gpsimd", b1S[:], H["b1" + sfx][:, :], "st")
                p.vector.memset(h2b[:, 2730:2816], 0.0).inc(ve)
                xhT_done = [0, 0]
                w1_done = [(0, 0), (0, 0)]
                d4a_load(0)
                for g in range(8):
                    for tl4 in range(4):
                        d4a_tile(g * 4 + tl4)
                    i = g % 2
                    if xhT_done[i]:
                        p.sync.wait(te, xhT_done[i])
                    p.sync.wait(S["stz"])
                    for c in range(NCH):
                        _dmaT(xhT[i][:, c * 512:(c + 1) * 512],
                              H["xh_bf"][g * 512:(g + 1) * 512, c * 128:(c + 1) * 128])
                    p.tensor.wait(ld)
                    if g == 0:
                        p.tensor.wait(S["st"])
                    for tl in range(4):
                        tt = g * 4 + tl
                        for nib, (ib, bw) in enumerate(AGB):
                            jp = nib % 2
                            if w1_done[jp][0]:
                                p.tensor.wait(ac, w1_done[jp][0])
                                p.tensor.wait(ve, w1_done[jp][1])
                            for half, psX in ((0, pb[2 * jp]), (1, pb[2 * jp + 1])):
                                off = ib + half * 2730
                                for cc in range(NCH):
                                    MM(psX[:, 0:bw],
                                       xhT[i][:, cc * 512 + tl * 128: cc * 512 + tl * 128 + 128],
                                       w1S[:, cc * 5460 + off: cc * 5460 + off + bw],
                                       start=(cc == 0), stop=False)
                                MM(psX[:, 0:bw], onesS[0:1, 0:128],
                                   b1S[0:1, off:off + bw], start=False, stop=True).inc(te)
                            p.scalar.wait(te)
                            p.scalar.activation(ggf[:, jp * 512: jp * 512 + bw], pb[2 * jp + 1][:, 0:bw],
                                                AF.Gelu).inc(ac)
                            p.vector.wait(ac)
                            p.vector.wait(st)
                            p.vector.tensor_tensor(h2b[:, ib:ib + bw], ggf[:, jp * 512: jp * 512 + bw],
                                                   pb[2 * jp][:, 0:bw], op=ALU.mult).inc(ve)
                            w1_done[jp] = (S["ac"].count, S["ve"].count)
                        p.gpsimd.wait(ve)
                        _dma("gpsimd", H["h2_hbm"][tt * 128:(tt + 1) * 128, :], h2b[:], "st")
                    xhT_done[i] = te.count
            ctx.barrier()

            with ExitStack() as s6k:
                sb6 = lambda n, sh, dt: ctx.sb(s6k, n + sfx, sh, dt)
                w2S = sb6("d_w2", [128, 22 * 1024], BF16)
                b2S = sb6("d_b2", [1, 1024], BF16)
                h2T = sb6("d_h2T", [128, 22 * 512], BF16)
                xt6 = sb6("d_xt6", [128, 1024], F32)
                zs6 = sb6("d_zs6", [128, 1024], F32)
                zb6 = sb6("d_zb6", [128, 1024], BF16)
                p.vector.memset(w2S[:, 21 * 1024:22 * 1024], 0.0).inc(ve)
                p.gpsimd.wait(ve)
                _dma("gpsimd", w2S[:].rearrange("p (c x) -> p c x", c=22)[:, 0:21, :],
                     H["w2" + sfx][0:2688, :].rearrange("(c p) x -> p c x", p=128), "st")
                _dma("gpsimd", w2S[0:42, 21 * 1024:22 * 1024], H["w2" + sfx][2688:2730, :], "st")
                _dma("gpsimd", b2S[:], H["b2" + sfx][:, :], "st")
                w2_done = [0, 0]
                for g in range(8):
                    p.sync.wait(te)
                    for c in range(22):
                        _dmaT(h2T[:, c * 512:(c + 1) * 512],
                              H["h2_hbm"][g * 512:(g + 1) * 512, c * 128:(c + 1) * 128])
                    for tl in range(4):
                        tt = g * 4 + tl
                        jp = tt % 2
                        p.sync.wait(ve)
                        _dma("sync", xt6[:], lx_mid[tt * 128:(tt + 1) * 128, :], "ld")
                        p.tensor.wait(ld)
                        p.tensor.wait(S["st"])
                        if tt >= 2:
                            p.tensor.wait(ve, w2_done[jp])
                        for ih in range(2):
                            for c in range(22):
                                MM(pb[2 * jp + ih][:, 0:512], h2T[:, c * 512 + tl * 128: c * 512 + tl * 128 + 128],
                                   w2S[:, c * 1024 + ih * 512: c * 1024 + (ih + 1) * 512],
                                   start=(c == 0), stop=False)
                            MM(pb[2 * jp + ih][:, 0:512], onesS[0:1, 0:128],
                               b2S[0:1, ih * 512:(ih + 1) * 512], start=False, stop=True).inc(te)
                        p.vector.wait(te)
                        p.vector.wait(ld)
                        p.vector.wait(st)
                        p.vector.tensor_tensor(zs6[:, 0:512], pb[2 * jp][:, 0:512], xt6[:, 0:512], op=ALU.add)
                        p.vector.tensor_tensor(zs6[:, 512:1024], pb[2 * jp + 1][:, 0:512], xt6[:, 512:1024],
                                               op=ALU.add).inc(ve)
                        w2_done[jp] = S["ve"].count
                        if l == 0 and x_src_bf is not None:
                            p.vector.tensor_copy(zb6[:], zs6[:]).inc(ve)
                        p.gpsimd.wait(ve)
                        _dma("gpsimd", lx_out[tt * 128:(tt + 1) * 128, :], zs6[:], "st")
                        if l == 0 and x_src_bf is not None:
                            _dma("gpsimd", H["xbbf"][tt * 128:(tt + 1) * 128, :], zb6[:], "st")
            ctx.barrier()

        if nlayers < 2:
            p.sync.wait(st)
            return
        with ExitStack() as s7k:
            sb7 = lambda n, sh, dt: ctx.sb(s7k, n, sh, dt)
            xt7 = sb7("d_xt7", [128, 1024], F32)
            xt8 = sb7("d_xt8", [128, 1024], F32)
            for tt in range(32):
                p.sync.wait(ve)
                p.sync.wait(S["st"])
                _dma("sync", xt7[:], x_src[tt * 128:(tt + 1) * 128, :], "ld")
                _dma("sync", xt8[:], H["xd_d"][tt * 128:(tt + 1) * 128, :], "ld")
                p.vector.wait(ld)
                p.vector.wait(st)
                p.vector.tensor_tensor(xt7[:], xt7[:], xt8[:], op=ALU.add).inc(ve)
                emit_ln(ctx, K, xt7, lnb)
                p.gpsimd.wait(ve)
                _dma("gpsimd", x_dst[tt * 128:(tt + 1) * 128, :], xt7[:], "st")
        p.sync.wait(st)


def decl_d(nc, H, K):
    F32, BF16, T = K.F32, K.BF16, K.T
    def di(name, shape, dt):
        if name not in H:
            H[name] = nc.dram_tensor(name, list(shape), dt, kind="ExternalInput").ap()
    def ds(name, shape, dt):
        if name not in H:
            H[name] = nc.dram_tensor(name, list(shape), dt, kind="Internal").ap()
    for l in ("0", "1"):
        di("wqkv" + l, (1024, 1536), F32)
        di("wo" + l, (512, 1024), F32)
        di("w1" + l, (1024, 5460), F32)
        di("w2" + l, (2730, 1024), F32)
        di("b1" + l, (1, 5460), F32)
        di("b2" + l, (1, 1024), F32)
        di("qs_bc" + l, (128, 512), F32)
        di("ks_bc" + l, (128, 512), F32)
        di("lng_bc" + l, (128, 1024), F32)
        di("lnb_bc" + l, (128, 1024), F32)
    for nm in ("dcq", "dsq", "dck", "dsk"):
        di(nm, (T, 64), BF16)
    di("dmask", (128, 8 * 512), BF16)
    ds("h1bf", (T, 1024), BF16)
    ds("xbbf", (T, 1024), BF16)
    ds("xshift0", (512, 512), BF16)
    ds("qd_hbm", (T, 512), BF16)
    ds("kd_hbm", (T, 512), BF16)
    ds("vd_hbm", (T, 512), BF16)
    ds("xh_bf", (T, 1024), BF16)
    ds("h2_hbm", (T, 2816), BF16)
    ds("xd_a", (T, 1024), F32)
    ds("xd_b", (T, 1024), F32)
    ds("xd_c", (T, 1024), F32)
    ds("xd_d", (T, 1024), F32)
    return H


def host_consts_d(K):
    import ml_dtypes
    BF = ml_dtypes.bfloat16
    T = K.T
    DIM_HEAD, WINDOW = 64, 512
    half_idx = np.arange(0, DIM_HEAD, 2, dtype=np.float64)
    inv_freq = 1.0 / 10000.0 ** (half_idx / DIM_HEAD)
    t = np.arange(T, dtype=np.float64)
    freqs = t[:, None] * inv_freq[None, :]
    freqs = np.concatenate([freqs, freqs], axis=1)
    base = (half_idx + 0.4 * DIM_HEAD) / (1.4 * DIM_HEAD)
    power = (t - T // 2) / float(WINDOW // 2)
    scale = base[None, :] ** power[:, None]
    scale = np.concatenate([scale, scale], axis=1)
    cos, sin = np.cos(freqs), np.sin(freqs)
    sgn = np.where(np.arange(64) < 32, -1.0, 1.0)[None, :]
    d = {
        "dcq": (cos * scale * 8.0),
        "dsq": (sin * scale * sgn * 8.0),
        "dck": (cos / scale),
        "dsk": (sin / scale * sgn),
    }
    d = {k: np.ascontiguousarray(v).astype(np.float32).astype(BF) for k, v in d.items()}
    m = np.zeros((128, 8 * 512), np.float32)
    sp = np.arange(128)[:, None]
    tq = np.arange(512)[None, :]
    for sbk in range(8):
        if sbk < 4:
            keep = (sbk * 128 + sp) >= tq
        else:
            keep = ((sbk - 4) * 128 + sp) <= tq
        m[:, sbk * 512:(sbk + 1) * 512] = np.where(keep, 0.0, -30000.0)
    d["dmask"] = m.astype(BF)
    return d


def host_inputs_d(inp, K):
    f32 = np.float32
    d = {}
    for l in range(2):
        s = str(l)
        d["wqkv" + s] = np.ascontiguousarray(np.asarray(inp["L1_ffn_wqkv"][l], f32))
        d["wo" + s] = np.ascontiguousarray(np.asarray(inp["L1_ffn_wo"][l], f32))
        d["w1" + s] = np.ascontiguousarray(np.asarray(inp["L1_ffn_w1"][l], f32))
        d["w2" + s] = np.ascontiguousarray(np.asarray(inp["L1_ffn_w2"][l], f32))
        d["b1" + s] = np.asarray(inp["L1_ffn_b1"][l], f32).reshape(1, 5460)
        d["b2" + s] = np.asarray(inp["L1_ffn_b2"][l], f32).reshape(1, 1024)
        d["qs_bc" + s] = np.ascontiguousarray(np.broadcast_to(
            np.tile(np.asarray(inp["L1_ffn_q_scale"][l], f32), 8)[None, :], (128, 512)).copy())
        d["ks_bc" + s] = np.ascontiguousarray(np.broadcast_to(
            np.tile(np.asarray(inp["L1_ffn_k_scale"][l], f32), 8)[None, :], (128, 512)).copy())
        d["lng_bc" + s] = np.ascontiguousarray(np.broadcast_to(
            np.asarray(inp["L1_ffn_ln_g"][l], f32)[None, :], (128, 1024)).copy())
        d["lnb_bc" + s] = np.ascontiguousarray(np.broadcast_to(
            np.asarray(inp["L1_ffn_ln_b"][l], f32)[None, :], (128, 1024)).copy())
    d.update(host_consts_d(K))
    return d



import types as _types
K = _types.SimpleNamespace(**{k: v for k, v in dict(globals()).items() if not k.startswith("_")})


PHASE_MARKS = {}


def _build_program():
    nc = bass.Bass("TRN2", target_bir_lowering=False)
    H = {}
    H["x0"] = nc.dram_tensor("x0", [T, D], F32, kind="ExternalInput").ap()
    H["out"] = nc.dram_tensor("out", [T, D], F32, kind="ExternalOutput").ap()
    decl_a(nc, H)
    decl_b(nc, H, K)
    decl_c(nc, H, K)
    decl_d(nc, H, K)
    for nm in ("x1s", "x2s", "x3s"):
        H[nm] = nc.dram_tensor(nm, [T, D], F32, kind="Internal").ap()
    H["x2bf"] = nc.dram_tensor("x2bf", [T, D], BF16, kind="Internal").ap()
    H["x3bf"] = nc.dram_tensor("x3bf", [T, D], BF16, kind="Internal").ap()
    ctx = Ctx(nc)
    def _mark(nm):
        PHASE_MARKS[nm] = {e: len(q) for e, q in ctx.p.q.items()}
    wq_items = []
    for wname, qname, nch in (("wg", "qwg", 22), ("wu", "qwu", 22)):
        for ch in range(nch):
            wq_items.append((1024, H[wname][ch * 128:(ch + 1) * 128, :],
                             H[qname][ch * 128:(ch + 1) * 128, :]))
    emit_phase_a(ctx, H, H["x0"], H["x1s"], K, qdx_dst=H["qdx"], wq_items=wq_items)
    ctx.barrier()
    _mark("A")
    emit_phase_b(ctx, H, H["x1s"], H["x2s"], K, dst_bf=H["x2bf"], coquant=True)
    ctx.barrier()
    _mark("B")
    emit_phase_c(ctx, H, H["x2s"], H["x2bf"], H["x3s"], K, dst_bf=H["x3bf"])
    ctx.barrier()
    _mark("C")
    emit_phase_d(ctx, H, H["x3s"], H["out"], K, x_src_bf=H["x3bf"])
    ctx.barrier()
    _mark("D")
    ctx.p.replay()
    ctx.stack.close()
    return nc


def kernel(**inputs):
    shared = {}
    shared.update(host_inputs_a(inputs))
    shared.update(host_inputs_b(inputs))
    shared.update(host_inputs_c(inputs, K))
    shared.update(host_inputs_d(inputs, K))
    x = np.asarray(inputs["L0_attn_x"], np.float32)
    nc = _build_program()
    in_maps = []
    for core in range(8):
        m = dict(shared)
        m["x0"] = np.ascontiguousarray(x[core])
        in_maps.append(m)
    res = run_bass_kernel_spmd(nc, in_maps, core_ids=list(range(8)))
    out = np.stack([np.asarray(r["out"], np.float32) for r in res.results], axis=0)
    return out


# version 20
# speedup vs baseline: 1.0156x; 1.0156x over previous
import numpy as np
import ml_dtypes
from contextlib import ExitStack
import concourse.bass as bass
import concourse.mybir as mybir
from concourse.bass_utils import run_bass_kernel_spmd

F32 = mybir.dt.float32
BF16 = mybir.dt.bfloat16
I32 = mybir.dt.int32
AF = mybir.ActivationFunctionType
ALU = mybir.AluOpType
AX = mybir.AxisListType
BF = ml_dtypes.bfloat16

T, D = 4096, 1024
NCH = D // 128

ENGS = ("sync", "tensor", "vector", "scalar", "gpsimd")


class Sem:
    def __init__(self, h):
        self.h = h
        self.count = 0


class _Pending:
    __slots__ = ("rec",)

    def __init__(self, rec):
        self.rec = rec

    def inc(self, sem, n=1):
        sem.count += n
        self.rec[3].append((sem.h, n, sem.count))
        return self


class _EngQ:
    def __init__(self, prog, name):
        self.__dict__["_prog"] = prog
        self.__dict__["_name"] = name

    def __getattr__(self, op):
        prog, name = self._prog, self._name

        def call(*args, **kwargs):
            rec = (op, args, kwargs, [])
            prog.q[name].append(rec)
            return _Pending(rec)

        return call

    def wait(self, sem, val=None):
        v = sem.count if val is None else val
        if v > 0:
            self._prog.q[self._name].append(("wait_ge", (sem.h, v), {}, []))


NOCHAIN = {"wait_ge", "dma_start", "dma_start_transpose", "nop", "memset", "memzero"}


class Prog:

    def __init__(self, nc):
        self.nc = nc
        self.q = {e: [] for e in ENGS}
        self.chain_sems = {}
        for e in ENGS:
            setattr(self, e, _EngQ(self, e))

    def replay(self):
        with self.nc.Block() as blk:
            for e in ENGS:
                def body(eng, _e=e):
                    ch = self.chain_sems.get(_e)
                    cnt = 0
                    last = None
                    for (op, args, kwargs, incs) in self.q[_e]:
                        chainable = ch is not None and op not in NOCHAIN
                        if chainable and last is not None:
                            eng.wait_ge(*last)
                        instr = getattr(eng, op)(*args, **kwargs)
                        if chainable and not incs:
                            cnt += 1
                            instr.then_inc(ch, 1)
                            last = (ch, cnt)
                        for (semh, n, after) in incs:
                            instr.then_inc(semh, n)
                        if chainable and incs:
                            last = (incs[0][0], incs[0][2])
                    if ch is not None and cnt:
                        pass
                getattr(blk, e)(body)


class Ctx:
    def __init__(self, nc):
        self.nc = nc
        self.p = Prog(nc)
        self.stack = ExitStack()
        self.S = {}
        for name in ("ld", "st", "te", "ve", "ac", "ld2", "po", "lda", "ldb", "stz"):
            self.S[name] = Sem(self.stack.enter_context(nc.semaphore(name)))
        for e in ("vector", "scalar", "gpsimd"):
            self.p.chain_sems[e] = self.stack.enter_context(nc.semaphore("ch_" + e))
        self.pb = [self.stack.enter_context(nc.psum_tensor(f"pb{i}", [128, 512], F32))
                   for i in range(6)]
        self.pbt = self.stack.enter_context(nc.psum_tensor("pbt", [128, 512], F32))
        self.pbx = self.stack.enter_context(nc.psum_tensor("pbx", [128, 1024], BF16))

    def sb(self, stk, name, shape, dt):
        return stk.enter_context(self.nc.sbuf_tensor(name, shape, dt))

    def barrier(self):
        for e in ENGS:
            eq = getattr(self.p, e)
            for s in self.S.values():
                if s.count > 0:
                    eq.wait(s)


def _dma(ctx, q, out, in_, sem="ld"):
    getattr(ctx.p, q).dma_start(out=out, in_=in_).inc(ctx.S[sem], 16)


def chunked(ap2d, csize=128):
    return ap2d.rearrange("(c p) x -> p c x", p=csize)



def host_consts_a():
    t = np.arange(T, dtype=np.float64)
    j = np.arange(0, D, 2, dtype=np.float64)
    freqs = 1.0 / (10000.0 ** (j / D))
    ang = np.outer(freqs, t)
    cosT = np.repeat(np.cos(ang), 2, axis=0).astype(np.float32).astype(BF)
    sinT = np.repeat(np.sin(ang), 2, axis=0).astype(np.float32).astype(BF)
    pm = np.zeros((128, 128), np.float32)
    for a in range(0, 128, 2):
        pm[a, a + 1] = -1.0
        pm[a + 1, a] = 1.0
    pmT = np.ascontiguousarray(pm.T).astype(BF)
    s = np.arange(128)[:, None]
    tt = np.arange(128)[None, :]
    ustri = (s <= tt).astype(np.float32).astype(BF)
    identbf = np.eye(128, dtype=np.float32).astype(BF)
    return {"ropecos": np.ascontiguousarray(cosT), "ropesin": np.ascontiguousarray(sinT),
            "pmT": pmT, "ustri": ustri, "identbf": identbf}


def decl_a(nc, H):
    def di(name, shape, dt):
        H[name] = nc.dram_tensor(name, list(shape), dt, kind="ExternalInput").ap()
    def ds(name, shape, dt):
        H[name] = nc.dram_tensor(name, list(shape), dt, kind="Internal").ap()
    for n in ("wqT", "wkT", "wvT", "wfT"):
        di(n, (D, D), F32)
    for n in ("bq", "bk", "bv", "bfb"):
        di(n, (1, D), F32)
    di("ropecos", (D, T), BF16)
    di("ropesin", (D, T), BF16)
    di("pmT", (128, 128), BF16)
    di("ustri", (128, 128), BF16)
    di("identbf", (128, 128), BF16)
    ds("x0bf", (T, D), BF16)
    ds("vtokA", (T, D), BF16)
    return H


def host_inputs_a(inp):
    f32 = np.float32
    d = {
        "wqT": np.ascontiguousarray(np.asarray(inp["L0_attn_Wq"], f32).T),
        "wkT": np.ascontiguousarray(np.asarray(inp["L0_attn_Wk"], f32).T),
        "wvT": np.ascontiguousarray(np.asarray(inp["L0_attn_Wv"], f32).T),
        "wfT": np.ascontiguousarray(np.asarray(inp["L0_attn_Wf"], f32).T),
        "bq": np.asarray(inp["L0_attn_bq"], f32).reshape(1, D),
        "bk": np.asarray(inp["L0_attn_bk"], f32).reshape(1, D),
        "bv": np.asarray(inp["L0_attn_bv"], f32).reshape(1, D),
        "bfb": np.asarray(inp["L0_attn_bf"], f32).reshape(1, D),
    }
    d.update(host_consts_a())
    return d




def emit_phase_a(ctx, H, x_src, x_dst, K, qdx_dst=None, wq_items=None):
    F32, BF16, AF, ALU, AX = K.F32, K.BF16, K.AF, K.ALU, K.AX
    T, D, NCH = K.T, K.D, K.NCH
    p, nc, S = ctx.p, ctx.nc, ctx.S
    ld, st, te, ve, ac, ld2, po = S["ld"], S["st"], S["te"], S["ve"], S["ac"], S["ld2"], S["po"]
    pb, pbt = ctx.pb, ctx.pbt
    MM = p.tensor.matmul
    _dma = lambda q, out, in_, sem="ld": getattr(p, q).dma_start(out=out, in_=in_).inc(S[sem], 16)
    chunked = K.chunked
    with ExitStack() as stk:
        sb = lambda n, sh, dt: ctx.sb(stk, n, sh, dt)
        ktS = sb("a_kt", [128, NCH * T], BF16)
        wS = sb("a_w", [128, NCH * 1024], BF16)
        wS2 = sb("a_w2", [128, NCH * 1024], BF16)
        biasS = sb("a_bias", [1, 1024], BF16)
        bias2S = sb("a_bias2", [1, 1024], BF16)
        onesS = sb("a_ones", [1, 512], BF16)
        ones128S = sb("a_ones128", [128, 1], BF16)
        pmS = sb("a_pm", [128, 128], BF16)
        idbfS = sb("a_idbf", [128, 128], BF16)
        ustriS = sb("a_ustri", [128, 128], BF16)
        knorm = sb("a_knorm", [1, 512], F32)
        kmax2 = sb("a_kmax2", [1, 1], F32)

        x0bf, vtok = H["x0bf"], H["vtokA"]
        for i in range(8):
            _dma("gpsimd", x0bf[i * 512:(i + 1) * 512, :], x_src[i * 512:(i + 1) * 512, :], "st")
        _dma("sync", pmS[:], H["pmT"][:, :])
        _dma("sync", idbfS[:], H["identbf"][:, :])
        _dma("sync", ustriS[:], H["ustri"][:, :])
        _dma("gpsimd", wS[:].rearrange("p (c x) -> p c x", c=NCH), chunked(H["wkT"]), "st")
        _dma("gpsimd", wS2[:].rearrange("p (c x) -> p c x", c=NCH), chunked(H["wvT"]), "st")
        _dma("gpsimd", biasS[:], H["bk"][:, :], "st")
        _dma("gpsimd", bias2S[:], H["bv"][:, :], "st")
        p.vector.memset(onesS[:], 1.0)
        p.vector.memset(ones128S[:], 1.0)
        p.vector.memset(knorm[:], 0.0).inc(ve)

        with ExitStack() as stk1:
            sb1 = lambda n, sh, dt: ctx.sb(stk1, n, sh, dt)
            xtb = [sb1(f"a_xtb{i}", [128, NCH * 512], BF16) for i in range(2)]
            cosb = sb1("a_cosb", [128, NCH * 512], BF16)
            sinb = sb1("a_sinb", [128, NCH * 512], BF16)
            tmpre = [sb1(f"a_tmpre{i}", [128, 512], BF16) for i in range(2)]
            tmp2 = sb1("a_tmp2", [128, 512], BF16)
            sqbf = [sb1(f"a_sqbf{i}", [128, 512], BF16) for i in range(2)]
            vstage = [sb1(f"a_vst{i}", [128, 1024], BF16) for i in range(2)]

            xtb_done = [0, 0]
            cs_done = 0
            kn_done = 0
            vst_done = [0, 0]
            for tb in range(8):
                i = tb % 2
                p.sync.wait(st)
                if xtb_done[i]:
                    p.sync.wait(te, xtb_done[i])
                if cs_done:
                    p.sync.wait(ve, cs_done)
                for c in range(NCH):
                    p.sync.dma_start_transpose(xtb[i][:, c * 512:(c + 1) * 512],
                         x0bf[tb * 512:(tb + 1) * 512, c * 128:(c + 1) * 128]).inc(ld, 16)
                _dma("sync", cosb[:].rearrange("p (c x) -> p c x", c=NCH),
                     chunked(H["ropecos"])[:, :, tb * 512:(tb + 1) * 512], "ld")
                _dma("sync", sinb[:].rearrange("p (c x) -> p c x", c=NCH),
                     chunked(H["ropesin"])[:, :, tb * 512:(tb + 1) * 512], "ld")
                p.tensor.wait(ld)
                p.tensor.wait(st)
                for ic in range(NCH):
                    psK = pb[ic % 2]
                    j = ic % 2
                    p.tensor.wait(ac)
                    for cc in range(NCH):
                        MM(psK[:, 0:512], wS[:, cc * 1024 + ic * 128: cc * 1024 + ic * 128 + 128],
                           xtb[i][:, cc * 512:(cc + 1) * 512], start=(cc == 0), stop=False)
                    MM(psK[:, 0:512], biasS[0:1, ic * 128:ic * 128 + 128],
                       onesS[0:1, 0:512], start=False, stop=True).inc(te)
                    p.scalar.wait(te)
                    p.scalar.wait(ve)
                    p.scalar.copy(tmpre[j][:], psK[:, 0:512])
                    p.scalar.square(sqbf[j][:], tmpre[j][:]).inc(ac)
                    p.tensor.wait(ac)
                    p.tensor.wait(ve)
                    MM(pb[2][:, 0:512], pmS[:], tmpre[j][:], start=True, stop=True).inc(te)
                    ks = ktS[:, ic * T + tb * 512: ic * T + tb * 512 + 512]
                    p.vector.wait(te)
                    p.vector.tensor_tensor(ks, tmpre[j][:], cosb[:, ic * 512:(ic + 1) * 512], op=ALU.mult)
                    p.vector.tensor_tensor(tmp2[:], pb[2][:, 0:512], sinb[:, ic * 512:(ic + 1) * 512], op=ALU.mult)
                    p.vector.tensor_tensor(ks, ks, tmp2[:], op=ALU.add).inc(ve)
                    if ic == 0 and kn_done:
                        p.tensor.wait(ve, kn_done)
                    MM(pb[3][0:1, 0:512], ones128S[:], sqbf[j][:], start=(ic == 0), stop=(ic == NCH - 1),
                       skip_group_check=True).inc(te)
                    if ic == NCH - 1:
                        p.vector.wait(te)
                        p.vector.tensor_tensor(knorm[:], knorm[:], pb[3][0:1, 0:512], op=ALU.max).inc(ve)
                        kn_done = ve.count
                cs_done = ve.count
                for tl in range(4):
                    vb = vstage[tl % 2]
                    for ih in range(2):
                        psV = pb[4 + ih]
                        p.tensor.wait(ac)
                        for cc in range(NCH):
                            MM(psV[:, 0:512],
                               xtb[i][:, cc * 512 + tl * 128: cc * 512 + tl * 128 + 128],
                               wS2[:, cc * 1024 + ih * 512: cc * 1024 + (ih + 1) * 512],
                               start=(cc == 0), stop=False)
                        MM(psV[:, 0:512], onesS[0:1, 0:128],
                           bias2S[0:1, ih * 512:(ih + 1) * 512], start=False, stop=True).inc(te)
                        p.scalar.wait(te)
                        if vst_done[tl % 2]:
                            p.scalar.wait(st, vst_done[tl % 2])
                        p.scalar.copy(vb[:, ih * 512:(ih + 1) * 512], psV[:, 0:512]).inc(ac)
                    p.gpsimd.wait(ac)
                    _dma("gpsimd", vtok[(tb * 4 + tl) * 128:(tb * 4 + tl + 1) * 128, :], vb[:], "st")
                    vst_done[tl % 2] = st.count
                xtb_done[i] = te.count

            p.vector.wait(te)
            p.vector.tensor_reduce(kmax2[:], knorm[:], axis=AX.X, op=ALU.max).inc(ve)

            p.gpsimd.wait(te)
            p.gpsimd.wait(ac)
            _dma("gpsimd", wS[:].rearrange("p (c x) -> p c x", c=NCH), chunked(H["wqT"]), "st")
            _dma("gpsimd", wS2[:].rearrange("p (c x) -> p c x", c=NCH), chunked(H["wfT"]), "st")
            _dma("gpsimd", biasS[:], H["bq"][:, :], "st")
            _dma("gpsimd", bias2S[:], H["bfb"][:, :], "st")
            p.vector.wait(st)
            p.vector.tensor_scalar_mul(wS[:], wS[:], 1.0 / 32.0)
            p.vector.tensor_scalar_mul(biasS[:], biasS[:], 1.0 / 32.0).inc(ve)

        dbg = getattr(ctx, 'dbg', None)
        if dbg is not None:
            p.gpsimd.wait(ve)
            p.gpsimd.wait(te)
            p.gpsimd.wait(ac)
            _dma("gpsimd", dbg["d_kt"][:, :], ktS[:], "st")
            _dma("gpsimd", dbg["d_x0bf"][0:512, :], x0bf[0:512, :], "st")
            _dma("gpsimd", dbg["d_vtok"][0:512, :], vtok[0:512, :], "st")
        ctx.barrier()
        with ExitStack() as stk2:
            sb2 = lambda n, sh, dt: ctx.sb(stk2, n, sh, dt)
            xtg = sb2("a_xtg", [128, NCH * 256], BF16)
            cosg = sb2("a_cosg", [128, NCH * 256], BF16)
            sing = sb2("a_sing", [128, NCH * 256], BF16)
            qTs = sb2("a_qT", [128, NCH * 256], BF16)
            q2bf = sb2("a_q2", [128, 256], BF16)
            tmq = [sb2(f"a_tmq{i}", [128, 256], BF16) for i in range(2)]
            tmp2q = sb2("a_tmp2q", [128, 256], BF16)
            mrow = sb2("a_mrow", [1, 256], F32)
            negm = sb2("a_negm", [1, 256], BF16)
            ptile = [sb2(f"a_pt{i}", [128, 128], BF16) for i in range(2)]
            vblk = [sb2(f"a_vb{i}", [128, 1024], BF16) for i in range(2)]
            rcol = sb2("a_rcol", [128, 1], F32)
            obf = sb2("a_obf", [128, 1024], BF16)
            oT = sb2("a_oT", [128, 1024], BF16)
            x0t = sb2("a_x0t", [128, 1024], F32)
            zs = sb2("a_zs", [128, 1024], F32)
            b6 = sb2("a_b6", [128, 12], F32)
            b2 = sb2("a_b2", [128, 2], F32)
            svar = sb2("a_svar", [128, 1], F32)
            rstd = sb2("a_rstd", [128, 1], F32)
            if qdx_dst is not None:
                qbA = mk_qdb(ctx, stk2, K, Wmax=1024, pfx="qa")
                qsrc = [sb2(f"a_qsrc{i}", [128, 1024], F32) for i in range(2)]
                qdst = [sb2(f"a_qdst{i}", [128, 1024], BF16) for i in range(2)]
                qdw = [sb2(f"a_qdw{i}", [128, 1024], BF16) for i in range(2)]
                wq_queue = list(wq_items or [])
                qst_done = [0, 0]
                qld = {}

            prev_ve = 0
            vblk_done = [0, 0]
            pt_done = [0, 0]
            e_ac = [0, 0]
            qe_ac = [0, 0]
            rope_v = [0, 0]
            rope_last = 0
            norm_te = 0
            xtg_te = 0
            negm_ve = 0
            obf_ve = 0
            oT_ac = 0
            psY_te = 0
            zq_ac = zq_po = 0
            for g in range(16):
                if xtg_te:
                    p.sync.wait(te, xtg_te)
                if rope_last:
                    p.sync.wait(ve, rope_last)
                for c in range(NCH):
                    p.sync.dma_start_transpose(xtg[:, c * 256:(c + 1) * 256],
                         x0bf[g * 256:(g + 1) * 256, c * 128:(c + 1) * 128]).inc(ld, 16)
                _dma("sync", cosg[:].rearrange("p (c x) -> p c x", c=NCH),
                     chunked(H["ropecos"])[:, :, g * 256:(g + 1) * 256], "ld")
                _dma("sync", sing[:].rearrange("p (c x) -> p c x", c=NCH),
                     chunked(H["ropesin"])[:, :, g * 256:(g + 1) * 256], "ld")
                p.tensor.wait(ld)
                if g == 0:
                    p.tensor.wait(st)
                for ic in range(NCH):
                    psQ = pb[ic % 2]
                    j = ic % 2
                    if qe_ac[j]:
                        p.tensor.wait(ac, qe_ac[j])
                    if ic < 2 and obf_ve:
                        p.tensor.wait(ve, obf_ve)
                    for cc in range(NCH):
                        MM(psQ[:, 0:256], wS[:, cc * 1024 + ic * 128: cc * 1024 + ic * 128 + 128],
                           xtg[:, cc * 256:(cc + 1) * 256], start=(cc == 0), stop=False)
                    MM(psQ[:, 0:256], biasS[0:1, ic * 128:ic * 128 + 128],
                       onesS[0:1, 0:256], start=False, stop=True).inc(te)
                    qmm_te = te.count
                    p.scalar.wait(te, qmm_te)
                    if rope_v[j]:
                        p.scalar.wait(ve, rope_v[j])
                    p.scalar.copy(tmq[j][:], psQ[:, 0:256]).inc(ac)
                    qe_ac[j] = S["ac"].count
                    p.tensor.wait(ac, qe_ac[j])
                    rv_g = rope_v[1 - j] if ic > 0 else obf_ve
                    if rv_g:
                        p.tensor.wait(ve, rv_g)
                    MM(pb[2][:, 0:256], pmS[:], tmq[j][:], start=True, stop=True).inc(te)
                    rot_te = te.count
                    qs = qTs[:, ic * 256:(ic + 1) * 256]
                    if norm_te:
                        p.scalar.wait(te, norm_te)
                    p.scalar.square(q2bf[:], tmq[j][:]).inc(ac)
                    sq_ac = S["ac"].count
                    p.vector.wait(te, rot_te)
                    p.vector.tensor_tensor(qs, tmq[j][:], cosg[:, ic * 256:(ic + 1) * 256], op=ALU.mult)
                    p.vector.tensor_tensor(tmp2q[:], pb[2][:, 0:256], sing[:, ic * 256:(ic + 1) * 256], op=ALU.mult)
                    p.vector.tensor_tensor(qs, qs, tmp2q[:], op=ALU.add).inc(ve)
                    rope_v[j] = S["ve"].count
                    if ic == NCH - 1:
                        rope_last = rope_v[j]
                    p.tensor.wait(ac, sq_ac)
                    MM(pb[3][0:1, 0:256], ones128S[:], q2bf[:],
                       start=(ic == 0), stop=(ic == NCH - 1), skip_group_check=True).inc(te)
                    norm_te = te.count
                xtg_te = norm_te
                p.scalar.wait(te, norm_te)
                if negm_ve:
                    p.scalar.wait(ve, negm_ve)
                p.scalar.activation(mrow[:], pb[3][0:1, 0:256], AF.Sqrt, scale=kmax2[0:1, 0:1]).inc(ac)
                mrow_ac = S["ac"].count
                p.vector.wait(ac, mrow_ac)
                if psY_te:
                    p.vector.wait(te, psY_te)
                p.vector.tensor_scalar_mul(negm[:], mrow[:], -1.0).inc(ve)
                negm_ve = S["ve"].count
                p.tensor.wait(ve, negm_ve)
                for th in range(2):
                    tt = 2 * g + th
                    nsb = tt + 1
                    if prev_ve:
                        p.tensor.wait(ve, prev_ve)
                    qsl = lambda ic: qTs[:, ic * 256 + th * 128: ic * 256 + th * 128 + 128]
                    ngm = negm[0:1, th * 128:(th + 1) * 128]
                    s_te = {}
                    vld = {}

                    def emit_S(sbk):
                        j = sbk % 2
                        sl = "lda" if j == 0 else "ldb"
                        psS = pb[4 + j]
                        if vblk_done[j]:
                            p.gpsimd.wait(te, vblk_done[j])
                        _dma("gpsimd", vblk[j][:], vtok[sbk * 128:(sbk + 1) * 128, :], sl)
                        vld[sbk] = S[sl].count
                        if e_ac[j]:
                            p.tensor.wait(ac, e_ac[j])
                        for cc in range(NCH):
                            MM(psS[:, 0:128],
                               ktS[:, cc * T + sbk * 128: cc * T + sbk * 128 + 128],
                               qsl(cc), start=(cc == 0), stop=False)
                        MM(psS[:, 0:128], onesS[0:1, 0:128], ngm, start=False, stop=True).inc(te)
                        s_te[sbk] = S["te"].count

                    emit_S(0)
                    for sbk in range(nsb):
                        j = sbk % 2
                        if sbk + 1 < nsb:
                            emit_S(sbk + 1)
                        p.scalar.wait(te, s_te[sbk])
                        if pt_done[j]:
                            p.scalar.wait(te, pt_done[j])
                        p.scalar.activation(ptile[j][:], pb[4 + j][:, 0:128], AF.Exp).inc(ac)
                        e_ac[j] = S["ac"].count
                        if sbk == tt:
                            p.vector.wait(ac, e_ac[j])
                            p.vector.tensor_tensor(ptile[j][:], ptile[j][:], ustriS[:], op=ALU.mult).inc(ve)
                            p.tensor.wait(ve)
                        else:
                            p.tensor.wait(ac, e_ac[j])
                        p.tensor.wait(S["lda" if j == 0 else "ldb"], vld[sbk])
                        MM(pb[2][:, 0:1], ptile[j][:], ones128S[:],
                           start=(sbk == 0), stop=(sbk == nsb - 1), skip_group_check=True)
                        MM(pb[0][:, 0:512], ptile[j][:], vblk[j][:, 0:512],
                           start=(sbk == 0), stop=(sbk == nsb - 1), skip_group_check=True)
                        MM(pb[1][:, 0:512], ptile[j][:], vblk[j][:, 512:1024],
                           start=(sbk == 0), stop=(sbk == nsb - 1), skip_group_check=True).inc(te)
                        pt_done[j] = te.count
                        vblk_done[j] = te.count
                    sloop_te = te.count
                    p.vector.wait(te, sloop_te)
                    p.vector.reciprocal(rcol[:], pb[2][:, 0:1])
                    p.vector.tensor_scalar(obf[:, 0:512], pb[0][:, 0:512], rcol[:], None, op0=ALU.mult)
                    p.vector.tensor_scalar(obf[:, 512:1024], pb[1][:, 0:512], rcol[:], None, op0=ALU.mult).inc(ve)
                    obf_ve = S["ve"].count
                    p.tensor.wait(ve, obf_ve)
                    for ic in range(NCH):
                        mm_i = MM(ctx.pbx[:, ic * 128:(ic + 1) * 128], obf[:, ic * 128:(ic + 1) * 128],
                                  idbfS[:], start=True, stop=True, is_transpose=True)
                    mm_i.inc(te)
                    oTT_te = te.count
                    p.scalar.wait(te, oTT_te)
                    p.scalar.copy(oT[:], ctx.pbx[:, 0:1024]).inc(ac)
                    oT_ac = S["ac"].count
                    if prev_ve:
                        p.sync.wait(ve, prev_ve)
                    _dma("sync", x0t[:], x_src[tt * 128:(tt + 1) * 128, :], "ld")
                    p.tensor.wait(ac, oT_ac)
                    for ih in range(2):
                        psY = pb[4 + ih]
                        for dc in range(NCH):
                            MM(psY[:, 0:512], oT[:, dc * 128:(dc + 1) * 128],
                               wS2[:, dc * 1024 + ih * 512: dc * 1024 + (ih + 1) * 512],
                               start=(dc == 0), stop=False)
                        MM(psY[:, 0:512], onesS[0:1, 0:128],
                           bias2S[0:1, ih * 512:(ih + 1) * 512], start=False, stop=True).inc(te)
                    psY_te = te.count
                    p.vector.wait(te, psY_te)
                    p.vector.wait(ld)
                    p.vector.wait(S["stz"])
                    if qdx_dst is not None:
                        if zq_po:
                            p.vector.wait(po, zq_po)
                        if zq_ac:
                            p.vector.wait(ac, zq_ac)
                    p.vector.tensor_tensor(zs[:, 0:512], pb[4][:, 0:512], x0t[:, 0:512], op=ALU.add)
                    p.vector.tensor_tensor(zs[:, 512:1024], pb[5][:, 0:512], x0t[:, 512:1024], op=ALU.add).inc(ve)
                    prev_ve = ve.count
                    if dbg is not None and tt == 0:
                        p.gpsimd.wait(ve)
                        _dma("gpsimd", dbg["d_zs"][:, :], zs[:], "st")
                        _dma("gpsimd", dbg["d_obf"][:, :], obf[:], "st")
                    if dbg is not None and g == 1 and th == 1:
                        p.gpsimd.wait(ve)
                        _dma("gpsimd", dbg["d_qT"][:, :], qTs[:], "st")
                    p.vector.bn_stats(b6[:, 0:6], zs[:, 0:512])
                    p.vector.bn_stats(b6[:, 6:12], zs[:, 512:1024])
                    p.vector.bn_aggr(b2[:], b6[:])
                    p.vector.tensor_single_scalar(svar[:], b2[:, 1:2], 1e-5, op=ALU.add).inc(ve)
                    p.scalar.wait(ve)
                    p.scalar.activation(rstd[:], svar[:], AF.Sqrt).inc(ac)
                    p.vector.wait(ac)
                    p.vector.reciprocal(rstd[:], rstd[:])
                    p.vector.tensor_scalar(zs[:], zs[:], b2[:, 0:1], rstd[:],
                                           op0=ALU.subtract, op1=ALU.mult).inc(ve)
                    p.gpsimd.wait(ve)
                    _dma("gpsimd", x_dst[tt * 128:(tt + 1) * 128, :], zs[:], "stz")
                    if qdx_dst is not None:
                        p.vector.wait(st)
                        p.gpsimd.wait(st)
                        emit_qd(ctx, K, qbA, 1024, zs, qdst[tt % 2])
                        zq_ac = S["ac"].count
                        zq_po = S["po"].count
                        p.gpsimd.wait(ve)
                        _dma("gpsimd", qdx_dst[tt * 128:(tt + 1) * 128, :], qdst[tt % 2][:], "st")
                        nw = min(2, len(wq_queue))
                        for _ in range(nw):
                            Wq_, src_ap, dst_ap = wq_queue.pop(0)
                            bi = len(wq_queue) % 2
                            if qst_done[bi]:
                                p.sync.wait(ve, qst_done[bi][0])
                                p.sync.wait(po, qst_done[bi][1])
                                p.sync.wait(ac, qst_done[bi][2])
                            _dma("sync", qsrc[bi][:, 0:Wq_], src_ap, "ld2")
                            qn = S["ld2"].count
                            for e_ in ("vector", "gpsimd"):
                                getattr(p, e_).wait(S["ld2"], qn)
                                getattr(p, e_).wait(st)
                            p.scalar.wait(S["ld2"], qn)
                            emit_qd(ctx, K, qbA, Wq_, qsrc[bi], qdw[bi])
                            qst_done[bi] = (S["ve"].count, S["po"].count, S["ac"].count)
                            p.gpsimd.wait(ve)
                            _dma("gpsimd", dst_ap, qdw[bi][:, 0:Wq_], "st")
                    if dbg is not None and tt == 0:
                        p.gpsimd.wait(ve)
                        _dma("gpsimd", dbg["d_zpost"][:, :], zs[:], "st")
                        _dma("gpsimd", dbg["d_sm"][:, :], smean[:], "st")
                        _dma("gpsimd", dbg["d_sv"][:, :], svar[:], "st")
                        _dma("gpsimd", dbg["d_rs"][:, :], rstd[:], "st")
                        _dma("gpsimd", dbg["d_ss"][:, :], ssum[:], "st")
        if dbg is not None:
            p.sync.wait(st)
            _dma("sync", dbg["d_x1r"][:, :], x_dst[0:256, :], "ld")
        p.sync.wait(st)
        p.sync.wait(S["stz"])



IBLOCKS = [(0, 512), (512, 512), (1024, 512), (1536, 512), (2048, 512), (2560, 256)]


def emit_ln(ctx, K, zs, lnb, out_bf=None):
    p, S = ctx.p, ctx.S
    AF, ALU, AX = K.AF, K.ALU, K.AX
    ve, ac = S["ve"], S["ac"]
    p.vector.bn_stats(lnb["b6"][:, 0:6], zs[:, 0:512])
    p.vector.bn_stats(lnb["b6"][:, 6:12], zs[:, 512:1024])
    p.vector.bn_aggr(lnb["b2"][:], lnb["b6"][:])
    p.vector.tensor_single_scalar(lnb["svar"][:], lnb["b2"][:, 1:2], 1e-5, op=ALU.add).inc(ve)
    p.scalar.wait(ve)
    p.scalar.activation(lnb["rstd"][:], lnb["svar"][:], AF.Sqrt).inc(ac)
    p.vector.wait(ac)
    p.vector.reciprocal(lnb["rstd"][:], lnb["rstd"][:])
    p.vector.tensor_scalar(zs[:], zs[:], lnb["b2"][:, 0:1], lnb["rstd"][:],
                           op0=ALU.subtract, op1=ALU.mult).inc(ve)
    if out_bf is not None:
        p.vector.tensor_copy(out_bf[:], zs[:]).inc(ve)


def mk_lnb(ctx, stk, pfx, K):
    sb = lambda n, sh, dt: ctx.sb(stk, n, sh, dt)
    return {
        "dumb": sb(pfx + "_dumb", [128, 1024], K.BF16),
        "b6": sb(pfx + "_b6", [128, 12], K.F32),
        "b2": sb(pfx + "_b2", [128, 2], K.F32),
        "svar": sb(pfx + "_svar", [128, 1], K.F32),
        "rstd": sb(pfx + "_rstd", [128, 1], K.F32),
    }


def mk_qdb(ctx, stk, K, Wmax=2816, pfx="qd"):
    sb = lambda n, sh, dt: ctx.sb(stk, n, sh, dt)
    qb = {
        "z": sb(pfx + "_z", [128, Wmax], K.F32),
        "t1": sb(pfx + "_t1", [128, Wmax], K.F32),
        "s1": sb(pfx + "_s1", [128, Wmax], K.F32),
        "bq": sb(pfx + "_b", [128, Wmax], K.F32),
        "sgn": sb(pfx + "_sgn", [128, Wmax], K.BF16),
        "u": sb(pfx + "_u", [128, Wmax], K.F32),
        "red": sb(pfx + "_red", [128, Wmax // 16], K.F32),
        "scl": sb(pfx + "_scl", [128, Wmax // 16], K.F32),
        "sc4": sb(pfx + "_sc4", [128, Wmax // 16], K.F32),
        "rcp": sb(pfx + "_rcp", [128, Wmax // 16], K.F32),
        "sa": [sb(pfx + f"_sa{i}", [128, Wmax], K.BF16) for i in range(4)],
    }
    ths = (0.25, 0.75, 1.25, 1.75, 2.5, 3.5, 5.0)
    qb["th"] = {}
    for t in ths:
        ap = sb(pfx + "_th%d" % int(t * 100), [128, 1], K.F32)
        mi = ctx.p.vector.memset(ap[:], -t)
        qb["th"][t] = ap
    mi.inc(ctx.S["ve"])
    return qb


def emit_qd(ctx, K, qb, W, src, dst):
    p, S = ctx.p, ctx.S
    AF, ALU, AX = K.AF, K.ALU, K.AX
    ve, ac, po = S["ve"], S["ac"], S["po"]
    nb = W // 16
    W1 = ((nb * 93) // 100) * 16
    halves = [("vector", 0, W1), ("gpsimd", W1, W)]
    v3 = lambda ap, lo, hi: ap[:, lo:hi].rearrange("p (a b) -> p a b", b=16)
    p.scalar.wait(ve)
    p.scalar.wait(po)
    p.scalar.sign(qb["sgn"][:, 0:W], src[:, 0:W]).inc(ac)
    p.scalar.activation(qb["z"][:, 0:W], src[:, 0:W], AF.Abs).inc(ac)
    entry_ac = S["ac"].count
    p.vector.wait(po)
    p.vector.tensor_reduce(qb["red"][:, 0:nb], v3(src, 0, W), axis=AX.X, op=ALU.max,
                           apply_absolute_value=True)
    p.vector.tensor_scalar(qb["scl"][:, 0:nb], qb["red"][:, 0:nb], 1.0 / 6.0, 1e-12,
                           op0=ALU.mult, op1=ALU.max)
    p.vector.tensor_scalar_mul(qb["sc4"][:, 0:nb], qb["scl"][:, 0:nb], 0.25)
    p.vector.reciprocal(qb["rcp"][:, 0:nb], qb["scl"][:, 0:nb]).inc(ve)
    p.gpsimd.wait(ve)
    for eng, lo, hi in halves:
        e = getattr(p, eng)
        bl, bh = lo // 16, hi // 16
        rb = qb["rcp"][:, bl:bh].unsqueeze(2).broadcast_to([128, bh - bl, 16])
        sb_ = qb["scl"][:, bl:bh].unsqueeze(2).broadcast_to([128, bh - bl, 16])
        s4_ = qb["sc4"][:, bl:bh].unsqueeze(2).broadcast_to([128, bh - bl, 16])
        z, t1, s1, b = qb["z"][:, lo:hi], qb["t1"][:, lo:hi], qb["s1"][:, lo:hi], qb["bq"][:, lo:hi]
        u = qb["u"][:, lo:hi]
        e.wait(ac, entry_ac)
        if eng == "vector":
            sa, sb2, sc, sd = (qb["sa"][i][:, lo:hi] for i in range(4))
            e.tensor_tensor(v3(qb["z"], lo, hi), v3(qb["z"], lo, hi), rb, op=ALU.mult).inc(ve)
            zv = S["ve"].count
            p.scalar.wait(ve, zv)
            p.scalar.activation(sa, z, AF.Sign, bias=qb["th"][0.25][:])
            p.scalar.activation(sb2, z, AF.Sign, bias=qb["th"][0.75][:])
            p.scalar.activation(sc, z, AF.Sign, bias=qb["th"][1.25][:])
            p.scalar.activation(sd, z, AF.Sign, bias=qb["th"][1.75][:]).inc(ac)
            c1 = S["ac"].count
            e.wait(ac, c1)
            e.tensor_tensor(sa, sa, sb2, op=ALU.add)
            e.tensor_tensor(sc, sc, sd, op=ALU.add)
            e.tensor_tensor(sa, sa, sc, op=ALU.add).inc(ve)
            v1 = S["ve"].count
            p.scalar.wait(ve, v1)
            p.scalar.activation(sb2, z, AF.Sign, bias=qb["th"][2.5][:])
            p.scalar.activation(sc, z, AF.Sign, bias=qb["th"][3.5][:])
            p.scalar.activation(sd, z, AF.Sign, bias=qb["th"][5.0][:]).inc(ac)
            c2 = S["ac"].count
            e.wait(ac, c2)
            e.tensor_tensor(sb2, sb2, sc, op=ALU.add)
            e.tensor_tensor(sb2, sb2, sd, op=ALU.add)
            e.tensor_tensor(sb2, sb2, sd, op=ALU.add)
            e.scalar_tensor_tensor(t1, sb2, 2.0, sa, op0=ALU.mult, op1=ALU.add)
            e.tensor_single_scalar(t1, t1, 12.0, op=ALU.add)
            e.tensor_tensor(s1, t1, qb["sgn"][:, lo:hi], op=ALU.mult)
            ei = e.tensor_tensor(v3(dst, lo, hi), v3(qb["s1"], lo, hi), s4_, op=ALU.mult)
            ei.inc(ve)
        else:
            e.tensor_tensor(v3(qb["z"], lo, hi), v3(qb["z"], lo, hi), rb, op=ALU.mult)
            e.tensor_single_scalar(s1, z, 5.0, op=ALU.is_gt)
            e.tensor_single_scalar(u, z, 3.5, op=ALU.is_gt)
            e.tensor_tensor(t1, u, s1, op=ALU.add)
            e.tensor_single_scalar(u, z, 2.5, op=ALU.is_gt)
            e.tensor_tensor(t1, t1, u, op=ALU.add)
            e.tensor_tensor(t1, t1, s1, op=ALU.add)
            e.tensor_single_scalar(b, z, 0.25, op=ALU.is_gt)
            for th in (0.75, 1.25, 1.75):
                e.tensor_single_scalar(u, z, th, op=ALU.is_gt)
                e.tensor_tensor(b, b, u, op=ALU.add)
            e.tensor_scalar_mul(b, b, 0.5)
            e.tensor_tensor(t1, t1, b, op=ALU.add)
            e.wait(ac, entry_ac)
            e.tensor_tensor(s1, t1, qb["sgn"][:, lo:hi], op=ALU.mult)
            ei = e.tensor_tensor(v3(dst, lo, hi), v3(qb["s1"], lo, hi), sb_, op=ALU.mult)
            ei.inc(po)


def emit_phase_b(ctx, H, x_src, x_dst, K, dst_bf=None, coquant=False):
    p, nc, S = ctx.p, ctx.nc, ctx.S
    F32, BF16, AF, ALU, AX = K.F32, K.BF16, K.AF, K.ALU, K.AX
    NCH = K.NCH
    ld, st, te, ve, ac, po = S["ld"], S["st"], S["te"], S["ve"], S["ac"], S["po"]
    pb = ctx.pb
    MM = p.tensor.matmul
    _dma = lambda q, out, in_, sem="ld": getattr(p, q).dma_start(out=out, in_=in_).inc(S[sem], 16)
    _dmaT = lambda out, in_: p.sync.dma_start_transpose(out, in_).inc(ld, 16)
    chunked = K.chunked

    with ExitStack() as stk:
        sb = lambda n, sh, dt: ctx.sb(stk, n, sh, dt)
        qb = mk_qdb(ctx, stk, K)
        lnb = mk_lnb(ctx, stk, "b", K)
        stkb1 = ExitStack()
        srcb2 = [ctx.sb(stkb1, f"b_src{i}", [128, 2816], F32) for i in range(2)]
        dstb2 = [ctx.sb(stkb1, f"b_dst{i}", [128, 2816], BF16) for i in range(2)]
        tiles = []
        if not coquant:
            for t in range(32):
                tiles.append((1024, x_src[t * 128:(t + 1) * 128, :], H["qdx"][t * 128:(t + 1) * 128, :]))
        wlist = (("wd", "qwd", 8, 2816),) if coquant else (
            ("wg", "qwg", 22, 1024), ("wu", "qwu", 22, 1024), ("wd", "qwd", 8, 2816))
        for wname, qname, nch, Wc in wlist:
            for ch in range(nch):
                tiles.append((Wc, H[wname][ch * 128:(ch + 1) * 128, :],
                              H[qname][ch * 128:(ch + 1) * 128, :]))
        buf_done = [(0, 0, 0, 0), (0, 0, 0, 0)]
        ld_cnt = {}

        def b1_load(idx):
            W, load_in, _ = tiles[idx]
            bi = idx % 2
            sl = "lda" if bi == 0 else "ldb"
            dn = buf_done[bi]
            if dn[0]:
                p.sync.wait(ve, dn[0])
                p.sync.wait(po, dn[1])
                p.sync.wait(ac, dn[2])
                p.sync.wait(st, dn[3])
            _dma("sync", srcb2[bi][:, 0:W], load_in, sl)
            ld_cnt[idx] = S[sl].count

        b1_load(0)
        for idx in range(len(tiles)):
            W, _, store_out = tiles[idx]
            bi = idx % 2
            if idx + 1 < len(tiles):
                b1_load(idx + 1)
            for e in ("vector", "gpsimd", "scalar"):
                getattr(p, e).wait(S["lda" if bi == 0 else "ldb"], ld_cnt[idx])
                getattr(p, e).wait(st, buf_done[bi][3])
            emit_qd(ctx, K, qb, W, srcb2[bi], dstb2[bi])
            p.gpsimd.wait(ve)
            _dma("gpsimd", store_out, dstb2[bi][:, 0:W], "st")
            buf_done[bi] = (S["ve"].count, S["po"].count, S["ac"].count, S["st"].count)
        ctx.barrier()
        stkb1.close()

        with ExitStack() as stk2:
            sb2 = lambda n, sh, dt: ctx.sb(stk2, n, sh, dt)
            qwgT = sb2("b_qwgT", [128, NCH * 2816], BF16)
            qwuT = sb2("b_qwuT", [128, NCH * 2816], BF16)
            qxT = sb2("b_qxT", [128, NCH * 512], BF16)
            dstb = sb2("b_dstb2", [128, 2816], BF16)
            sg = sb2("b_sg", [128, 1024], F32)
            hbuf = sb2("b_h", [128, 2816], F32)
            for c in range(NCH):
                _dmaT(qwgT[:, c * 2816:(c + 1) * 2816], H["qwg"][:, c * 128:(c + 1) * 128])
                _dmaT(qwuT[:, c * 2816:(c + 1) * 2816], H["qwu"][:, c * 128:(c + 1) * 128])
            for g in range(8):
                p.sync.wait(te)
                for c in range(NCH):
                    _dmaT(qxT[:, c * 512:(c + 1) * 512],
                          H["qdx"][g * 512:(g + 1) * 512, c * 128:(c + 1) * 128])
                for tl in range(4):
                    con_done = [(0, 0), (0, 0)]
                    for nib, (ib, bw) in enumerate(IBLOCKS):
                        jp = nib % 2
                        psG, psU = pb[2 * jp], pb[2 * jp + 1]
                        p.tensor.wait(ld)
                        if con_done[jp][0]:
                            p.tensor.wait(ac, con_done[jp][0])
                            p.tensor.wait(ve, con_done[jp][1])
                        for cc in range(NCH):
                            MM(psG[:, 0:bw], qxT[:, cc * 512 + tl * 128: cc * 512 + tl * 128 + 128],
                               qwgT[:, cc * 2816 + ib: cc * 2816 + ib + bw], start=(cc == 0), stop=(cc == NCH - 1))
                        for cc in range(NCH):
                            mm = MM(psU[:, 0:bw], qxT[:, cc * 512 + tl * 128: cc * 512 + tl * 128 + 128],
                                    qwuT[:, cc * 2816 + ib: cc * 2816 + ib + bw], start=(cc == 0),
                                    stop=(cc == NCH - 1))
                        mm.inc(te)
                        p.scalar.wait(te)
                        p.scalar.wait(po)
                        p.scalar.wait(ve)
                        p.scalar.activation(sg[:, jp * 512: jp * 512 + bw], psG[:, 0:bw], AF.Silu).inc(ac)
                        p.vector.wait(ac)
                        p.vector.wait(po)
                        p.vector.tensor_tensor(hbuf[:, ib:ib + bw], sg[:, jp * 512: jp * 512 + bw], psU[:, 0:bw],
                                               op=ALU.mult).inc(ve)
                        con_done[jp] = (S["ac"].count, S["ve"].count)
                    for e in ("vector", "gpsimd", "scalar"):
                        getattr(p, e).wait(ve)
                        getattr(p, e).wait(st)
                    emit_qd(ctx, K, qb, 2816, hbuf, dstb)
                    p.gpsimd.wait(ve)
                    tt = g * 4 + tl
                    _dma("gpsimd", H["qdh"][tt * 128:(tt + 1) * 128, :], dstb[:, 0:2816], "st")
        ctx.barrier()

        with ExitStack() as stk3:
            sb3 = lambda n, sh, dt: ctx.sb(stk3, n, sh, dt)
            qwdT = sb3("b_qwdT", [128, 22 * 1024], BF16)
            qhT = sb3("b_qhT", [128, 22 * 512], BF16)
            x1t = sb3("b_x1t", [128, 1024], F32)
            zs = sb3("b_zs", [128, 1024], F32)
            zbf = sb3("b_zbf", [128, 1024], BF16)
            for ch in range(22):
                _dmaT(qwdT[:, ch * 1024:(ch + 1) * 1024], H["qwd"][:, ch * 128:(ch + 1) * 128])
            bank_done = [0, 0]
            for g in range(8):
                p.sync.wait(te)
                for ch in range(22):
                    _dmaT(qhT[:, ch * 512:(ch + 1) * 512],
                          H["qdh"][g * 512:(g + 1) * 512, ch * 128:(ch + 1) * 128])
                for tl in range(4):
                    tt = g * 4 + tl
                    jp = tt % 2
                    psD0, psD1 = pb[2 * jp], pb[2 * jp + 1]
                    p.sync.wait(ve)
                    _dma("sync", x1t[:], x_src[tt * 128:(tt + 1) * 128, :], "ld")
                    p.tensor.wait(ld)
                    if tt >= 2:
                        p.tensor.wait(ve, bank_done[jp])
                    for ch in range(22):
                        MM(psD0[:, 0:512], qhT[:, ch * 512 + tl * 128: ch * 512 + tl * 128 + 128],
                           qwdT[:, ch * 1024: ch * 1024 + 512], start=(ch == 0), stop=(ch == 21))
                    for ch in range(22):
                        mm = MM(psD1[:, 0:512], qhT[:, ch * 512 + tl * 128: ch * 512 + tl * 128 + 128],
                                qwdT[:, ch * 1024 + 512: ch * 1024 + 1024], start=(ch == 0),
                                stop=(ch == 21))
                    mm.inc(te)
                    p.vector.wait(te)
                    p.vector.wait(ld)
                    p.vector.wait(st)
                    p.vector.tensor_tensor(zs[:, 0:512], psD0[:, 0:512], x1t[:, 0:512], op=ALU.add)
                    p.vector.tensor_tensor(zs[:, 512:1024], psD1[:, 0:512], x1t[:, 512:1024],
                                           op=ALU.add).inc(ve)
                    bank_done[jp] = S["ve"].count
                    emit_ln(ctx, K, zs, lnb, out_bf=zbf if dst_bf is not None else None)
                    p.gpsimd.wait(ve)
                    _dma("gpsimd", x_dst[tt * 128:(tt + 1) * 128, :], zs[:], "st")
                    if dst_bf is not None:
                        _dma("gpsimd", dst_bf[tt * 128:(tt + 1) * 128, :], zbf[:], "st")
        p.sync.wait(st)


def decl_b(nc, H, K):
    F32, BF16 = K.F32, K.BF16
    def di(name, shape, dt):
        H[name] = nc.dram_tensor(name, list(shape), dt, kind="ExternalInput").ap()
    def ds(name, shape, dt):
        H[name] = nc.dram_tensor(name, list(shape), dt, kind="Internal").ap()
    di("wg", (2816, 1024), F32)
    di("wu", (2816, 1024), F32)
    di("wd", (1024, 2816), F32)
    ds("qdx", (4096, 1024), BF16)
    ds("qdh", (4096, 2816), BF16)
    ds("qwg", (2816, 1024), BF16)
    ds("qwu", (2816, 1024), BF16)
    ds("qwd", (1024, 2816), BF16)
    return H


def host_inputs_b(inp):
    f32 = np.float32
    return {
        "wg": np.ascontiguousarray(np.asarray(inp["L0_ffn_w_gate"], f32)),
        "wu": np.ascontiguousarray(np.asarray(inp["L0_ffn_w_up"], f32)),
        "wd": np.ascontiguousarray(np.asarray(inp["L0_ffn_w_down"], f32)),
    }




def emit_phase_c(ctx, H, x_src, x_src_bf, x_dst, K, dst_bf=None):
    p, nc, S = ctx.p, ctx.nc, ctx.S
    F32, BF16, AF, ALU, AX = K.F32, K.BF16, K.AF, K.ALU, K.AX
    NCH, T = K.NCH, K.T
    ld, st, te, ve, ac, po, ld2 = (S["ld"], S["st"], S["te"], S["ve"], S["ac"], S["po"], S["ld2"])
    pb, pbt, pbx = ctx.pb, ctx.pbt, ctx.pbx
    MM = p.tensor.matmul
    _dma = lambda q, out, in_, sem="ld": getattr(p, q).dma_start(out=out, in_=in_).inc(S[sem], 16)
    _dmaT = lambda out, in_: p.sync.dma_start_transpose(out, in_).inc(ld, 16)
    chunked = K.chunked

    with ExitStack() as stk:
        sb = lambda n, sh, dt: ctx.sb(stk, n, sh, dt)
        lnb = mk_lnb(ctx, stk, "c", K)
        onesS = sb("c_ones", [1, 512], BF16)
        ustriS = sb("c_ustri", [128, 128], BF16)
        identS = sb("c_ident", [128, 128], BF16)
        gnwS = sb("c_gnw", [128, 256], F32)
        skw2S = sb("c_skw2", [16, 256], BF16)
        svw2S = sb("c_svw2", [16, 256], BF16)
        skb2S = sb("c_skb2", [1, 256], BF16)
        svb2S = sb("c_svb2", [1, 256], BF16)
        sk1T = sb("c_sk1T", [16, T], BF16)
        sv1T = sb("c_sv1T", [16, T], BF16)
        wkTs = sb("c_wkT", [128, 2 * T], BF16)
        wvTs = sb("c_wvT", [128, 2 * T], BF16)
        zkTs = sb("c_zkT", [128, 2 * T], BF16)
        zvTs = sb("c_zvT", [128, 2 * T], BF16)

        p.vector.memset(onesS[:], 1.0)
        _dma("sync", ustriS[:], H["ustri"][:, :])
        _dma("sync", identS[:], H["identbf"][:, :])
        _dma("sync", gnwS[:], H["gnw_bc"][:, :])
        _dma("gpsimd", skw2S[:], H["skw2"][:, :], "st")
        _dma("gpsimd", svw2S[:], H["svw2"][:, :], "st")
        _dma("gpsimd", skb2S[:], H["skb2"][:, :], "st")
        _dma("gpsimd", svb2S[:], H["svb2"][:, :], "st")

        with ExitStack() as s1k:
            sb1 = lambda n, sh, dt: ctx.sb(s1k, n, sh, dt)
            wS = sb1("c_w", [128, NCH * 1024], BF16)
            skw1S = sb1("c_skw1", [128, NCH * 16], BF16)
            svw1S = sb1("c_svw1", [128, NCH * 16], BF16)
            cfm = sb1("c_cfm", [128, T], BF16)
            sfm = sb1("c_sfm", [128, T], BF16)
            ctk = sb1("c_ctk", [128, 128], BF16)
            stk_ = sb1("c_stk", [128, 128], BF16)
            xT = [sb1(f"c_xT{i}", [128, NCH * 512], BF16) for i in range(2)]
            e0 = sb1("c_e0", [128, 512], BF16)
            e1 = sb1("c_e1", [128, 512], BF16)
            t1b = sb1("c_t1", [128, 512], BF16)
            t2b = sb1("c_t2", [128, 512], BF16)
            ktile = sb1("c_ktile", [128, 1024], BF16)
            gtile = sb1("c_gtile", [128, 1024], F32)
            clipb = sb1("c_clip", [128, 512], F32)

            _dma("gpsimd", skw1S[:].rearrange("p (c x) -> p c x", c=NCH), chunked(H["skw1"]), "st")
            _dma("gpsimd", svw1S[:].rearrange("p (c x) -> p c x", c=NCH), chunked(H["svw1"]), "st")
            _dma("sync", cfm[:], H["c1cos"][:, :])
            _dma("sync", sfm[:], H["c1sin"][:, :])

            def load_w(name):
                p.gpsimd.wait(te)
                _dma("gpsimd", wS[:].rearrange("p (c x) -> p c x", c=NCH), chunked(H[name]), "st")

            def xbar_x2(i, tb):
                for c in range(NCH):
                    _dmaT(xT[i][:, c * 512:(c + 1) * 512],
                          x_src_bf[tb * 512:(tb + 1) * 512, c * 128:(c + 1) * 128])

            load_w("wq2")
            p.vector.wait(S["st"])
            p.vector.tensor_scalar_mul(wS[:], wS[:], 1.0 / 16.0).inc(ve)
            p.tensor.wait(ve)
            xT_done = [0, 0]
            q_done = [0, 0]
            for tb in range(8):
                i = tb % 2
                if xT_done[i]:
                    p.sync.wait(te, xT_done[i])
                xbar_x2(i, tb)
                p.tensor.wait(ld)
                p.tensor.wait(S["st"])
                for (w1S, oT) in ((skw1S, sk1T), (svw1S, sv1T)):
                    p.tensor.wait(ac)
                    for cc in range(NCH):
                        MM(pbt[0:16, 0:512], w1S[:, cc * 16:(cc + 1) * 16],
                           xT[i][:, cc * 512:(cc + 1) * 512], start=(cc == 0), stop=(cc == NCH - 1)).inc(te)
                    p.scalar.wait(te)
                    p.scalar.copy(oT[0:16, tb * 512:(tb + 1) * 512], pbt[0:16, 0:512]).inc(ac)
                for h in range(4):
                    hp = h % 2
                    for u in range(2):
                        psQ = pb[2 * hp + u]
                        ic = 2 * h + u
                        if q_done[hp]:
                            p.tensor.wait(ac, q_done[hp])
                        for cc in range(NCH):
                            MM(psQ[:, 0:512], wS[:, cc * 1024 + ic * 128: cc * 1024 + ic * 128 + 128],
                               xT[i][:, cc * 512:(cc + 1) * 512], start=(cc == 0), stop=(cc == NCH - 1)).inc(te)
                    p.scalar.wait(te)
                    p.scalar.wait(ve)
                    p.scalar.copy(e0[:], pb[2 * hp][:, 0:512])
                    p.scalar.copy(e1[:], pb[2 * hp + 1][:, 0:512]).inc(ac)
                    q_done[hp] = S["ac"].count
                    cs = cfm[:, tb * 512:(tb + 1) * 512]
                    sn = sfm[:, tb * 512:(tb + 1) * 512]
                    p.vector.wait(ac)
                    p.vector.wait(st)
                    p.vector.tensor_tensor(t1b[:], e0[:], cs, op=ALU.mult)
                    p.vector.tensor_tensor(t2b[:], e1[:], sn, op=ALU.mult)
                    p.vector.tensor_tensor(t1b[:], t1b[:], t2b[:], op=ALU.subtract)
                    p.vector.tensor_tensor(t2b[:], e1[:], cs, op=ALU.mult)
                    p.vector.tensor_tensor(e1[:], e0[:], sn, op=ALU.mult)
                    p.vector.tensor_tensor(t2b[:], t2b[:], e1[:], op=ALU.add).inc(ve)
                    p.gpsimd.wait(ve)
                    _dma("gpsimd", H["qT_hbm"][(2 * h) * 128:(2 * h + 1) * 128, tb * 512:(tb + 1) * 512],
                         t1b[:], "st")
                    _dma("gpsimd", H["qT_hbm"][(2 * h + 1) * 128:(2 * h + 2) * 128, tb * 512:(tb + 1) * 512],
                         t2b[:], "st")
                xT_done[i] = te.count

            for (wname, dst, isf32, rope) in (("wk2", H["k_hbm"], False, True),
                                              ("wv2", H["v_hbm"], False, False),
                                              ("wg2", H["g_hbm"], True, False)):
                load_w(wname)
                xT_done = [0, 0]
                kv_done = [0, 0, 0, 0]
                for tb in range(8):
                    i = tb % 2
                    if xT_done[i]:
                        p.sync.wait(te, xT_done[i])
                    xbar_x2(i, tb)
                    p.tensor.wait(ld)
                    p.tensor.wait(S["st"])
                    for tl in range(4):
                        tt = tb * 4 + tl
                        stage = gtile if isf32 else ktile
                        for ih in range(2):
                            bi = (2 * tl + ih) % 4
                            psV = pb[2 + bi]
                            if kv_done[bi]:
                                p.tensor.wait(ac, kv_done[bi])
                            for cc in range(NCH):
                                MM(psV[:, 0:512],
                                   xT[i][:, cc * 512 + tl * 128: cc * 512 + tl * 128 + 128],
                                   wS[:, cc * 1024 + ih * 512: cc * 1024 + (ih + 1) * 512],
                                   start=(cc == 0), stop=(cc == NCH - 1)).inc(te)
                            p.scalar.wait(te)
                            p.scalar.wait(st)
                            p.scalar.wait(ve)
                            p.scalar.copy(stage[:, ih * 512:(ih + 1) * 512], psV[:, 0:512]).inc(ac)
                            kv_done[bi] = S["ac"].count
                        if rope:
                            _dma("sync", ctk[:], H["c1cosTok"][tt * 128:(tt + 1) * 128, :])
                            _dma("sync", stk_[:], H["c1sinTok"][tt * 128:(tt + 1) * 128, :])
                            kv = ktile[:].rearrange("p (h u d) -> p h u d", h=4, u=2)
                            h4 = lambda ap: ap[:].rearrange("p (h d) -> p h d", h=4)
                            cb = ctk[:].unsqueeze(1).broadcast_to([128, 4, 128])
                            sbb = stk_[:].unsqueeze(1).broadcast_to([128, 4, 128])
                            p.vector.wait(ac)
                            p.vector.wait(ld)
                            p.vector.tensor_tensor(h4(t1b), kv[:, :, 0, :], cb, op=ALU.mult)
                            p.vector.tensor_tensor(h4(t2b), kv[:, :, 1, :], sbb, op=ALU.mult)
                            p.vector.tensor_tensor(t1b[:], t1b[:], t2b[:], op=ALU.subtract)
                            p.vector.tensor_tensor(h4(t2b), kv[:, :, 1, :], cb, op=ALU.mult)
                            p.vector.tensor_tensor(h4(e0), kv[:, :, 0, :], sbb, op=ALU.mult)
                            p.vector.tensor_tensor(t2b[:], t2b[:], e0[:], op=ALU.add)
                            p.vector.tensor_copy(kv[:, :, 0, :], h4(t1b))
                            p.vector.tensor_copy(kv[:, :, 1, :], h4(t2b)).inc(ve)
                            p.gpsimd.wait(ve)
                        else:
                            p.gpsimd.wait(ac)
                        _dma("gpsimd", dst[tt * 128:(tt + 1) * 128, :], stage[:], "st")
                    xT_done[i] = te.count

            for tt in range(32):
                for (w2S, b2S, dst) in ((skw2S, skb2S, H["wk_hbm"]), (svw2S, svb2S, H["wv_hbm"])):
                    p.tensor.wait(ve)
                    p.tensor.wait(ac)
                    MM(pb[0][:, 0:256], sk1T[0:16, tt * 128:(tt + 1) * 128] if w2S is skw2S else
                       sv1T[0:16, tt * 128:(tt + 1) * 128], w2S[:], start=True, stop=False)
                    MM(pb[0][:, 0:256], onesS[0:1, 0:128], b2S[:], start=False, stop=True).inc(te)
                    p.vector.wait(te)
                    p.vector.tensor_scalar(clipb[:, 0:256], pb[0][:, 0:256], 32.0, -32.0,
                                           op0=ALU.min, op1=ALU.max).inc(ve)
                    p.scalar.wait(ve)
                    p.scalar.wait(st)
                    p.scalar.activation(ktile[:, 0:256], clipb[:, 0:256], AF.Exp).inc(ac)
                    p.gpsimd.wait(ac)
                    _dma("gpsimd", dst[tt * 128:(tt + 1) * 128, :], ktile[:, 0:256], "st")
            for u in range(2):
                for (w2S, b2S, s1S, wT) in ((skw2S, skb2S, sk1T, wkTs), (svw2S, svb2S, sv1T, wvTs)):
                    for tb in range(8):
                        p.tensor.wait(ve)
                        MM(pb[0][:, 0:512], w2S[0:16, u * 128:(u + 1) * 128],
                           s1S[0:16, tb * 512:(tb + 1) * 512], start=True, stop=False)
                        MM(pb[0][:, 0:512], b2S[0:1, u * 128:(u + 1) * 128],
                           onesS[0:1, 0:512], start=False, stop=True).inc(te)
                        p.vector.wait(te)
                        p.vector.tensor_scalar(clipb[:, 0:512], pb[0][:, 0:512], 32.0, -32.0,
                                               op0=ALU.min, op1=ALU.max).inc(ve)
                        p.scalar.wait(ve)
                        p.scalar.activation(wT[:, u * T + tb * 512: u * T + (tb + 1) * 512],
                                            clipb[:, 0:512], AF.Exp).inc(ac)
            p.vector.wait(ac)
            for (wT, zT) in ((wkTs, zkTs), (wvTs, zvTs)):
                for u in range(2):
                    p.vector.tensor_tensor_scan(zT[:, u * T:(u + 1) * T], wT[:, u * T:(u + 1) * T],
                                                wT[:, u * T:(u + 1) * T], 0.0,
                                                op0=ALU.add, op1=ALU.bypass).inc(ve)
        ctx.barrier()

        with ExitStack() as s2k:
            sb2 = lambda n, sh, dt: ctx.sb(s2k, n, sh, dt)
            qTh = sb2("c_qTh", [128, 2 * T], BF16)
            kTh = sb2("c_kTh", [128, 2 * T], BF16)
            ktokh = sb2("c_ktokh", [128, 32 * 256], BF16)
            vh = sb2("c_vh", [128, 32 * 256], BF16)
            wkh = sb2("c_wkh", [128, 32 * 64], BF16)
            wvh = sb2("c_wvh", [128, 32 * 64], BF16)
            wvTh = sb2("c_wvTh", [128, 32 * 128], BF16)
            g2b = [sb2(f"c_g2b{i}", [128, 256], F32) for i in range(2)]
            slocb = sb2("c_sloc", [128, 128], BF16)
            abf = sb2("c_abf", [128, 128], BF16)
            pTb = sb2("c_pT", [128, 128], BF16)
            Skbf = [sb2(f"c_Skbf{i}", [128, 128], BF16) for i in range(2)]
            Svbf = [sb2(f"c_Svbf{i}", [128, 256], BF16) for i in range(2)]
            rk = sb2("c_rk", [128, 64], F32)
            rv = sb2("c_rv", [128, 64], F32)
            lg = sb2("c_lg", [128, 64], F32)
            mx = sb2("c_mx", [128, 1], F32)
            negmx = sb2("c_negmx", [128, 1], F32)
            es = sb2("c_es", [128, 64], F32)
            se = sb2("c_se", [128, 1], F32)
            rs = sb2("c_rs", [128, 1], F32)
            p1 = sb2("c_p1", [128, 64], F32)
            pbf = sb2("c_pbf", [128, 64], BF16)
            odum = sb2("c_odum", [128, 256], BF16)
            osum = sb2("c_osum", [128, 1], F32)
            ro = sb2("c_ro", [128, 1], F32)
            sgm = sb2("c_sgm", [128, 256], F32)
            gg = sb2("c_gg", [128, 256], F32)
            on_s = sb2("c_on", [128, 256], F32)
            on2 = sb2("c_on2", [128, 256], F32)
            obft = sb2("c_obft", [128, 256], BF16)

            for h in range(4):
                ctx.barrier()
                u2 = h // 2
                o2 = (h % 2) * 64
                _dma("sync", qTh[:].rearrange("p (c t) -> p c t", c=2),
                     H["qT_hbm"][h * 256:(h + 1) * 256, :].rearrange("(c p) t -> p c t", p=128))
                for u in range(2):
                    _dmaT(kTh[:, u * T:(u + 1) * T], H["k_hbm"][:, h * 256 + u * 128: h * 256 + (u + 1) * 128])
                _dma("sync", ktokh[:].rearrange("p (n c) -> p n c", n=32),
                     H["k_hbm"].rearrange("(n p) c -> p n c", p=128)[:, :, h * 256:(h + 1) * 256])
                _dma("sync", vh[:].rearrange("p (n c) -> p n c", n=32),
                     H["v_hbm"].rearrange("(n p) c -> p n c", p=128)[:, :, h * 256:(h + 1) * 256])
                _dma("sync", wkh[:].rearrange("p (n c) -> p n c", n=32),
                     H["wk_hbm"].rearrange("(n p) c -> p n c", p=128)[:, :, h * 64:(h + 1) * 64])
                _dma("sync", wvh[:].rearrange("p (n c) -> p n c", n=32),
                     H["wv_hbm"].rearrange("(n p) c -> p n c", p=128)[:, :, h * 64:(h + 1) * 64])
                _dmaT(wvTh[:, :], H["wv_hbm"][:, (h // 2) * 128:(h // 2 + 1) * 128])
                gld = [0, 0]
                a_E = v_sloc = v_rcp = v_pbf = v_abf = 0
                a_pT = v_ro = v_on = v_obf = a_sq = a_sg = 0
                t_U = t_S = t_L = t_T = t_A = t_O = 0
                for c in range(32):
                    j = c % 2
                    jm = (c - 1) % 2
                    if gld[j]:
                        p.sync.wait(po, gld[j])
                    _dma("sync", g2b[j][:], H["g_hbm"][c * 128:(c + 1) * 128, h * 256:(h + 1) * 256],
                         "lda" if j == 0 else "ldb")
                    gcnt = S["lda" if j == 0 else "ldb"].count
                    if c == 0:
                        p.tensor.wait(ld)
                    else:
                        p.tensor.wait(ac, a_E)
                    MM(pb[2][:, 0:64], ktokh[:, c * 256: c * 256 + 128], wkh[:, c * 64:(c + 1) * 64],
                       start=(c == 0), stop=(c == 31), skip_group_check=True)
                    MM(pb[3][:, 0:64], ktokh[:, c * 256 + 128: c * 256 + 256], wkh[:, c * 64:(c + 1) * 64],
                       start=(c == 0), stop=(c == 31), skip_group_check=True)
                    MM(pb[4][0:64, 0:256], wvh[:, c * 64:(c + 1) * 64], vh[:, c * 256:(c + 1) * 256],
                       start=(c == 0), stop=(c == 31), skip_group_check=True).inc(te)
                    t_U = te.count
                    if v_sloc:
                        p.tensor.wait(ve, v_sloc)
                    for u in range(2):
                        MM(pb[0][:, 0:128], kTh[:, u * T + c * 128: u * T + c * 128 + 128],
                           qTh[:, u * T + c * 128: u * T + c * 128 + 128],
                           start=(u == 0), stop=(u == 1)).inc(te)
                    t_S = te.count
                    if v_rcp:
                        p.tensor.wait(ve, v_rcp)
                    MM(pbx[:, 0:64], zkTs[o2:o2 + 64, u2 * T + c * 128: u2 * T + c * 128 + 128],
                       identS[o2:o2 + 64, o2:o2 + 64], start=True, stop=True, is_transpose=True)
                    MM(pbx[:, 64:128], zvTs[o2:o2 + 64, u2 * T + c * 128: u2 * T + c * 128 + 128],
                       identS[o2:o2 + 64, o2:o2 + 64], start=True, stop=True, is_transpose=True).inc(te)
                    p.scalar.wait(te, t_U)
                    p.scalar.copy(Skbf[j][:, 0:64], pb[2][:, 0:64])
                    p.scalar.copy(Skbf[j][:, 64:128], pb[3][:, 0:64])
                    p.scalar.copy(Svbf[j][o2:o2 + 64, :], pb[4][0:64, 0:256]).inc(ac)
                    a_E = S["ac"].count
                    p.vector.wait(te, t_S)
                    p.vector.tensor_tensor(slocb[:], pb[0][:, 0:128], ustriS[:], op=ALU.mult).inc(ve)
                    v_sloc = S["ve"].count
                    p.tensor.wait(ve, v_sloc)
                    if c == 0:
                        MM(pbt[:, 0:64], slocb[:], wkh[:, 0:64], start=True, stop=True).inc(te)
                    else:
                        MM(pbt[:, 0:64], slocb[:], wkh[:, c * 64:(c + 1) * 64], start=True, stop=False)
                        MM(pbt[:, 0:64], qTh[:, 0 * T + c * 128: 0 * T + c * 128 + 128],
                           Skbf[jm][:, 0:64], start=False, stop=False)
                        MM(pbt[:, 0:64], qTh[:, 1 * T + c * 128: 1 * T + c * 128 + 128],
                           Skbf[jm][:, 64:128], start=False, stop=True).inc(te)
                    t_L = te.count
                    p.vector.wait(te, t_L)
                    p.vector.reciprocal(rk[:], pbx[:, 0:64])
                    p.vector.reciprocal(rv[:], pbx[:, 64:128])
                    p.vector.tensor_tensor(lg[:], pbt[:, 0:64], rk[:], op=ALU.mult)
                    p.vector.tensor_reduce(mx[:], lg[:], axis=AX.X, op=ALU.max)
                    p.vector.tensor_scalar_mul(negmx[:], mx[:], -1.0).inc(ve)
                    v_rcp = S["ve"].count
                    p.scalar.wait(ve, v_rcp)
                    p.scalar.activation(es[:], lg[:], AF.Exp, bias=negmx[:], accum_out=se[:]).inc(ac)
                    a_exp = S["ac"].count
                    p.vector.wait(ac, a_exp)
                    p.vector.reciprocal(rs[:], se[:])
                    p.vector.tensor_scalar(p1[:], es[:], rs[:], None, op0=ALU.mult)
                    p.vector.tensor_tensor(pbf[:], p1[:], rv[:], op=ALU.mult).inc(ve)
                    v_pbf = S["ve"].count
                    p.tensor.wait(ve, v_pbf)
                    MM(pbx[0:64, 128:256], pbf[:], identS[:], start=True, stop=True,
                       is_transpose=True).inc(te)
                    t_T = te.count
                    p.scalar.wait(te, t_T)
                    p.scalar.copy(pTb[o2:o2 + 64, :], pbx[0:64, 128:256]).inc(ac)
                    a_pT = S["ac"].count
                    p.tensor.wait(ac, a_pT)
                    MM(pb[1][:, 0:128], wvTh[o2:o2 + 64, c * 128:(c + 1) * 128], pTb[o2:o2 + 64, :],
                       start=True, stop=True).inc(te)
                    t_A = te.count
                    p.vector.wait(te, t_A)
                    p.vector.tensor_tensor(abf[:], pb[1][:, 0:128], ustriS[:], op=ALU.mult).inc(ve)
                    v_abf = S["ve"].count
                    p.tensor.wait(ve, v_abf)
                    if c == 0:
                        MM(pb[5][:, 0:256], abf[:], vh[:, 0:256], start=True, stop=True).inc(te)
                    else:
                        MM(pb[5][:, 0:256], abf[:], vh[:, c * 256:(c + 1) * 256], start=True, stop=False)
                        MM(pb[5][:, 0:256], pTb[o2:o2 + 64, :], Svbf[jm][o2:o2 + 64, :],
                           start=False, stop=True).inc(te)
                    t_O = te.count
                    p.scalar.wait(te, t_O)
                    if v_ro:
                        p.scalar.wait(ve, v_ro)
                    p.scalar.activation(odum[:], pb[5][:, 0:256], AF.Square, accum_out=osum[:]).inc(ac)
                    a_sq = S["ac"].count
                    p.vector.wait(ac, a_sq)
                    p.vector.tensor_scalar(ro[:], osum[:], 1.0 / 256.0, 1e-5, op0=ALU.mult, op1=ALU.add).inc(ve)
                    v_ro = S["ve"].count
                    p.scalar.wait(ve, v_ro)
                    p.scalar.activation(ro[:], ro[:], AF.Sqrt)
                    p.scalar.wait(S["lda" if j == 0 else "ldb"], gcnt)
                    if gld[jm]:
                        p.scalar.wait(po, gld[jm])
                    p.scalar.activation(sgm[:], g2b[j][:], AF.Sigmoid).inc(ac)
                    a_sg = S["ac"].count
                    p.gpsimd.wait(ac, a_sg)
                    if v_obf:
                        p.gpsimd.wait(ve, v_obf)
                    p.gpsimd.tensor_tensor(gg[:], g2b[j][:], sgm[:], op=ALU.mult).inc(po)
                    gld[j] = S["po"].count
                    p.vector.wait(ac, a_sg)
                    p.vector.reciprocal(ro[:], ro[:])
                    p.vector.tensor_scalar(on_s[:], pb[5][:, 0:256], ro[:], None, op0=ALU.mult)
                    p.vector.wait(po, gld[j])
                    p.vector.tensor_tensor(on2[:], on_s[:], gnwS[:], op=ALU.mult)
                    p.vector.wait(st)
                    p.vector.tensor_tensor(obft[:], on2[:], gg[:], op=ALU.mult).inc(ve)
                    v_obf = S["ve"].count
                    p.gpsimd.wait(ve, v_obf)
                    _dma("gpsimd", H["o2_hbm"][c * 128:(c + 1) * 128, h * 256:(h + 1) * 256], obft[:], "st")
        ctx.barrier()

        with ExitStack() as s3k:
            sb3 = lambda n, sh, dt: ctx.sb(s3k, n, sh, dt)
            wo2S = sb3("c_wo2", [128, NCH * 1024], BF16)
            oTg = sb3("c_oTg", [128, NCH * 512], BF16)
            x2t = sb3("c_x2t", [128, 1024], F32)
            zs = sb3("c_zs", [128, 1024], F32)
            zbf = sb3("c_zbf", [128, 1024], BF16)
            _dma("gpsimd", wo2S[:].rearrange("p (c x) -> p c x", c=NCH), chunked(H["wo2"]), "st")
            for g in range(8):
                p.sync.wait(te)
                for dc in range(NCH):
                    _dmaT(oTg[:, dc * 512:(dc + 1) * 512],
                          H["o2_hbm"][g * 512:(g + 1) * 512, dc * 128:(dc + 1) * 128])
                for tl in range(4):
                    tt = g * 4 + tl
                    p.sync.wait(ve)
                    _dma("sync", x2t[:], x_src[tt * 128:(tt + 1) * 128, :], "ld")
                    p.tensor.wait(ld)
                    p.tensor.wait(S["st"])
                    p.tensor.wait(ve)
                    for ih in range(2):
                        for dc in range(NCH):
                            mm = MM(pb[ih][:, 0:512], oTg[:, dc * 512 + tl * 128: dc * 512 + tl * 128 + 128],
                                    wo2S[:, dc * 1024 + ih * 512: dc * 1024 + (ih + 1) * 512],
                                    start=(dc == 0), stop=(dc == NCH - 1))
                    mm.inc(te)
                    p.vector.wait(te)
                    p.vector.wait(ld)
                    p.vector.wait(st)
                    p.vector.tensor_tensor(zs[:, 0:512], pb[0][:, 0:512], x2t[:, 0:512], op=ALU.add)
                    p.vector.tensor_tensor(zs[:, 512:1024], pb[1][:, 0:512], x2t[:, 512:1024],
                                           op=ALU.add).inc(ve)
                    emit_ln(ctx, K, zs, lnb, out_bf=zbf if dst_bf is not None else None)
                    p.gpsimd.wait(ve)
                    _dma("gpsimd", x_dst[tt * 128:(tt + 1) * 128, :], zs[:], "st")
                    if dst_bf is not None:
                        _dma("gpsimd", dst_bf[tt * 128:(tt + 1) * 128, :], zbf[:], "st")
        p.sync.wait(st)


def decl_c(nc, H, K):
    F32, BF16, T = K.F32, K.BF16, K.T
    def di(name, shape, dt):
        if name not in H:
            H[name] = nc.dram_tensor(name, list(shape), dt, kind="ExternalInput").ap()
    def ds(name, shape, dt):
        if name not in H:
            H[name] = nc.dram_tensor(name, list(shape), dt, kind="Internal").ap()
    for n in ("wq2", "wk2", "wv2", "wg2", "wo2"):
        di(n, (1024, 1024), F32)
    di("skw1", (1024, 16), F32)
    di("svw1", (1024, 16), F32)
    di("skw2", (16, 256), F32)
    di("svw2", (16, 256), F32)
    di("skb2", (1, 256), F32)
    di("svb2", (1, 256), F32)
    di("gnw_bc", (128, 256), F32)
    di("c1cos", (128, T), BF16)
    di("c1sin", (128, T), BF16)
    di("c1cosTok", (T, 128), BF16)
    di("c1sinTok", (T, 128), BF16)
    di("ustri", (128, 128), BF16)
    di("identbf", (128, 128), BF16)
    ds("qT_hbm", (1024, T), BF16)
    ds("k_hbm", (T, 1024), BF16)
    ds("v_hbm", (T, 1024), BF16)
    ds("g_hbm", (T, 1024), F32)
    ds("wk_hbm", (T, 256), BF16)
    ds("wv_hbm", (T, 256), BF16)
    ds("o2_hbm", (T, 1024), BF16)
    return H


def host_consts_c(K):
    import ml_dtypes
    BF = ml_dtypes.bfloat16
    T = K.T
    j = np.arange(128, dtype=np.float64)
    inv = 1.0 / (10000.0 ** (j / 128.0))
    ang = np.arange(T, dtype=np.float64)[:, None] * inv[None, :]
    s_ = np.arange(128)[:, None]
    t_ = np.arange(128)[None, :]
    return {"c1cos": np.ascontiguousarray(np.cos(ang).T).astype(np.float32).astype(BF),
            "c1sin": np.ascontiguousarray(np.sin(ang).T).astype(np.float32).astype(BF),
            "c1cosTok": np.ascontiguousarray(np.cos(ang)).astype(np.float32).astype(BF),
            "c1sinTok": np.ascontiguousarray(np.sin(ang)).astype(np.float32).astype(BF),
            "ustri": (s_ <= t_).astype(np.float32).astype(BF),
            "identbf": np.eye(128, dtype=np.float32).astype(BF)}


def host_inputs_c(inp, K):
    f32 = np.float32
    d = {
        "wq2": np.ascontiguousarray(np.asarray(inp["L1_attn_Wq"], f32)),
        "wk2": np.ascontiguousarray(np.asarray(inp["L1_attn_Wk"], f32)),
        "wv2": np.ascontiguousarray(np.asarray(inp["L1_attn_Wv"], f32)),
        "wg2": np.ascontiguousarray(np.asarray(inp["L1_attn_Wg"], f32)),
        "wo2": np.ascontiguousarray(np.asarray(inp["L1_attn_Wo"], f32)),
        "skw1": np.ascontiguousarray(np.asarray(inp["L1_attn_sk_w1"], f32)),
        "svw1": np.ascontiguousarray(np.asarray(inp["L1_attn_sv_w1"], f32)),
        "skw2": np.ascontiguousarray(np.asarray(inp["L1_attn_sk_w2"], f32)),
        "svw2": np.ascontiguousarray(np.asarray(inp["L1_attn_sv_w2"], f32)),
        "skb2": np.asarray(inp["L1_attn_sk_b2"], f32).reshape(1, 256),
        "svb2": np.asarray(inp["L1_attn_sv_b2"], f32).reshape(1, 256),
        "gnw_bc": np.ascontiguousarray(np.broadcast_to(
            np.asarray(inp["L1_attn_gnorm_w"], f32).reshape(1, 256), (128, 256)).copy()),
    }
    d.update(host_consts_c(K))
    return d



AGB = [(0, 512), (512, 512), (1024, 512), (1536, 512), (2048, 512), (2560, 170)]


def emit_phase_d(ctx, H, x_src, x_dst, K, nlayers=2, x_src_bf=None):
    p, nc, S = ctx.p, ctx.nc, ctx.S
    F32, BF16, AF, ALU, AX = K.F32, K.BF16, K.AF, K.ALU, K.AX
    NCH, T = K.NCH, K.T
    ld, st, te, ve, ac, po, ld2 = (S["ld"], S["st"], S["te"], S["ve"], S["ac"], S["po"], S["ld2"])
    pb, pbt, pbx = ctx.pb, ctx.pbt, ctx.pbx
    MM = p.tensor.matmul
    _dma = lambda q, out, in_, sem="ld": getattr(p, q).dma_start(out=out, in_=in_).inc(S[sem], 16)
    _dmaT = lambda out, in_: p.sync.dma_start_transpose(out, in_).inc(ld, 16)
    chunked = K.chunked

    with ExitStack() as stk:
        sb = lambda n, sh, dt: ctx.sb(stk, n, sh, dt)
        lnb = mk_lnb(ctx, stk, "d", K)
        onesS = sb("d_ones", [1, 512], BF16)
        p.vector.memset(onesS[:], 1.0)

        xa, xb = H["xd_a"], H["xd_b"]
        for l in range(nlayers):
            lx_in = x_src if l == 0 else xb
            lx_mid = xa
            lx_out = xb if l == 0 else H["xd_d"]
            sfx = str(l)

            lxbf = (x_src_bf if l == 0 else H["xbbf"]) if x_src_bf is not None else None
            if lxbf is None:
                p.gpsimd.wait(st)
                for i in range(8):
                    _dma("gpsimd", H["h1bf"][i * 512:(i + 1) * 512, 0:512],
                         lx_in[i * 512:(i + 1) * 512, 0:512], "st")
                _dma("gpsimd", H["h1bf"][1:T, 512:1024], lx_in[0:T - 1, 512:1024], "st")
                p.vector.memset(lnb["dumb"][0:1, 0:512], 0.0)
                p.vector.nop().inc(ve)
                p.gpsimd.wait(ve)
                _dma("gpsimd", H["h1bf"][0:1, 512:1024], lnb["dumb"][0:1, 0:512], "st")
                ctx.barrier()

            with ExitStack() as s2k:
                sb2 = lambda n, sh, dt: ctx.sb(s2k, n + sfx, sh, dt)
                wqkvS = sb2("d_wqkv", [128, NCH * 1536], BF16)
                hT = [sb2(f"d_hT{i}", [128, NCH * 512], BF16) for i in range(2)]
                qkvf2 = [sb2(f"d_qkvf{i}", [128, 1536], F32) for i in range(2)]
                outq = sb2("d_outq", [128, 512], BF16)
                outk = sb2("d_outk", [128, 512], BF16)
                s2q = sb2("d_s2q", [128, 8], F32)
                rnq = sb2("d_rnq", [128, 8], F32)
                q2f = sb2("d_q2f", [128, 512], F32)
                t1d = sb2("d_t1", [128, 512], F32)
                t2d = sb2("d_t2", [128, 512], F32)
                outb = sb2("d_outb", [128, 512], BF16)
                qsS = sb2("d_qs", [128, 512], F32)
                ksS = sb2("d_ks", [128, 512], F32)
                tcq = sb2("d_tcq", [128, 4 * 64], BF16)
                tsq = sb2("d_tsq", [128, 4 * 64], BF16)
                tck = sb2("d_tck", [128, 4 * 64], BF16)
                tsk = sb2("d_tsk", [128, 4 * 64], BF16)
                _dma("gpsimd", wqkvS[:].rearrange("p (c x) -> p c x", c=NCH), chunked(H["wqkv" + sfx]), "st")
                _dma("sync", qsS[:], H["qs_bc" + sfx][:, :])
                _dma("sync", ksS[:], H["ks_bc" + sfx][:, :])
                if lxbf is not None:
                    p.vector.memset(lnb["dumb"][0:1, 0:512], 0.0)
                    p.vector.nop().inc(ve)
                    p.gpsimd.wait(ve)
                    _dma("gpsimd", H["xshift0"][0:1, :], lnb["dumb"][0:1, 0:512], "st")
                    _dma("gpsimd", H["xshift0"][1:512, :], lxbf[0:511, 512:1024], "st")
                v38 = lambda ap: ap[:].rearrange("p (h j) -> p h j", h=8)
                hT_done = [0, 0]
                d2_ve = 0
                qkv_done = [0, 0]
                rope_done = [0, 0]
                for tb in range(8):
                    i = tb % 2
                    if hT_done[i]:
                        p.sync.wait(te, hT_done[i])
                    if d2_ve:
                        p.sync.wait(ve, d2_ve)
                    if lxbf is None:
                        for c in range(NCH):
                            _dmaT(hT[i][:, c * 512:(c + 1) * 512],
                                  H["h1bf"][tb * 512:(tb + 1) * 512, c * 128:(c + 1) * 128])
                    else:
                        if tb == 0:
                            p.sync.wait(st)
                        for c in range(NCH):
                            if c < 4:
                                _dmaT(hT[i][:, c * 512:(c + 1) * 512],
                                      lxbf[tb * 512:(tb + 1) * 512, c * 128:(c + 1) * 128])
                            elif tb == 0:
                                _dmaT(hT[i][:, c * 512:(c + 1) * 512],
                                      H["xshift0"][0:512, (c - 4) * 128:(c - 3) * 128])
                            else:
                                _dmaT(hT[i][:, c * 512:(c + 1) * 512],
                                      lxbf[tb * 512 - 1:(tb + 1) * 512 - 1, c * 128:(c + 1) * 128])
                    for nm, tbl in (("dcq", tcq), ("dsq", tsq), ("dck", tck), ("dsk", tsk)):
                        _dma("sync", tbl[:].rearrange("p (a j) -> p a j", a=4),
                             H[nm].rearrange("(a p) j -> p a j", p=128)[:, tb * 4:(tb + 1) * 4, :])
                    p.tensor.wait(ld)
                    p.tensor.wait(S["st"])
                    for tl in range(4):
                        tt = tb * 4 + tl
                        jp = tt % 2
                        qkvf = qkvf2[jp]
                        if tt >= 2:
                            p.tensor.wait(ac, qkv_done[jp])
                        for ih in range(3):
                            for cc in range(NCH):
                                MM(pb[3 * jp + ih][:, 0:512],
                                   hT[i][:, cc * 512 + tl * 128: cc * 512 + tl * 128 + 128],
                                   wqkvS[:, cc * 1536 + ih * 512: cc * 1536 + (ih + 1) * 512],
                                   start=(cc == 0), stop=(cc == NCH - 1)).inc(te)
                        mm_te = S["te"].count
                        p.scalar.wait(te, mm_te)
                        if rope_done[jp]:
                            p.scalar.wait(ve, rope_done[jp])
                        for ih in range(3):
                            p.scalar.copy(qkvf[:, ih * 512:(ih + 1) * 512], pb[3 * jp + ih][:, 0:512]).inc(ac)
                        qkv_done[jp] = S["ac"].count
                        p.vector.wait(ac, qkv_done[jp])
                        p.vector.wait(st)
                        p.vector.tensor_copy(outb[:], qkvf[:, 1024:1536]).inc(ve)
                        p.gpsimd.wait(ve)
                        _dma("gpsimd", H["vd_hbm"][tt * 128:(tt + 1) * 128, :], outb[:], "st")
                        for (off, scS, tc, ts_, dhbm, outx) in ((0, qsS, tcq, tsq, H["qd_hbm"], outq),
                                                                (512, ksS, tck, tsk, H["kd_hbm"], outk)):
                            qv = qkvf[:, off:off + 512]
                            p.vector.wait(ac)
                            p.vector.tensor_tensor(t1d[:], qv, qv, op=ALU.mult)
                            p.vector.tensor_reduce(s2q[:], t1d[:].rearrange("p (h j) -> p h j", h=8),
                                                   axis=AX.X, op=ALU.add).inc(ve)
                            p.scalar.wait(ve)
                            p.scalar.sqrt(rnq[:], s2q[:]).inc(ac)
                            p.vector.wait(ac)
                            p.vector.tensor_single_scalar(rnq[:], rnq[:], 1e-12, op=ALU.max)
                            p.vector.reciprocal(rnq[:], rnq[:])
                            p.vector.tensor_tensor(v38(q2f), v38(qv),
                                                   rnq[:].unsqueeze(2).broadcast_to([128, 8, 64]), op=ALU.mult)
                            p.vector.tensor_tensor(q2f[:], q2f[:], scS[:], op=ALU.mult)
                            tcb = tc[:].rearrange("p (a j) -> p a j", a=4)[:, tl, :].unsqueeze(1).broadcast_to([128, 8, 64])
                            tsb = ts_[:].rearrange("p (a j) -> p a j", a=4)[:, tl, :].unsqueeze(1).broadcast_to([128, 8, 64])
                            p.vector.tensor_tensor(v38(t1d), v38(q2f), tcb, op=ALU.mult)
                            q3 = q2f[:].rearrange("p (h u j) -> p h u j", h=8, u=2)
                            t23 = t2d[:].rearrange("p (h u j) -> p h u j", h=8, u=2)
                            tsb2 = ts_[:].rearrange("p (a j) -> p a j", a=4)[:, tl, :]\
                                .rearrange("p (u j) -> p u j", u=2).unsqueeze(1).broadcast_to([128, 8, 2, 32])
                            p.vector.tensor_tensor(t23[:, :, 0, :], q3[:, :, 1, :], tsb2[:, :, 0, :], op=ALU.mult)
                            p.vector.tensor_tensor(t23[:, :, 1, :], q3[:, :, 0, :], tsb2[:, :, 1, :], op=ALU.mult)
                            p.vector.tensor_tensor(outx[:], t1d[:], t2d[:], op=ALU.add).inc(ve)
                            dbg = getattr(ctx, 'dbgd', None)
                            if dbg is not None and tt == 0 and l == 0 and off == 0:
                                p.gpsimd.wait(ve)
                                _dma("gpsimd", dbg["dq2f"][:, :], q2f[:], "st")
                                _dma("gpsimd", dbg["ds2q"][:, :], s2q[:], "st")
                                _dma("gpsimd", dbg["dt1d"][:, :], t1d[:], "st")
                                _dma("gpsimd", dbg["dt2d"][:, :], t2d[:], "st")
                                _dma("gpsimd", dbg["dqkvf"][:, :], qkvf[:], "st")
                            p.gpsimd.wait(ve)
                            _dma("gpsimd", dhbm[tt * 128:(tt + 1) * 128, :], outx[:], "st")
                        rope_done[jp] = S["ve"].count
                    hT_done[i] = te.count
                    d2_ve = ve.count
            ctx.barrier()

            with ExitStack() as s3k:
                sb3 = lambda n, sh, dt: ctx.sb(s3k, n + sfx, sh, dt)
                woS = sb3("d_wo", [128, 4 * 1024], BF16)
                qTw = [sb3(f"d_qTw{i}", [128, 4 * 512], BF16) for i in range(2)]
                kTw = [sb3(f"d_kTw{i}", [128, 4 * 1024], BF16) for i in range(2)]
                vw = [sb3(f"d_vw{i}", [128, 8 * 512], BF16) for i in range(2)]
                oTw = sb3("d_oTw", [128, 4 * 512], BF16)
                dmaskS = sb3("d_dmask", [128, 8 * 512], BF16)
                ones128d = sb3("d_ones128", [128, 1], BF16)
                sfd = [sb3(f"d_sf{i}", [128, 512], F32) for i in range(2)]
                pbfd = [sb3(f"d_pbf{i}", [128, 512], BF16) for i in range(2)]
                rld = sb3("d_rl", [1, 512], F32)
                rlbfd = sb3("d_rlbf", [1, 512], BF16)
                rbd = sb3("d_rb", [64, 512], F32)
                xt = sb3("d_xt", [128, 1024], F32)
                zsd = sb3("d_zs", [128, 1024], F32)
                _dma("gpsimd", woS[:].rearrange("p (c x) -> p c x", c=4),
                     H["wo" + sfx].rearrange("(c p) x -> p c x", p=128), "st")
                _dma("sync", dmaskS[:], H["dmask"][:, :])
                p.vector.memset(kTw[0][:], 0.0)
                p.vector.memset(ones128d[:], 1.0)
                p.vector.memset(vw[0][:], 0.0).inc(ve)
                p.sync.wait(ve)
                pf_done = [0, 0]
                stt_done = [0, 0]
                wb_done = [0, 0]
                ld3 = {}

                def d3_load(w):
                    bw = w % 2
                    sl = "lda" if bw == 0 else "ldb"
                    if wb_done[bw]:
                        p.sync.wait(te, wb_done[bw])
                    for c in range(4):
                        p.sync.dma_start_transpose(qTw[bw][:, c * 512:(c + 1) * 512],
                              H["qd_hbm"][w * 512:(w + 1) * 512, c * 128:(c + 1) * 128]).inc(S[sl], 16)
                    lo = (w - 1) * 512
                    for c in range(4):
                        p.sync.dma_start_transpose(kTw[bw][:, c * 1024 + (0 if w > 0 else 512):(c + 1) * 1024],
                              H["kd_hbm"][max(lo, 0):(w + 1) * 512, c * 128:(c + 1) * 128]).inc(S[sl], 16)
                    _dma("sync", vw[bw][:].rearrange("p (n c) -> p n c", n=8)[:, (0 if w > 0 else 4):8, :],
                         H["vd_hbm"].rearrange("(n p) c -> p n c", p=128)[:, max(0, (w - 1) * 4):(w + 1) * 4, :], sl)
                    ld3[w] = S[sl].count

                d3_load(0)
                for w in range(8):
                    bw = w % 2
                    qTwb, kTwb, vwb = qTw[bw], kTw[bw], vw[bw]
                    if w + 1 < 8:
                        d3_load(w + 1)
                    p.tensor.wait(S["lda" if bw == 0 else "ldb"], ld3[w])
                    p.tensor.wait(S["st"])
                    for h in range(8):
                        hb = (h % 2) * 64
                        hc = h // 2
                        sb0 = 0 if w > 0 else 4
                        s3_te = {}

                        def emit_S3(sbk):
                            j = sbk % 2
                            if stt_done[j]:
                                p.tensor.wait(ve, stt_done[j])
                            MM(pb[4 + j][:, 0:512],
                               kTwb[hb:hb + 64, hc * 1024 + sbk * 128: hc * 1024 + sbk * 128 + 128],
                               qTwb[hb:hb + 64, hc * 512: (hc + 1) * 512], start=True, stop=True).inc(te)
                            s3_te[sbk] = S["te"].count

                        emit_S3(sb0)
                        for sbk in range(sb0, 8):
                            j = sbk % 2
                            if sbk + 1 < 8:
                                emit_S3(sbk + 1)
                            p.vector.wait(te, s3_te[sbk])
                            p.vector.wait(ac)
                            p.vector.scalar_tensor_tensor(sfd[j][:], pb[4 + j][:, 0:512], 1.0,
                                                          dmaskS[:, sbk * 512:(sbk + 1) * 512],
                                                          op0=ALU.mult, op1=ALU.add).inc(ve)
                            stt_done[j] = S["ve"].count
                            p.scalar.wait(ve, stt_done[j])
                            if pf_done[j]:
                                p.scalar.wait(te, pf_done[j])
                            p.scalar.activation(pbfd[j][:], sfd[j][:], AF.Exp).inc(ac)
                            p.tensor.wait(ac)
                            MM(pbt[0:1, 0:512], ones128d[:], pbfd[j][:],
                               start=(sbk == sb0), stop=(sbk == 7), skip_group_check=True)
                            MM(pb[0][0:64, 0:512], vwb[:, sbk * 512 + h * 64: sbk * 512 + (h + 1) * 64],
                               pbfd[j][:], start=(sbk == sb0), stop=(sbk == 7), skip_group_check=True).inc(te)
                            pf_done[j] = te.count
                        p.vector.wait(te)
                        p.vector.reciprocal(rld[:], pbt[0:1, 0:512])
                        p.vector.tensor_copy(rlbfd[:], rld[:]).inc(ve)
                        p.tensor.wait(ve)
                        MM(pb[1][0:64, 0:512], onesS[0:1, 0:64], rlbfd[:], start=True, stop=True).inc(te)
                        p.vector.wait(te)
                        p.vector.tensor_copy(rbd[0:64, :], pb[1][0:64, 0:512])
                        p.vector.tensor_tensor(oTw[hb:hb + 64, hc * 512:(hc + 1) * 512],
                                               pb[0][0:64, 0:512], rbd[0:64, :], op=ALU.mult).inc(ve)
                    for tl in range(4):
                        tt = w * 4 + tl
                        p.sync.wait(ve)
                        _dma("sync", xt[:], lx_in[tt * 128:(tt + 1) * 128, :], "ld")
                        p.tensor.wait(ld)
                        p.tensor.wait(ve)
                        for ih in range(2):
                            for dc in range(4):
                                mm = MM(pb[2 + ih][:, 0:512],
                                        oTw[:, dc * 512 + tl * 128: dc * 512 + tl * 128 + 128],
                                        woS[:, dc * 1024 + ih * 512: dc * 1024 + (ih + 1) * 512],
                                        start=(dc == 0), stop=(dc == 3))
                        mm.inc(te)
                        p.vector.wait(te)
                        p.vector.wait(ld)
                        p.vector.wait(st)
                        p.vector.tensor_tensor(zsd[:, 0:512], pb[2][:, 0:512], xt[:, 0:512], op=ALU.add)
                        p.vector.tensor_tensor(zsd[:, 512:1024], pb[3][:, 0:512], xt[:, 512:1024],
                                               op=ALU.add).inc(ve)
                        p.gpsimd.wait(ve)
                        _dma("gpsimd", lx_mid[tt * 128:(tt + 1) * 128, :], zsd[:], "st")
                    wb_done[bw] = S["te"].count
            ctx.barrier()

            with ExitStack() as s5k:
                sb5 = lambda n, sh, dt: ctx.sb(s5k, n + sfx, sh, dt)
                lngS = sb5("d_lng", [128, 1024], F32)
                lnbS = sb5("d_lnb", [128, 1024], F32)
                xt4 = [sb5(f"d_xt4{i}", [128, 1024], F32) for i in range(2)]
                zbf4 = [sb5(f"d_zbf4{i}", [128, 1024], BF16) for i in range(2)]
                _dma("sync", lngS[:], H["lng_bc" + sfx][:, :])
                _dma("sync", lnbS[:], H["lnb_bc" + sfx][:, :])
                xt_done = [0, 0]
                ld4 = {}

                def d4a_load(tt):
                    bi = tt % 2
                    sl = "lda" if bi == 0 else "ldb"
                    if xt_done[bi]:
                        p.sync.wait(ve, xt_done[bi])
                    _dma("sync", xt4[bi][:, 0:512], lx_mid[tt * 128:(tt + 1) * 128, 0:512], sl)
                    if tt == 0:
                        _dma("sync", xt4[bi][1:128, 512:1024], lx_mid[0:127, 512:1024], sl)
                    else:
                        _dma("sync", xt4[bi][:, 512:1024], lx_mid[tt * 128 - 1:(tt + 1) * 128 - 1, 512:1024], sl)
                    ld4[tt] = S[sl].count

                def d4a_tile(tt):
                    bi = tt % 2
                    if tt + 1 < 32:
                        d4a_load(tt + 1)
                    sl = "lda" if bi == 0 else "ldb"
                    p.vector.wait(S[sl], ld4[tt])
                    p.vector.wait(S["stz"])
                    if tt == 0:
                        p.vector.wait(ld)
                        p.vector.memset(xt4[0][0:1, 512:1024], 0.0).inc(ve)
                    p.scalar.wait(S[sl], ld4[tt])
                    emit_ln(ctx, K, xt4[bi], lnb)
                    p.vector.tensor_tensor(xt4[bi][:], xt4[bi][:], lngS[:], op=ALU.mult)
                    p.vector.tensor_tensor(zbf4[bi][:], xt4[bi][:], lnbS[:], op=ALU.add).inc(ve)
                    xt_done[bi] = S["ve"].count
                    p.gpsimd.wait(ve)
                    _dma("gpsimd", H["xh_bf"][tt * 128:(tt + 1) * 128, :], zbf4[bi][:], "stz")

                w1S = sb5("d_w1", [128, NCH * 5460], BF16)
                b1S = sb5("d_b1", [1, 5460], BF16)
                xhT = [sb5(f"d_xhT{i}", [128, NCH * 512], BF16) for i in range(2)]
                ggf = sb5("d_gg", [128, 1024], F32)
                h2b = sb5("d_h2", [128, 2816], BF16)
                _dma("gpsimd", w1S[:].rearrange("p (c x) -> p c x", c=NCH), chunked(H["w1" + sfx]), "st")
                _dma("gpsimd", b1S[:], H["b1" + sfx][:, :], "st")
                p.vector.memset(h2b[:, 2730:2816], 0.0).inc(ve)
                xhT_done = [0, 0]
                w1_done = [(0, 0), (0, 0)]
                d4a_load(0)
                for g in range(8):
                    for tl4 in range(4):
                        d4a_tile(g * 4 + tl4)
                    i = g % 2
                    if xhT_done[i]:
                        p.sync.wait(te, xhT_done[i])
                    p.sync.wait(S["stz"])
                    for c in range(NCH):
                        _dmaT(xhT[i][:, c * 512:(c + 1) * 512],
                              H["xh_bf"][g * 512:(g + 1) * 512, c * 128:(c + 1) * 128])
                    p.tensor.wait(ld)
                    if g == 0:
                        p.tensor.wait(S["st"])
                    for tl in range(4):
                        tt = g * 4 + tl
                        for nib, (ib, bw) in enumerate(AGB):
                            jp = nib % 2
                            if w1_done[jp][0]:
                                p.tensor.wait(ac, w1_done[jp][0])
                                p.tensor.wait(ve, w1_done[jp][1])
                            for half, psX in ((0, pb[2 * jp]), (1, pb[2 * jp + 1])):
                                off = ib + half * 2730
                                for cc in range(NCH):
                                    MM(psX[:, 0:bw],
                                       xhT[i][:, cc * 512 + tl * 128: cc * 512 + tl * 128 + 128],
                                       w1S[:, cc * 5460 + off: cc * 5460 + off + bw],
                                       start=(cc == 0), stop=False)
                                MM(psX[:, 0:bw], onesS[0:1, 0:128],
                                   b1S[0:1, off:off + bw], start=False, stop=True).inc(te)
                            p.scalar.wait(te)
                            p.scalar.activation(ggf[:, jp * 512: jp * 512 + bw], pb[2 * jp + 1][:, 0:bw],
                                                AF.Gelu).inc(ac)
                            p.vector.wait(ac)
                            p.vector.wait(st)
                            p.vector.tensor_tensor(h2b[:, ib:ib + bw], ggf[:, jp * 512: jp * 512 + bw],
                                                   pb[2 * jp][:, 0:bw], op=ALU.mult).inc(ve)
                            w1_done[jp] = (S["ac"].count, S["ve"].count)
                        p.gpsimd.wait(ve)
                        _dma("gpsimd", H["h2_hbm"][tt * 128:(tt + 1) * 128, :], h2b[:], "st")
                    xhT_done[i] = te.count
            ctx.barrier()

            with ExitStack() as s6k:
                sb6 = lambda n, sh, dt: ctx.sb(s6k, n + sfx, sh, dt)
                w2S = sb6("d_w2", [128, 22 * 1024], BF16)
                b2S = sb6("d_b2", [1, 1024], BF16)
                h2T = sb6("d_h2T", [128, 22 * 512], BF16)
                xt6 = sb6("d_xt6", [128, 1024], F32)
                zs6 = sb6("d_zs6", [128, 1024], F32)
                zb6 = sb6("d_zb6", [128, 1024], BF16)
                p.vector.memset(w2S[:, 21 * 1024:22 * 1024], 0.0).inc(ve)
                p.gpsimd.wait(ve)
                _dma("gpsimd", w2S[:].rearrange("p (c x) -> p c x", c=22)[:, 0:21, :],
                     H["w2" + sfx][0:2688, :].rearrange("(c p) x -> p c x", p=128), "st")
                _dma("gpsimd", w2S[0:42, 21 * 1024:22 * 1024], H["w2" + sfx][2688:2730, :], "st")
                _dma("gpsimd", b2S[:], H["b2" + sfx][:, :], "st")
                w2_done = [0, 0]
                for g in range(8):
                    p.sync.wait(te)
                    for c in range(22):
                        _dmaT(h2T[:, c * 512:(c + 1) * 512],
                              H["h2_hbm"][g * 512:(g + 1) * 512, c * 128:(c + 1) * 128])
                    for tl in range(4):
                        tt = g * 4 + tl
                        jp = tt % 2
                        p.sync.wait(ve)
                        _dma("sync", xt6[:], lx_mid[tt * 128:(tt + 1) * 128, :], "ld")
                        p.tensor.wait(ld)
                        p.tensor.wait(S["st"])
                        if tt >= 2:
                            p.tensor.wait(ve, w2_done[jp])
                        for ih in range(2):
                            for c in range(22):
                                MM(pb[2 * jp + ih][:, 0:512], h2T[:, c * 512 + tl * 128: c * 512 + tl * 128 + 128],
                                   w2S[:, c * 1024 + ih * 512: c * 1024 + (ih + 1) * 512],
                                   start=(c == 0), stop=False)
                            MM(pb[2 * jp + ih][:, 0:512], onesS[0:1, 0:128],
                               b2S[0:1, ih * 512:(ih + 1) * 512], start=False, stop=True).inc(te)
                        p.vector.wait(te)
                        p.vector.wait(ld)
                        p.vector.wait(st)
                        p.vector.tensor_tensor(zs6[:, 0:512], pb[2 * jp][:, 0:512], xt6[:, 0:512], op=ALU.add)
                        p.vector.tensor_tensor(zs6[:, 512:1024], pb[2 * jp + 1][:, 0:512], xt6[:, 512:1024],
                                               op=ALU.add).inc(ve)
                        w2_done[jp] = S["ve"].count
                        if l == 0 and x_src_bf is not None:
                            p.vector.tensor_copy(zb6[:], zs6[:]).inc(ve)
                        p.gpsimd.wait(ve)
                        _dma("gpsimd", lx_out[tt * 128:(tt + 1) * 128, :], zs6[:], "st")
                        if l == 0 and x_src_bf is not None:
                            _dma("gpsimd", H["xbbf"][tt * 128:(tt + 1) * 128, :], zb6[:], "st")
            ctx.barrier()

        if nlayers < 2:
            p.sync.wait(st)
            return
        with ExitStack() as s7k:
            sb7 = lambda n, sh, dt: ctx.sb(s7k, n, sh, dt)
            xt7 = sb7("d_xt7", [128, 1024], F32)
            xt8 = sb7("d_xt8", [128, 1024], F32)
            for tt in range(32):
                p.sync.wait(ve)
                p.sync.wait(S["st"])
                _dma("sync", xt7[:], x_src[tt * 128:(tt + 1) * 128, :], "ld")
                _dma("sync", xt8[:], H["xd_d"][tt * 128:(tt + 1) * 128, :], "ld")
                p.vector.wait(ld)
                p.vector.wait(st)
                p.vector.tensor_tensor(xt7[:], xt7[:], xt8[:], op=ALU.add).inc(ve)
                emit_ln(ctx, K, xt7, lnb)
                p.gpsimd.wait(ve)
                _dma("gpsimd", x_dst[tt * 128:(tt + 1) * 128, :], xt7[:], "st")
        p.sync.wait(st)


def decl_d(nc, H, K):
    F32, BF16, T = K.F32, K.BF16, K.T
    def di(name, shape, dt):
        if name not in H:
            H[name] = nc.dram_tensor(name, list(shape), dt, kind="ExternalInput").ap()
    def ds(name, shape, dt):
        if name not in H:
            H[name] = nc.dram_tensor(name, list(shape), dt, kind="Internal").ap()
    for l in ("0", "1"):
        di("wqkv" + l, (1024, 1536), F32)
        di("wo" + l, (512, 1024), F32)
        di("w1" + l, (1024, 5460), F32)
        di("w2" + l, (2730, 1024), F32)
        di("b1" + l, (1, 5460), F32)
        di("b2" + l, (1, 1024), F32)
        di("qs_bc" + l, (128, 512), F32)
        di("ks_bc" + l, (128, 512), F32)
        di("lng_bc" + l, (128, 1024), F32)
        di("lnb_bc" + l, (128, 1024), F32)
    for nm in ("dcq", "dsq", "dck", "dsk"):
        di(nm, (T, 64), BF16)
    di("dmask", (128, 8 * 512), BF16)
    ds("h1bf", (T, 1024), BF16)
    ds("xbbf", (T, 1024), BF16)
    ds("xshift0", (512, 512), BF16)
    ds("qd_hbm", (T, 512), BF16)
    ds("kd_hbm", (T, 512), BF16)
    ds("vd_hbm", (T, 512), BF16)
    ds("xh_bf", (T, 1024), BF16)
    ds("h2_hbm", (T, 2816), BF16)
    ds("xd_a", (T, 1024), F32)
    ds("xd_b", (T, 1024), F32)
    ds("xd_c", (T, 1024), F32)
    ds("xd_d", (T, 1024), F32)
    return H


def host_consts_d(K):
    import ml_dtypes
    BF = ml_dtypes.bfloat16
    T = K.T
    DIM_HEAD, WINDOW = 64, 512
    half_idx = np.arange(0, DIM_HEAD, 2, dtype=np.float64)
    inv_freq = 1.0 / 10000.0 ** (half_idx / DIM_HEAD)
    t = np.arange(T, dtype=np.float64)
    freqs = t[:, None] * inv_freq[None, :]
    freqs = np.concatenate([freqs, freqs], axis=1)
    base = (half_idx + 0.4 * DIM_HEAD) / (1.4 * DIM_HEAD)
    power = (t - T // 2) / float(WINDOW // 2)
    scale = base[None, :] ** power[:, None]
    scale = np.concatenate([scale, scale], axis=1)
    cos, sin = np.cos(freqs), np.sin(freqs)
    sgn = np.where(np.arange(64) < 32, -1.0, 1.0)[None, :]
    d = {
        "dcq": (cos * scale * 8.0),
        "dsq": (sin * scale * sgn * 8.0),
        "dck": (cos / scale),
        "dsk": (sin / scale * sgn),
    }
    d = {k: np.ascontiguousarray(v).astype(np.float32).astype(BF) for k, v in d.items()}
    m = np.zeros((128, 8 * 512), np.float32)
    sp = np.arange(128)[:, None]
    tq = np.arange(512)[None, :]
    for sbk in range(8):
        if sbk < 4:
            keep = (sbk * 128 + sp) >= tq
        else:
            keep = ((sbk - 4) * 128 + sp) <= tq
        m[:, sbk * 512:(sbk + 1) * 512] = np.where(keep, 0.0, -30000.0)
    d["dmask"] = m.astype(BF)
    return d


def host_inputs_d(inp, K):
    f32 = np.float32
    d = {}
    for l in range(2):
        s = str(l)
        d["wqkv" + s] = np.ascontiguousarray(np.asarray(inp["L1_ffn_wqkv"][l], f32))
        d["wo" + s] = np.ascontiguousarray(np.asarray(inp["L1_ffn_wo"][l], f32))
        d["w1" + s] = np.ascontiguousarray(np.asarray(inp["L1_ffn_w1"][l], f32))
        d["w2" + s] = np.ascontiguousarray(np.asarray(inp["L1_ffn_w2"][l], f32))
        d["b1" + s] = np.asarray(inp["L1_ffn_b1"][l], f32).reshape(1, 5460)
        d["b2" + s] = np.asarray(inp["L1_ffn_b2"][l], f32).reshape(1, 1024)
        d["qs_bc" + s] = np.ascontiguousarray(np.broadcast_to(
            np.tile(np.asarray(inp["L1_ffn_q_scale"][l], f32), 8)[None, :], (128, 512)).copy())
        d["ks_bc" + s] = np.ascontiguousarray(np.broadcast_to(
            np.tile(np.asarray(inp["L1_ffn_k_scale"][l], f32), 8)[None, :], (128, 512)).copy())
        d["lng_bc" + s] = np.ascontiguousarray(np.broadcast_to(
            np.asarray(inp["L1_ffn_ln_g"][l], f32)[None, :], (128, 1024)).copy())
        d["lnb_bc" + s] = np.ascontiguousarray(np.broadcast_to(
            np.asarray(inp["L1_ffn_ln_b"][l], f32)[None, :], (128, 1024)).copy())
    d.update(host_consts_d(K))
    return d



import types as _types
K = _types.SimpleNamespace(**{k: v for k, v in dict(globals()).items() if not k.startswith("_")})


PHASE_MARKS = {}


def _build_program():
    nc = bass.Bass("TRN2", target_bir_lowering=False)
    H = {}
    H["x0"] = nc.dram_tensor("x0", [T, D], F32, kind="ExternalInput").ap()
    H["out"] = nc.dram_tensor("out", [T, D], F32, kind="ExternalOutput").ap()
    decl_a(nc, H)
    decl_b(nc, H, K)
    decl_c(nc, H, K)
    decl_d(nc, H, K)
    for nm in ("x1s", "x2s", "x3s"):
        H[nm] = nc.dram_tensor(nm, [T, D], F32, kind="Internal").ap()
    H["x2bf"] = nc.dram_tensor("x2bf", [T, D], BF16, kind="Internal").ap()
    H["x3bf"] = nc.dram_tensor("x3bf", [T, D], BF16, kind="Internal").ap()
    ctx = Ctx(nc)
    def _mark(nm):
        PHASE_MARKS[nm] = {e: len(q) for e, q in ctx.p.q.items()}
    wq_items = []
    for wname, qname, nch in (("wg", "qwg", 22), ("wu", "qwu", 22)):
        for ch in range(nch):
            wq_items.append((1024, H[wname][ch * 128:(ch + 1) * 128, :],
                             H[qname][ch * 128:(ch + 1) * 128, :]))
    emit_phase_a(ctx, H, H["x0"], H["x1s"], K, qdx_dst=H["qdx"], wq_items=wq_items)
    ctx.barrier()
    _mark("A")
    emit_phase_b(ctx, H, H["x1s"], H["x2s"], K, dst_bf=H["x2bf"], coquant=True)
    ctx.barrier()
    _mark("B")
    emit_phase_c(ctx, H, H["x2s"], H["x2bf"], H["x3s"], K, dst_bf=H["x3bf"])
    ctx.barrier()
    _mark("C")
    emit_phase_d(ctx, H, H["x3s"], H["out"], K, x_src_bf=H["x3bf"])
    ctx.barrier()
    _mark("D")
    ctx.p.replay()
    ctx.stack.close()
    return nc


def kernel(**inputs):
    shared = {}
    shared.update(host_inputs_a(inputs))
    shared.update(host_inputs_b(inputs))
    shared.update(host_inputs_c(inputs, K))
    shared.update(host_inputs_d(inputs, K))
    x = np.asarray(inputs["L0_attn_x"], np.float32)
    nc = _build_program()
    in_maps = []
    for core in range(8):
        m = dict(shared)
        m["x0"] = np.ascontiguousarray(x[core])
        in_maps.append(m)
    res = run_bass_kernel_spmd(nc, in_maps, core_ids=list(range(8)))
    out = np.stack([np.asarray(r["out"], np.float32) for r in res.results], axis=0)
    return out
